# version 02
# baseline (speedup 1.0000x reference)
import numpy as np
import ml_dtypes
import concourse.bass as bass
import concourse.mybir as mybir
from concourse.bass_utils import run_bass_kernel_spmd

F32 = mybir.dt.float32
BF16 = mybir.dt.bfloat16
I32 = mybir.dt.int32
AF = mybir.ActivationFunctionType
ALU = mybir.AluOpType
BF = ml_dtypes.bfloat16

D = 2048
T = 1024
KC = D // 128
NCORES = 8



class Ctx:

    def __init__(self, nc, block, stack=None):
        self.nc = nc
        self.block = block
        self.sems = {}
        self.counts = {}
        self._stack = stack

    def sem(self, name):
        cm = self.nc.semaphore(name)
        h = self._stack.enter_context(cm) if self._stack else cm.__enter__()
        self.sems[name] = h
        self.counts[name] = 0
        return h

    def inc(self, instr, sem, n=1):
        instr.then_inc(self.sems[sem], n)
        self.counts[sem] += n
        return self.counts[sem]

    def wait(self, eng, sem, val=None):
        eng.wait_ge(self.sems[sem], self.counts[sem] if val is None else val)

    def new_phase(self):
        if not hasattr(self, "nc0"):
            self.nc0 = self.nc
        self.phase = getattr(self, "phase", -1) + 1
        self.nc = _NcProxy(self.nc0, self.phase)
        for s in ("lds", "ldg", "st"):
            cm = self.nc0.semaphore(f"{s}{self.phase}")
            self.sems[s] = self._stack.enter_context(cm)
            self.counts[s] = 0

    def ccsem(self, name):
        h = self._stack.enter_context(self.nc.semaphore(name))
        self.sems[name] = h
        self.counts[name] = 0
        return h


class _NcProxy:
    def __init__(self, nc, tag):
        self._nc = nc
        self._tag = tag

    def sbuf_tensor(self, name, *a, **k):
        return self._nc.sbuf_tensor(f"{name}_p{self._tag}", *a, **k)

    def psum_tensor(self, name, *a, **k):
        return self._nc.psum_tensor(f"{name}_p{self._tag}", *a, **k)

    def __getattr__(self, x):
        return getattr(self._nc, x)


def mk_ctx(nc, stack):
    c = Ctx(nc, None, stack)
    for s in ("te", "ve", "se", "bar"):
        c.sem(s)
    c.barn = 0
    return c


def emit_barrier(c):
    snap = {s: c.counts[s] for s in ("te", "ve", "se", "lds", "ldg", "st")}
    c.barn += 1
    target = 5 * c.barn
    for name in ("tensor", "vector", "scalar", "gpsimd", "sync"):
        def body(eng, snap=snap, target=target):
            for s_, v_ in snap.items():
                if v_:
                    eng.wait_ge(c.sems[s_], v_)
            eng.sem_inc(c.sems["bar"], 1)
            eng.wait_ge(c.sems["bar"], target)
        getattr(c.block, name)(body)


def mm_tiles(ct, tensor, psum_ap, w_sb, x_sb, m, n0, nn, kcs=range(KC),
             wait=None):
    ks = list(kcs)
    for i, k in enumerate(ks):
        ins = tensor.matmul(psum_ap, w_sb(k)[:, m * 128:(m + 1) * 128],
                            x_sb(k)[:, n0:n0 + nn],
                            start=(i == 0), stop=(i == len(ks) - 1))
    return ins



def emit_l0a(c, io):
    nc = c.nc
    block = c.block
    x, wqk, wv, wu = io["x"], io["wqk"], io["wv"], io["wu"]
    bqk, bv, bu, ident = io["bqk"], io["bv"], io["bu"], io["ident"]
    q_out, u_out, kv_out = io["q_out"], io["u_out"], io["kv_out"]

    from contextlib import ExitStack
    with ExitStack() as ctx:
        e = ctx.enter_context
        xbf = e(nc.sbuf_tensor("xbf", [128, KC * T], BF16))
        wsb = e(nc.sbuf_tensor("wsb", [128, KC * D], BF16))
        qbf = e(nc.sbuf_tensor("qbf", [128, KC * T], BF16))
        vbf = e(nc.sbuf_tensor("vbf", [128, KC * T], BF16))
        uscr = e(nc.sbuf_tensor("uscr", [128, 2 * T], F32))
        r1 = e(nc.sbuf_tensor("r1", [128, T], F32))
        r2 = e(nc.sbuf_tensor("r2", [128, T], F32))
        sbias = e(nc.sbuf_tensor("sbias", [128, 3 * KC], F32))
        sident = e(nc.sbuf_tensor("sident", [128, 128], BF16))
        ktm = e(nc.sbuf_tensor("ktm", [128, 128], BF16))
        vtm = e(nc.sbuf_tensor("vtm", [128, 128], BF16))
        kvsb = e(nc.sbuf_tensor("kvsb", [128, KC * 128], F32))
        pa = e(nc.psum_tensor("pa", [128, T], F32))
        pkv = e(nc.psum_tensor("pkv", [128, 4 * 128], F32))
        ptr = e(nc.psum_tensor("ptr", [128, 256], BF16))

        def loads(g):
            for k in range(KC):
                c.inc(g.dma_start(out=xbf[:, k * T:(k + 1) * T],
                                  in_=x[k * 128:(k + 1) * 128, :]), "ldg", 16)
            for k in range(KC):
                c.inc(g.dma_start(out=wsb[:, k * D:(k + 1) * D],
                                  in_=wqk[k * 128:(k + 1) * 128, :]), "ldg", 16)
        block.gpsimd(loads)
        ld_xw = c.counts["ldg"]

        def loads2(s):
            c.inc(s.dma_start(out=sbias[:, 0:KC], in_=bqk), "lds", 16)
            c.inc(s.dma_start(out=sbias[:, KC:2 * KC], in_=bv), "lds", 16)
            c.inc(s.dma_start(out=sbias[:, 2 * KC:3 * KC], in_=bu), "lds", 16)
            c.inc(s.dma_start(out=sident[:], in_=ident), "lds", 16)
        block.sync(loads2)

        for proj in range(3):
            wname = [wqk, wv, wu][proj]
            if proj > 0:
                te_before = c.counts["te"]

                def reload(g, wname=wname, need=te_before):
                    g.wait_ge(c.sems["te"], need)
                    for k in range(KC):
                        c.inc(g.dma_start(out=wsb[:, k * D:(k + 1) * D],
                                          in_=wname[k * 128:(k + 1) * 128, :]),
                              "ldg", 16)
                block.gpsimd(reload)
            ld_now = c.counts["ldg"]

            for m in range(KC):
                ve_now = c.counts["ve"]
                se_now = c.counts["se"]

                def mms(tensor, m=m, ld_now=ld_now, ve_now=ve_now, se_now=se_now):
                    tensor.wait_ge(c.sems["ldg"], ld_now)
                    if m == 0:
                        tensor.wait_ge(c.sems["lds"], 64)
                    tensor.wait_ge(c.sems["ve"], ve_now)
                    tensor.wait_ge(c.sems["se"], se_now)
                    for n0 in (0, 512):
                        ins = mm_tiles(c, tensor, pa[:, n0:n0 + 512],
                                       lambda k: wsb[:, k * D:(k + 1) * D],
                                       lambda k: xbf[:, k * T:(k + 1) * T],
                                       m, n0, 512)
                        c.inc(ins, "te")
                block.tensor(mms)
                te_now = c.counts["te"]

                if proj == 0:
                    def ev_s(scalar, m=m, te_now=te_now):
                        scalar.wait_ge(c.sems["te"], te_now)
                        c.inc(scalar.activation(out=r1[:], in_=pa[:],
                                                func=AF.Relu,
                                                bias=sbias[:, m:m + 1]), "se")
                    block.scalar(ev_s)

                    def ev_v(vector, m=m, te_now=te_now):
                        vector.wait_ge(c.sems["te"], te_now)
                        c.inc(vector.tensor_scalar(
                            out=r2[:], in0=pa[:], scalar1=sbias[:, m:m + 1],
                            scalar2=0.0, op0=ALU.add, op1=ALU.min), "ve")
                    block.vector(ev_v)
                    se_now2 = c.counts["se"]
                    ve_now2 = c.counts["ve"]

                    def ev_s2(scalar, ve_now2=ve_now2):
                        c.wait(scalar, "ve", ve_now2)
                        c.inc(scalar.activation(out=r2[:], in_=r2[:],
                                                func=AF.Exp), "se")
                    block.scalar(ev_s2)
                    se_now3 = c.counts["se"]

                    def ev_v2(vector, m=m, se_now3=se_now3):
                        vector.wait_ge(c.sems["se"], se_now3)
                        c.inc(vector.scalar_tensor_tensor(
                            out=qbf[:, m * T:(m + 1) * T], in0=r2[:],
                            scalar=-1.0, in1=r1[:],
                            op0=ALU.add, op1=ALU.add), "ve")
                    block.vector(ev_v2)
                elif proj == 1:
                    def ev_s(scalar, m=m, te_now=te_now):
                        scalar.wait_ge(c.sems["te"], te_now)
                        c.inc(scalar.activation(
                            out=vbf[:, m * T:(m + 1) * T], in_=pa[:],
                            func=AF.Silu, bias=sbias[:, KC + m:KC + m + 1]),
                            "se")
                    block.scalar(ev_s)
                else:
                    st_now = c.counts["st"]

                    def ev_s(scalar, m=m, te_now=te_now, st_now=st_now):
                        scalar.wait_ge(c.sems["te"], te_now)
                        if m >= 2:
                            scalar.wait_ge(c.sems["st"], (m - 1) * 16)
                        c.inc(scalar.activation(
                            out=uscr[:, (m % 2) * T:(m % 2 + 1) * T],
                            in_=pa[:], func=AF.Silu,
                            bias=sbias[:, 2 * KC + m:2 * KC + m + 1]), "se")
                    block.scalar(ev_s)
                    se_now2 = c.counts["se"]

                    def st_u(s, m=m, se_now2=se_now2):
                        s.wait_ge(c.sems["se"], se_now2)
                        c.inc(s.dma_start(
                            out=u_out[m * 128:(m + 1) * 128, :],
                            in_=uscr[:, (m % 2) * T:(m % 2 + 1) * T]), "st", 16)
                    block.sync(st_u)

        ve_q = c.counts["ve"]
        se_v = c.counts["se"]

        def kv_phase(tensor):
            tensor.wait_ge(c.sems["ve"], ve_q)
            tensor.wait_ge(c.sems["se"], se_v)
        block.tensor(kv_phase)

        for h in range(KC):
            for t in range(8):
                ve_now = c.counts["ve"]
                se_now = c.counts["se"]

                def trs(tensor, h=h, t=t, ve_now=ve_now, se_now=se_now):
                    tensor.wait_ge(c.sems["ve"], ve_now)
                    tensor.wait_ge(c.sems["se"], se_now)
                    c.inc(tensor.transpose(
                        ptr[:, 0:128],
                        qbf[:, h * T + t * 128:h * T + (t + 1) * 128],
                        sident[:]), "te")
                    c.inc(tensor.transpose(
                        ptr[:, 128:256],
                        vbf[:, h * T + t * 128:h * T + (t + 1) * 128],
                        sident[:]), "te")
                block.tensor(trs)
                te_now = c.counts["te"]

                def evk(vector, te_now=te_now):
                    vector.wait_ge(c.sems["te"], te_now)
                    c.inc(vector.tensor_copy(out=ktm[:], in_=ptr[:, 0:128]),
                          "ve")
                block.vector(evk)

                def evv(scalar, te_now=te_now):
                    scalar.wait_ge(c.sems["te"], te_now)
                    c.inc(scalar.copy(out=vtm[:], in_=ptr[:, 128:256]), "se")
                block.scalar(evv)
                ve_now2 = c.counts["ve"]
                se_now2 = c.counts["se"]

                def mmkv(tensor, h=h, t=t, ve_now2=ve_now2, se_now2=se_now2):
                    tensor.wait_ge(c.sems["ve"], ve_now2)
                    tensor.wait_ge(c.sems["se"], se_now2)
                    c.inc(tensor.matmul(
                        pkv[:, (h % 4) * 128:(h % 4 + 1) * 128], ktm[:], vtm[:],
                        start=(t == 0), stop=(t == 7)), "te")
                block.tensor(mmkv)

            te_now = c.counts["te"]

            def evkv(vector, h=h, te_now=te_now):
                vector.wait_ge(c.sems["te"], te_now)
                c.inc(vector.tensor_copy(
                    out=kvsb[:, h * 128:(h + 1) * 128],
                    in_=pkv[:, (h % 4) * 128:(h % 4 + 1) * 128]), "ve")
            block.vector(evkv)
            ve_kv = c.counts["ve"]

            def guard(tensor, ve_kv=ve_kv):
                tensor.wait_ge(c.sems["ve"], ve_kv)
            block.tensor(guard)

        ve_end = c.counts["ve"]
        se_end = c.counts["se"]

        def stores(s):
            s.wait_ge(c.sems["ve"], ve_end)
            s.wait_ge(c.sems["se"], se_end)
            for m in range(KC):
                c.inc(s.dma_start(out=q_out[m * 128:(m + 1) * 128, :],
                                  in_=qbf[:, m * T:(m + 1) * T]), "st", 16)
            c.inc(s.dma_start(out=kv_out, in_=kvsb[:, :]), "st", 16)
        block.sync(stores)


def _wrap(emitter, io_decl, extra=()):
    from contextlib import ExitStack

    def build():
        nc = bass.Bass("TRN2", target_bir_lowering=False)
        io = {}
        for name, shape, dt, kind in io_decl:
            io[name] = nc.dram_tensor(name, shape, dt, kind=kind).ap()
        with ExitStack() as st0:
            c = mk_ctx(nc, st0)
            c.block = st0.enter_context(nc.Block())
            c.new_phase()
            emitter(c, io, *extra)
            st_end = c.counts["st"]

            def fin(g):
                g.wait_ge(c.sems["st"], st_end)
            c.block.gpsimd(fin)
        return nc
    return build


build_l0a = _wrap(emit_l0a, [
    ("x", [D, T], F32, "ExternalInput"),
    ("wqk", [D, D], F32, "ExternalInput"),
    ("wv", [D, D], F32, "ExternalInput"),
    ("wu", [D, D], F32, "ExternalInput"),
    ("bqk", [128, KC], F32, "ExternalInput"),
    ("bv", [128, KC], F32, "ExternalInput"),
    ("bu", [128, KC], F32, "ExternalInput"),
    ("ident", [128, 128], BF16, "ExternalInput"),
    ("q_out", [D, T], BF16, "ExternalOutput"),
    ("u_out", [D, T], F32, "ExternalOutput"),
    ("kv_out", [128, KC * 128], F32, "ExternalOutput"),
])


def _dump(c, src_sb, dst_hbm, N, bf=False):
    block = c.block
    ve0, se0, te0 = c.counts["ve"], c.counts["se"], c.counts["te"]

    def st(s):
        s.wait_ge(c.sems["ve"], ve0)
        s.wait_ge(c.sems["se"], se0)
        s.wait_ge(c.sems["te"], te0)
        for k in range(KC):
            c.inc(s.dma_start(out=dst_hbm[k * 128:(k + 1) * 128, :],
                              in_=src_sb[:, k * N:(k + 1) * N]), "st", 16)
    block.sync(st)
    st1 = c.counts["st"]

    def gate(vector):
        c.wait(vector, "st", st1)
    block.vector(gate)

    def gate2(scalar):
        c.wait(scalar, "st", st1)
    block.scalar(gate2)



def emit_ln(c, src, dst, N, scr, gcol=None, bcol=None, eps=1e-5, src_wait=None):
    block = c.block
    if src_wait:
        def _gate(vector, src_wait=dict(src_wait)):
            for s_, v_ in src_wait.items():
                vector.wait_ge(c.sems[s_], v_)
        block.vector(_gate)
    for k in range(KC):
        ve0, se0, te0 = c.counts["ve"], c.counts["se"], c.counts["te"]

        def cast(vector, k=k, te0=te0):
            if k >= 2:
                vector.wait_ge(c.sems["te"], te0)
            c.inc(vector.tensor_copy(out=scr["obscr"][:, (k % 2) * N:(k % 2 + 1) * N],
                                     in_=src(k)), "ve")
        block.vector(cast)
        ve1 = c.counts["ve"]

        def sq(scalar, k=k, ve1=ve1):
            c.wait(scalar, "ve", ve1)
            c.inc(scalar.activation(out=scr["sqscr"][:, (k % 2) * N:(k % 2 + 1) * N],
                                    in_=scr["obscr"][:, (k % 2) * N:(k % 2 + 1) * N],
                                    func=AF.Square), "se")
        block.scalar(sq)
        se1 = c.counts["se"]

        def red(tensor, k=k, ve1=ve1, se1=se1):
            c.wait(tensor, "ve", ve1)
            c.wait(tensor, "se", se1)
            c.inc(tensor.matmul(scr["pstat"][0:1, 0:N], scr["onescol"],
                                scr["obscr"][:, (k % 2) * N:(k % 2 + 1) * N],
                                start=(k == 0), stop=(k == KC - 1)), "te")
            c.inc(tensor.matmul(scr["pstat"][0:1, N:2 * N], scr["onescol"],
                                scr["sqscr"][:, (k % 2) * N:(k % 2 + 1) * N],
                                start=(k == 0), stop=(k == KC - 1)), "te")
        block.tensor(red)
    te_red = c.counts["te"]

    def rows(vector):
        c.wait(vector, "te", te_red)
        c.inc(vector.tensor_copy(out=scr["rowbuf"][0:1, :], in_=scr["pstat"][0:1, :]),
              "ve")
    block.vector(rows)
    ve_rows = c.counts["ve"]

    def bc(tensor):
        c.wait(tensor, "ve", ve_rows)
        c.inc(tensor.matmul(scr["pb"][:, 0:N], scr["onesrow"],
                            scr["rowbuf"][0:1, 0:N], start=True, stop=True), "te")
        c.inc(tensor.matmul(scr["pb"][:, N:2 * N], scr["onesrow"],
                            scr["rowbuf"][0:1, N:2 * N], start=True, stop=True),
              "te")
    block.tensor(bc)
    te_bc = c.counts["te"]

    def stats_v(vector):
        c.wait(vector, "te", te_bc)
        vector.memset(scr["epscol"][:, :], float(eps))
        c.inc(vector.tensor_scalar_mul(out=scr["meanB"][:, :], in0=scr["pb"][:, 0:N],
                                       scalar1=1.0 / D), "ve")
        c.inc(vector.tensor_scalar_mul(out=scr["rstdB"][:, :], in0=scr["pb"][:, N:2 * N],
                                       scalar1=1.0 / D), "ve")
        c.inc(vector.scalar_tensor_tensor(out=scr["mrstdB"][:, :], in0=scr["meanB"][:, :],
                                          scalar=0.0, in1=scr["meanB"][:, :],
                                          op0=ALU.add, op1=ALU.mult), "ve")
        c.inc(vector.tensor_sub(out=scr["rstdB"][:, :], in0=scr["rstdB"][:, :],
                                in1=scr["mrstdB"][:, :]), "ve")
    block.vector(stats_v)
    ve_st = c.counts["ve"]

    def stats_s(scalar):
        c.wait(scalar, "ve", ve_st)
        c.inc(scalar.activation(out=scr["rstdB"][:, :], in_=scr["rstdB"][:, :], func=AF.Ln,
                                bias=scr["epscol"][:, 0:1]), "se")
        c.inc(scalar.activation(out=scr["rstdB"][:, :], in_=scr["rstdB"][:, :], func=AF.Exp,
                                scale=-0.5), "se")
    block.scalar(stats_s)
    se_st = c.counts["se"]

    def mr(vector):
        c.wait(vector, "se", se_st)
        c.inc(vector.tensor_mul(out=scr["mrstdB"][:, :], in0=scr["meanB"][:, :],
                                in1=scr["rstdB"][:, :]), "ve")
    block.vector(mr)

    for k in range(KC):
        def norm(vector, k=k):
            c.inc(vector.tensor_mul(out=dst(k), in0=src(k), in1=scr["rstdB"][:, :]),
                  "ve")
            c.inc(vector.tensor_sub(out=dst(k), in0=dst(k), in1=scr["mrstdB"][:, :]),
                  "ve")
        block.vector(norm)
        if gcol is not None:
            def aff(vector, k=k):
                c.inc(vector.tensor_scalar(out=dst(k), in0=dst(k),
                                           scalar1=gcol(k), scalar2=bcol(k),
                                           op0=ALU.mult, op1=ALU.add), "ve")
            block.vector(aff)


def make_ln_scratch(ctx, nc, N, tag="", psum_from=None):
    e = ctx.enter_context
    if psum_from is not None:
        pstat = psum_from[0][0:1, 0:2 * N]
        pb = psum_from[1][:, 0:2 * N]
    else:
        pstat = e(nc.psum_tensor(f"ln_ps{tag}", [1, 2 * N], F32))
        pb = e(nc.psum_tensor(f"ln_pb{tag}", [128, 2 * N], F32))
    return {
        "pstat": pstat,
        "pb": pb,
        "obscr": e(nc.sbuf_tensor(f"ln_ob{tag}", [128, 2 * N], BF16)),
        "sqscr": e(nc.sbuf_tensor(f"ln_sq{tag}", [128, 2 * N], BF16)),
        "rowbuf": e(nc.sbuf_tensor(f"ln_row{tag}", [1, 2 * N], F32)),
        "meanB": e(nc.sbuf_tensor(f"ln_mb{tag}", [128, N], F32)),
        "rstdB": e(nc.sbuf_tensor(f"ln_rb{tag}", [128, N], F32)),
        "mrstdB": e(nc.sbuf_tensor(f"ln_mrb{tag}", [128, N], F32)),
        "epscol": e(nc.sbuf_tensor(f"ln_eps{tag}", [128, 1], F32)),
    }



def emit_l0b(c, io, kv_pair=True, cc_wait=None):
    nc = c.nc
    block = c.block
    q, u, x, wo = io["q"], io["u"], io["x"], io["wo"]
    lng, lnb, bo = io["lng"], io["lnb"], io["bo"]
    allones, onesrow_f = io["allones"], io["onesrow_f"]
    x1_out = io["x1_out"]
    debug = False

    N = 512
    from contextlib import ExitStack
    with ExitStack() as ctx:
        e = ctx.enter_context
        qh = e(nc.sbuf_tensor("qh", [128, KC * N], BF16))
        kvbf = e(nc.sbuf_tensor("kvbf", [128, KC * 128], BF16))
        outsb = e(nc.sbuf_tensor("outsb", [128, KC * N], F32))
        kvf = outsb[:, 0:KC * 128]
        kvf2 = outsb[:, KC * 128:2 * KC * 128]
        uh = e(nc.sbuf_tensor("uh", [128, KC * N], F32))
        xh = e(nc.sbuf_tensor("xh", [128, KC * N], F32))
        zbf = e(nc.sbuf_tensor("zbf", [128, KC * N], BF16))
        wsb = e(nc.sbuf_tensor("wsb", [128, KC * 1024], BF16))
        cols = e(nc.sbuf_tensor("cols", [128, 3 * KC], F32))
        sones = e(nc.sbuf_tensor("sones", [128, 128], BF16))
        sonesrow_f = e(nc.sbuf_tensor("sonesrow_f", [1, 128], F32))
        ressb = outsb
        scr = make_ln_scratch(ctx, nc, N)
        scr["onescol"] = sones[:, 0:1]
        scr["onesrow"] = sonesrow_f[:]
        po = e(nc.psum_tensor("po", [128, 2 * N], F32))

        def loads(s):
            if cc_wait:
                s.wait_ge(c.sems[cc_wait], 1)
            if kv_pair:
                c.inc(s.dma_start(out=kvf, in_=io["kv0"]), "lds", 16)
                c.inc(s.dma_start(out=kvf2, in_=io["kv1"]), "lds", 16)
            else:
                c.inc(s.dma_start(out=kvf, in_=io["kvr"]), "lds", 16)
            c.inc(s.dma_start(out=cols[:, 0:KC], in_=lng), "lds", 16)
            c.inc(s.dma_start(out=cols[:, KC:2 * KC], in_=lnb), "lds", 16)
            c.inc(s.dma_start(out=cols[:, 2 * KC:3 * KC], in_=bo), "lds", 16)
            c.inc(s.dma_start(out=sones[:], in_=allones), "lds", 16)
            c.inc(s.dma_start(out=sonesrow_f[:], in_=onesrow_f), "lds", 16)
        block.sync(loads)
        ld_const = c.counts["lds"]

        def addkv(vector):
            c.wait(vector, "lds", 32 if kv_pair else 16)
            if kv_pair:
                c.inc(vector.tensor_add(out=kvbf[:], in0=kvf[:], in1=kvf2[:]),
                      "ve")
            else:
                c.inc(vector.tensor_copy(out=kvbf[:], in_=kvf[:]), "ve")
        block.vector(addkv)
        ve_kv = c.counts["ve"]

        for nh in range(2):
            n0 = nh * N
            ld_h = None

            def loads_h(s, n0=n0):
                for k in range(KC):
                    c.inc(s.dma_start(out=qh[:, k * N:(k + 1) * N],
                                      in_=q[k * 128:(k + 1) * 128, n0:n0 + N]),
                          "lds", 16)
                for k in range(KC):
                    c.inc(s.dma_start(out=uh[:, k * N:(k + 1) * N],
                                      in_=u[k * 128:(k + 1) * 128, n0:n0 + N]),
                          "lds", 16)
                for k in range(KC):
                    c.inc(s.dma_start(out=xh[:, k * N:(k + 1) * N],
                                      in_=x[k * 128:(k + 1) * 128, n0:n0 + N]),
                          "lds", 16)
            te_prev = c.counts["te"]
            ve_prev = c.counts["ve"]

            def loads_h_guard(s, te_prev=te_prev, ve_prev=ve_prev, n0=n0):
                s.wait_ge(c.sems["te"], te_prev)
                s.wait_ge(c.sems["ve"], ve_prev)
                loads_h(s, n0)
            block.sync(loads_h_guard)
            ld_h = c.counts["lds"]

            for h in range(KC):
                ve0, se0 = c.counts["ve"], c.counts["se"]

                def mmh(tensor, h=h, ld_h=ld_h, ve0=ve0, se0=se0):
                    if h == 0:
                        tensor.wait_ge(c.sems["lds"], ld_h)
                        tensor.wait_ge(c.sems["ve"], ve_kv)
                    else:
                        tensor.wait_ge(c.sems["se"], se0)
                    c.inc(tensor.matmul(po[:, (h % 2) * N:(h % 2 + 1) * N],
                                        kvbf[:, h * 128:(h + 1) * 128],
                                        qh[:, h * N:(h + 1) * N],
                                        start=True, stop=True), "te")
                block.tensor(mmh)
                te_h = c.counts["te"]

                def evh(scalar, h=h, te_h=te_h):
                    c.wait(scalar, "te", te_h)
                    c.inc(scalar.copy(out=outsb[:, h * N:(h + 1) * N],
                                      in_=po[:, (h % 2) * N:(h % 2 + 1) * N]),
                          "se")
                block.scalar(evh)

            if debug and nh == 0:
                _dump(c, outsb, dbg[0], N)
            emit_ln(c, lambda k: outsb[:, k * N:(k + 1) * N],
                    lambda k: outsb[:, k * N:(k + 1) * N], N, scr,
                    gcol=lambda k: cols[:, k:k + 1],
                    bcol=lambda k: cols[:, KC + k:KC + k + 1],
                    src_wait={"se": c.counts["se"]})
            se_ln = c.counts["se"]
            ve_ln = c.counts["ve"]

            if debug and nh == 0:
                _dump(c, outsb, dbg[1], N)

            def mulu(vector, se_ln=se_ln):
                c.wait(vector, "se", se_ln)
                c.wait(vector, "lds", ld_h)
                for k in range(KC):
                    c.inc(vector.tensor_mul(out=zbf[:, k * N:(k + 1) * N],
                                            in0=outsb[:, k * N:(k + 1) * N],
                                            in1=uh[:, k * N:(k + 1) * N]), "ve")
            block.vector(mulu)
            ve_z = c.counts["ve"]

            for mh in range(2):
                te_prev2 = c.counts["te"]

                def ldw(g, mh=mh, te_prev2=te_prev2):
                    g.wait_ge(c.sems["te"], te_prev2)
                    for k in range(KC):
                        c.inc(g.dma_start(
                            out=wsb[:, k * 1024:(k + 1) * 1024],
                            in_=wo[k * 128:(k + 1) * 128,
                                   mh * 1024:(mh + 1) * 1024]), "ldg", 16)
                block.gpsimd(ldw)
                ld_w = c.counts["ldg"]

                for mm_ in range(8):
                    m = mh * 8 + mm_
                    ve0 = c.counts["ve"]

                    def mmo(tensor, m=m, mm_=mm_, ld_w=ld_w, ve0=ve0, ve_z=ve_z):
                        if mm_ == 0:
                            tensor.wait_ge(c.sems["ldg"], ld_w)
                            tensor.wait_ge(c.sems["ve"], ve_z)
                        else:
                            tensor.wait_ge(c.sems["ve"], ve0)
                        ins = mm_tiles(c, tensor, po[:, (mm_ % 2) * N:(mm_ % 2 + 1) * N],
                                       lambda k: wsb[:, k * 1024:(k + 1) * 1024],
                                       lambda k: zbf[:, k * N:(k + 1) * N],
                                       mm_, 0, N)
                        c.inc(ins, "te")
                    block.tensor(mmo)
                    te_o = c.counts["te"]

                    def evo(vector, m=m, mm_=mm_, te_o=te_o):
                        c.wait(vector, "te", te_o)
                        c.inc(vector.scalar_tensor_tensor(
                            out=ressb[:, m * N:(m + 1) * N],
                            in0=po[:, (mm_ % 2) * N:(mm_ % 2 + 1) * N],
                            scalar=cols[:, 2 * KC + m:2 * KC + m + 1],
                            in1=xh[:, m * N:(m + 1) * N],
                            op0=ALU.add, op1=ALU.add), "ve")
                    block.vector(evo)

            if debug and nh == 0:
                _dump(c, zbf, dbg[2], N, bf=True)
                _dump(c, ressb, dbg[3], N)
            emit_ln(c, lambda k: ressb[:, k * N:(k + 1) * N],
                    lambda k: ressb[:, k * N:(k + 1) * N], N, scr,
                    src_wait={"ve": c.counts["ve"]})
            ve_pl = c.counts["ve"]
            se_pl = c.counts["se"]

            def store(s, n0=n0, ve_pl=ve_pl, se_pl=se_pl):
                s.wait_ge(c.sems["ve"], ve_pl)
                s.wait_ge(c.sems["se"], se_pl)
                for k in range(KC):
                    c.inc(s.dma_start(out=x1_out[k * 128:(k + 1) * 128, n0:n0 + N],
                                      in_=ressb[:, k * N:(k + 1) * N]), "st", 16)
            block.sync(store)


build_l0b = _wrap(emit_l0b, [
    ("q", [D, T], BF16, "ExternalInput"),
    ("kv0", [128, KC * 128], F32, "ExternalInput"),
    ("kv1", [128, KC * 128], F32, "ExternalInput"),
    ("u", [D, T], F32, "ExternalInput"),
    ("x", [D, T], F32, "ExternalInput"),
    ("wo", [D, D], F32, "ExternalInput"),
    ("lng", [128, KC], F32, "ExternalInput"),
    ("lnb", [128, KC], F32, "ExternalInput"),
    ("bo", [128, KC], F32, "ExternalInput"),
    ("allones", [128, 128], BF16, "ExternalInput"),
    ("onesrow_f", [1, 128], F32, "ExternalInput"),
    ("x1_out", [D, T], F32, "ExternalOutput"),
])


def run_l0b(q_sh, kv_sh, u_sh, x_shards, Wo, bo, ln_g, ln_b):
    nc = build_l0b()
    shared = {
        "wo": np.ascontiguousarray(Wo), "bo": _bias_cols(bo),
        "lng": _bias_cols(ln_g), "lnb": _bias_cols(ln_b),
        "allones": np.ones((128, 128), BF),
        "onesrow_f": np.ones((1, 128), np.float32),
    }
    in_maps = []
    for cix in range(NCORES):
        partner = cix ^ 1
        in_maps.append(dict(shared, q=q_sh[cix], kv0=kv_sh[cix],
                            kv1=kv_sh[partner], u=u_sh[cix], x=x_shards[cix]))
    res = run_bass_kernel_spmd(nc, in_maps, core_ids=list(range(NCORES)))
    return res.results



INTER = 1408
IC = INTER // 128
SH_G = 2


def emit_l0f(c, io):
    nc = c.nc
    block = c.block
    debug = False
    x1, gwt = io["x1"], io["gwt"]
    w1t, w3t, w2t = io["w1t"], io["w3t"], io["w2t"]
    sw1t, sw3t, sw2t = io["sw1t"], io["sw3t"], io["sw2t"]
    ind, ident = io["ind"], io["ident"]
    allones, onesrow_f = io["allones"], io["onesrow_f"]
    x2_out = io["x2_out"]

    N = 512
    from contextlib import ExitStack
    with ExitStack() as ctx:
        e = ctx.enter_context
        xbf = e(nc.sbuf_tensor("xbf", [128, KC * T], BF16))
        yacc = e(nc.sbuf_tensor("yacc", [128, KC * T], F32))
        wbuf = e(nc.sbuf_tensor("wbuf", [128, KC * INTER], BF16))
        hsc = e(nc.sbuf_tensor("hsc", [128, IC * T], BF16))
        hx = e(nc.sbuf_tensor("hx", [128, T], F32))
        gbuf = e(nc.sbuf_tensor("gbuf", [128, KC * 16], BF16))
        stm = e(nc.sbuf_tensor("stm", [128, 8 * 16], F32))
        ptile = e(nc.sbuf_tensor("ptile", [128, 128], F32))
        mk = e(nc.sbuf_tensor("mk", [128, 128], F32))
        rcols = e(nc.sbuf_tensor("rcols", [128, 40], F32))
        wtmb = e(nc.sbuf_tensor("wtmb", [128, 128], BF16))
        wfm = e(nc.sbuf_tensor("wfm", [16, T], BF16))
        sind = e(nc.sbuf_tensor("sind", [16, D], BF16))
        sident = e(nc.sbuf_tensor("sident", [128, 128], BF16))
        sones = e(nc.sbuf_tensor("sones", [128, 128], BF16))
        sonesrow_f = e(nc.sbuf_tensor("sonesrow_f", [1, 128], F32))
        ph = e(nc.psum_tensor("ph", [128, T], F32))
        py = e(nc.psum_tensor("py", [128, T], F32))
        pmisc = e(nc.psum_tensor("pmisc", [128, T], F32))
        pw = e(nc.psum_tensor("pw", [16, 1024], BF16))
        scr = make_ln_scratch(ctx, nc, N, psum_from=(ph, py))
        scr["onescol"] = sones[:, 0:1]
        scr["onesrow"] = sonesrow_f[:]

        def loads_s(s):
            for k in range(KC):
                c.inc(s.dma_start(out=yacc[:, k * T:(k + 1) * T],
                                  in_=x1[k * 128:(k + 1) * 128, :]), "lds", 16)
            c.inc(s.dma_start(out=sind[:], in_=ind), "lds", 16)
            c.inc(s.dma_start(out=sident[:], in_=ident), "lds", 16)
            c.inc(s.dma_start(out=sones[:], in_=allones), "lds", 16)
            c.inc(s.dma_start(out=sonesrow_f[:], in_=onesrow_f), "lds", 16)
        block.sync(loads_s)
        ld_consts = c.counts["lds"]

        def loads_g(g):
            for k in range(KC):
                c.inc(g.dma_start(out=xbf[:, k * T:(k + 1) * T],
                                  in_=x1[k * 128:(k + 1) * 128, :]), "ldg", 16)
            for k in range(KC):
                c.inc(g.dma_start(out=gbuf[:, k * 16:(k + 1) * 16],
                                  in_=gwt[k * 128:(k + 1) * 128, :]), "ldg", 16)
        block.gpsimd(loads_g)
        ld_xg = c.counts["ldg"]

        def rmms(tensor):
            tensor.wait_ge(c.sems["ldg"], ld_xg)
            for t in range(8):
                for k in range(KC):
                    c.inc(tensor.matmul(
                        pmisc[:, t * 16:(t + 1) * 16],
                        xbf[:, k * T + t * 128:k * T + (t + 1) * 128],
                        gbuf[:, k * 16:(k + 1) * 16],
                        start=(k == 0), stop=(k == KC - 1)), "te")
        block.tensor(rmms)
        te_r = c.counts["te"]

        def rv(vector):
            c.wait(vector, "te", te_r)
            c.inc(vector.tensor_copy(out=stm[:], in_=pmisc[:, 0:128]), "ve")
        block.vector(rv)
        ve_r = c.counts["ve"]

        def sl(t):
            return stm[:, t * 16:(t + 1) * 16]

        def rtop(vector):
            for t in range(8):
                c.inc(vector.reduce_max(out=rcols[:, t:t + 1], in_=sl(t),
                                        axis=mybir.AxisListType.X), "ve")
            for t in range(8):
                c.inc(vector.tensor_scalar_mul(out=rcols[:, 8 + t:9 + t],
                                               in0=rcols[:, t:t + 1],
                                               scalar1=-1.0), "ve")
            for t in range(8):
                c.inc(vector.tensor_scalar(out=mk[:, t * 16:(t + 1) * 16],
                                           in0=sl(t),
                                           scalar1=rcols[:, t:t + 1],
                                           scalar2=None, op0=ALU.is_ge), "ve")
            for t in range(8):
                c.inc(vector.scalar_tensor_tensor(
                    out=mk[:, t * 16:(t + 1) * 16],
                    in0=mk[:, t * 16:(t + 1) * 16], scalar=-1e30, in1=sl(t),
                    op0=ALU.mult, op1=ALU.add), "ve")
            for t in range(8):
                c.inc(vector.reduce_max(out=rcols[:, 16 + t:17 + t],
                                        in_=mk[:, t * 16:(t + 1) * 16],
                                        axis=mybir.AxisListType.X), "ve")
        block.vector(rtop)
        ve_t = c.counts["ve"]

        def rexp(scalar):
            c.wait(scalar, "ve", ve_t)
            for t in range(8):
                c.inc(scalar.activation(out=ptile[:, t * 16:(t + 1) * 16],
                                        in_=sl(t), func=AF.Exp,
                                        bias=rcols[:, 8 + t:9 + t]), "se")
        block.scalar(rexp)
        se_t = c.counts["se"]

        def rw(vector):
            c.wait(vector, "se", se_t)
            for t in range(8):
                c.inc(vector.reduce_sum(out=rcols[:, 24 + t:25 + t],
                                        in_=ptile[:, t * 16:(t + 1) * 16],
                                        axis=mybir.AxisListType.X), "ve")
            for t in range(8):
                c.inc(vector.reciprocal(out=rcols[:, 32 + t:33 + t],
                                        in_=rcols[:, 24 + t:25 + t]), "ve")
            for t in range(8):
                c.inc(vector.tensor_scalar(out=mk[:, t * 16:(t + 1) * 16],
                                           in0=sl(t),
                                           scalar1=rcols[:, 16 + t:17 + t],
                                           scalar2=None, op0=ALU.is_ge), "ve")
            for t in range(8):
                c.inc(vector.tensor_mul(out=ptile[:, t * 16:(t + 1) * 16],
                                        in0=ptile[:, t * 16:(t + 1) * 16],
                                        in1=mk[:, t * 16:(t + 1) * 16]), "ve")
            for t in range(8):
                c.inc(vector.tensor_scalar(out=wtmb[:, t * 16:(t + 1) * 16],
                                           in0=ptile[:, t * 16:(t + 1) * 16],
                                           scalar1=rcols[:, 32 + t:33 + t],
                                           scalar2=None, op0=ALU.mult), "ve")
        block.vector(rw)
        ve_t2 = c.counts["ve"]

        def rtr(tensor):
            c.wait(tensor, "ve", ve_t2)
            for t in range(8):
                c.inc(tensor.transpose(pw[:, t * 128:(t + 1) * 128],
                                       wtmb[:, t * 16:(t + 1) * 16],
                                       sident[:]), "te")
        block.tensor(rtr)
        te_t = c.counts["te"]

        def rev(scalar):
            c.wait(scalar, "te", te_t)
            c.inc(scalar.copy(out=wfm[:], in_=pw[:]), "se")
        block.scalar(rev)

        se_wfm = c.counts["se"]
        if debug:
            ve_d, se_d = c.counts["ve"], c.counts["se"]

            def dmp0(s, ve_d=ve_d, se_d=se_d):
                s.wait_ge(c.sems["ve"], ve_d)
                s.wait_ge(c.sems["se"], se_d)
                c.inc(s.dma_start(out=dstm, in_=stm[:]), "st", 16)
                c.inc(s.dma_start(out=dwfm, in_=wfm[:]), "st", 16)
            block.sync(dmp0)

        for ep in range(18):
            gated = ep < 16
            eix = ep if gated else None
            g = None if gated else ep - 16
            nic = IC

            if gated:
                ve_pm = c.counts["ve"]
                se_pm = c.counts["se"]

                def bca(tensor, eix=eix, ve_pm=ve_pm, se_pm=se_pm):
                    tensor.wait_ge(c.sems["se"], se_wfm)
                    tensor.wait_ge(c.sems["ve"], ve_pm)
                    tensor.wait_ge(c.sems["se"], se_pm)
                    for n0 in (0, N):
                        c.inc(tensor.matmul(
                            pmisc[:, n0:n0 + N],
                            sind[:, eix * 128:(eix + 1) * 128],
                            wfm[:, n0:n0 + N], start=True, stop=True), "te")
                block.tensor(bca)

            te_prevC = c.counts["te"]

            def lw1(gp, ep=ep, g=g, te_prevC=te_prevC):
                gp.wait_ge(c.sems["te"], te_prevC)
                for k in range(KC):
                    if ep < 16:
                        src = w1t[ep, k * 128:(k + 1) * 128, :]
                    else:
                        src = sw1t[k * 128:(k + 1) * 128,
                                   g * INTER:(g + 1) * INTER]
                    c.inc(gp.dma_start(out=wbuf[:, k * INTER:(k + 1) * INTER],
                                       in_=src), "ldg", 16)
            block.gpsimd(lw1)
            ld_w1 = c.counts["ldg"]

            for i in range(nic):
                se_pa = c.counts["se"]

                def mma(tensor, i=i, ld_w1=ld_w1, se_pa=se_pa):
                    tensor.wait_ge(c.sems["ldg"], ld_w1)
                    tensor.wait_ge(c.sems["se"], se_pa)
                    for n0 in (0, N):
                        ins = mm_tiles(c, tensor, ph[:, n0:n0 + N],
                                       lambda k: wbuf[:, k * INTER:(k + 1) * INTER],
                                       lambda k: xbf[:, k * T:(k + 1) * T],
                                       i, n0, N)
                        c.inc(ins, "te")
                block.tensor(mma)
                te_a = c.counts["te"]

                def eva(scalar, i=i, te_a=te_a):
                    scalar.wait_ge(c.sems["te"], te_a)
                    c.inc(scalar.activation(out=hsc[:, i * T:(i + 1) * T],
                                            in_=ph[:], func=AF.Silu), "se")
                block.scalar(eva)

            te_prevA = c.counts["te"]

            def lw3(gp, ep=ep, g=g, te_prevA=te_prevA):
                gp.wait_ge(c.sems["te"], te_prevA)
                for k in range(KC):
                    if ep < 16:
                        src = w3t[ep, k * 128:(k + 1) * 128, :]
                    else:
                        src = sw3t[k * 128:(k + 1) * 128,
                                   g * INTER:(g + 1) * INTER]
                    c.inc(gp.dma_start(out=wbuf[:, k * INTER:(k + 1) * INTER],
                                       in_=src), "ldg", 16)
            block.gpsimd(lw3)
            ld_w3 = c.counts["ldg"]
            se_afterA = c.counts["se"]

            for i in range(nic):
                ve_pb = c.counts["ve"]

                def mmb(tensor, i=i, ld_w3=ld_w3, ve_pb=ve_pb,
                        se_afterA=se_afterA):
                    tensor.wait_ge(c.sems["ldg"], ld_w3)
                    tensor.wait_ge(c.sems["ve"], ve_pb)
                    if i == 0:
                        tensor.wait_ge(c.sems["se"], se_afterA)
                    for n0 in (0, N):
                        ins = mm_tiles(c, tensor, ph[:, n0:n0 + N],
                                       lambda k: wbuf[:, k * INTER:(k + 1) * INTER],
                                       lambda k: xbf[:, k * T:(k + 1) * T],
                                       i, n0, N)
                        c.inc(ins, "te")
                block.tensor(mmb)
                te_b = c.counts["te"]

                def evb(vector, i=i, te_b=te_b, gated=gated):
                    vector.wait_ge(c.sems["te"], te_b)
                    if gated:
                        c.inc(vector.tensor_mul(out=hx[:], in0=ph[:],
                                                in1=hsc[:, i * T:(i + 1) * T]),
                              "ve")
                        c.inc(vector.tensor_mul(out=hsc[:, i * T:(i + 1) * T],
                                                in0=hx[:], in1=pmisc[:]), "ve")
                    else:
                        c.inc(vector.tensor_mul(out=hsc[:, i * T:(i + 1) * T],
                                                in0=ph[:],
                                                in1=hsc[:, i * T:(i + 1) * T]),
                              "ve")
                block.vector(evb)

            te_prevB = c.counts["te"]
            ve_hsc = c.counts["ve"]

            def lw2(gp, ep=ep, g=g, te_prevB=te_prevB):
                gp.wait_ge(c.sems["te"], te_prevB)
                for i in range(IC):
                    if ep < 16:
                        src = w2t[ep, i * 128:(i + 1) * 128, :]
                    else:
                        src = sw2t[(g * IC + i) * 128:(g * IC + i + 1) * 128, :]
                    c.inc(gp.dma_start(out=wbuf[:, i * D:(i + 1) * D],
                                       in_=src), "ldg", 16)
            block.gpsimd(lw2)
            ld_w2 = c.counts["ldg"]

            for m in range(KC):
                ve_pc = c.counts["ve"]

                def mmc(tensor, m=m, ld_w2=ld_w2, ve_pc=ve_pc, ve_hsc=ve_hsc):
                    tensor.wait_ge(c.sems["ldg"], ld_w2)
                    tensor.wait_ge(c.sems["ve"], max(ve_pc, ve_hsc))
                    for n0 in (0, N):
                        ins = mm_tiles(c, tensor, py[:, n0:n0 + N],
                                       lambda i: wbuf[:, i * D:(i + 1) * D],
                                       lambda i: hsc[:, i * T:(i + 1) * T],
                                       m, n0, N, kcs=range(IC))
                        c.inc(ins, "te")
                block.tensor(mmc)
                te_c = c.counts["te"]

                def evc(vector, m=m, te_c=te_c):
                    vector.wait_ge(c.sems["te"], te_c)
                    if m == 0:
                        vector.wait_ge(c.sems["lds"], 16 * KC)
                    c.inc(vector.tensor_add(out=yacc[:, m * T:(m + 1) * T],
                                            in0=py[:],
                                            in1=yacc[:, m * T:(m + 1) * T]),
                          "ve")
                block.vector(evc)

            if debug and ep == 0:
                ve_d, se_d, te_d, st_d0 = (c.counts["ve"], c.counts["se"],
                                           c.counts["te"], c.counts["st"])

                def dmp1(s, ve_d=ve_d, se_d=se_d, te_d=te_d):
                    s.wait_ge(c.sems["ve"], ve_d)
                    s.wait_ge(c.sems["se"], se_d)
                    s.wait_ge(c.sems["te"], te_d)
                    for i in range(IC):
                        c.inc(s.dma_start(out=dhsc[:, i * T:(i + 1) * T],
                                          in_=hsc[:, i * T:(i + 1) * T]),
                              "st", 16)
                    for k in range(KC):
                        c.inc(s.dma_start(out=dyac[k * 128:(k + 1) * 128, :],
                                          in_=yacc[:, k * T:(k + 1) * T]),
                              "st", 16)
                block.sync(dmp1)
                st_d1 = c.counts["st"]

                def dgate_v(vector, st_d1=st_d1):
                    vector.wait_ge(c.sems["st"], st_d1)
                block.vector(dgate_v)

                def dgate_t(tensor, st_d1=st_d1):
                    tensor.wait_ge(c.sems["st"], st_d1)
                block.tensor(dgate_t)

                def dgate_s(scalar, st_d1=st_d1):
                    scalar.wait_ge(c.sems["st"], st_d1)
                block.scalar(dgate_s)

        for nh in range(2):
            n0 = nh * N
            emit_ln(c, lambda k: yacc[:, k * T + n0:k * T + n0 + N],
                    lambda k: yacc[:, k * T + n0:k * T + n0 + N], N, scr,
                    src_wait={"ve": c.counts["ve"]})
            ve_pl = c.counts["ve"]

            def store(s, n0=n0, ve_pl=ve_pl):
                s.wait_ge(c.sems["ve"], ve_pl)
                for k in range(KC):
                    c.inc(s.dma_start(out=x2_out[k * 128:(k + 1) * 128,
                                                 n0:n0 + N],
                                      in_=yacc[:, k * T + n0:k * T + n0 + N]),
                          "st", 16)
            block.sync(store)


build_l0f = _wrap(emit_l0f, [
    ("x1", [D, T], F32, "ExternalInput"),
    ("gwt", [D, 16], F32, "ExternalInput"),
    ("w1t", [16, D, INTER], F32, "ExternalInput"),
    ("w3t", [16, D, INTER], F32, "ExternalInput"),
    ("w2t", [16, INTER, D], F32, "ExternalInput"),
    ("sw1t", [D, 2 * INTER], F32, "ExternalInput"),
    ("sw3t", [D, 2 * INTER], F32, "ExternalInput"),
    ("sw2t", [2 * INTER, D], F32, "ExternalInput"),
    ("ind", [16, D], BF16, "ExternalInput"),
    ("ident", [128, 128], BF16, "ExternalInput"),
    ("allones", [128, 128], BF16, "ExternalInput"),
    ("onesrow_f", [1, 128], F32, "ExternalInput"),
    ("x2_out", [D, T], F32, "ExternalOutput"),
])


def _make_ind():
    ind = np.zeros((16, D), BF)
    for e_ in range(16):
        ind[e_, e_ * 128:(e_ + 1) * 128] = 1
    return ind


def run_l0f(x1_sh, gate_w, w1, w2, w3, sw1, sw2, sw3):
    nc = build_l0f()
    shared = {
        "gwt": np.ascontiguousarray(gate_w.T),
        "w1t": np.ascontiguousarray(w1.transpose(0, 2, 1)),
        "w3t": np.ascontiguousarray(w3.transpose(0, 2, 1)),
        "w2t": np.ascontiguousarray(w2.transpose(0, 2, 1)),
        "sw1t": np.ascontiguousarray(sw1.T),
        "sw3t": np.ascontiguousarray(sw3.T),
        "sw2t": np.ascontiguousarray(sw2.T),
        "ind": _make_ind(),
        "ident": _IDENT_BF,
        "allones": np.ones((128, 128), BF),
        "onesrow_f": np.ones((1, 128), np.float32),
    }
    in_maps = [dict(shared, x1=x1_sh[cix]) for cix in range(NCORES)]
    res = run_bass_kernel_spmd(nc, in_maps, core_ids=list(range(NCORES)))
    return res.results



CH = 256
NCH = T // CH


def emit_l1a(c, io):
    nc = c.nc
    block = c.block
    x2, wqkvt = io["x2"], io["wqkvt"]
    cosf, sinf, rtc, maskt = io["cosf"], io["sinf"], io["rtc"], io["maskt"]
    allones, ident = io["allones"], io["ident"]
    q_out, num_out, den_out = io["q_out"], io["num_out"], io["den_out"]
    s_out, z_out = io["s_out"], io["z_out"]

    from contextlib import ExitStack
    with ExitStack() as ctx:
        e = ctx.enter_context
        xbf = e(nc.sbuf_tensor("xbf", [128, KC * T], BF16))
        wg = e(nc.sbuf_tensor("wg", [128, KC * 512], BF16))
        qsb = e(nc.sbuf_tensor("qsb", [128, KC * T], BF16))
        ksb = e(nc.sbuf_tensor("ksb", [128, KC * T], BF16))
        vsb = e(nc.sbuf_tensor("vsb", [128, KC * T], BF16))
        scos = e(nc.sbuf_tensor("scos", [128, T], F32))
        ssin = e(nc.sbuf_tensor("ssin", [128, T], F32))
        srt = e(nc.sbuf_tensor("srt", [128, 128], BF16))
        smask = e(nc.sbuf_tensor("smaskt", [128, 2 * CH], BF16))
        sident = e(nc.sbuf_tensor("sident", [128, 128], BF16))
        sones = e(nc.sbuf_tensor("sones", [128, 128], BF16))
        qb = e(nc.sbuf_tensor("qb", [128, T], BF16))
        s1 = e(nc.sbuf_tensor("s1", [128, T], F32))
        s2 = e(nc.sbuf_tensor("s2", [128, T], F32))
        ktm = e(nc.sbuf_tensor("ktm", [128, 2 * 128], BF16))
        vtm = e(nc.sbuf_tensor("vtm", [128, 2 * 128], BF16))
        attm = e(nc.sbuf_tensor("attm", [128, 2 * CH], BF16))
        sloc = e(nc.sbuf_tensor("sloc", [128, KC * 128], F32))
        sbf = e(nc.sbuf_tensor("sbf", [128, KC * 128], BF16))
        zloc = e(nc.sbuf_tensor("zloc", [128, KC], F32))
        zbf = e(nc.sbuf_tensor("zbf", [128, KC], BF16))
        ztmp = e(nc.sbuf_tensor("ztmp", [128, KC], F32))
        numscr = e(nc.sbuf_tensor("numscr", [128, 2 * CH], F32))
        denscr = e(nc.sbuf_tensor("denscr", [1, 2 * CH], F32))
        pa = e(nc.psum_tensor("pa", [128, T], F32))
        pr = e(nc.psum_tensor("pr", [128, T], F32))
        patt = e(nc.psum_tensor("patt", [128, 2 * CH], F32))
        pmix = e(nc.psum_tensor("pmix", [128, T], F32))
        ptr = e(nc.psum_tensor("ptr", [128, 4 * 128], BF16))
        pnum = pmix[:, 0:CH]
        pS = pmix[:, CH:CH + 128]
        pden = pmix[0:1, 512:512 + CH]

        def loads_s(s):
            c.inc(s.dma_start(out=scos[:], in_=cosf), "lds", 16)
            c.inc(s.dma_start(out=ssin[:], in_=sinf), "lds", 16)
            c.inc(s.dma_start(out=srt[:], in_=rtc), "lds", 16)
            c.inc(s.dma_start(out=smask[:], in_=maskt), "lds", 16)
            c.inc(s.dma_start(out=sident[:], in_=ident), "lds", 16)
            c.inc(s.dma_start(out=sones[:], in_=allones), "lds", 16)
        block.sync(loads_s)
        ld_c = c.counts["lds"]

        def loads_g(g):
            for k in range(KC):
                c.inc(g.dma_start(out=xbf[:, k * T:(k + 1) * T],
                                  in_=x2[k * 128:(k + 1) * 128, :]), "ldg", 16)
        block.gpsimd(loads_g)

        def init_v(vector):
            vector.memset(sloc[:, :], 0.0)
            vector.memset(zloc[:, :], 0.0)
            c.inc(vector.tensor_copy(out=sbf[:, :], in_=sloc[:, :]), "ve")
            c.inc(vector.tensor_copy(out=zbf[:, :], in_=zloc[:, :]), "ve")
        block.vector(init_v)

        for g in range(12):
            te_prev = c.counts["te"]

            def lw(gp, g=g, te_prev=te_prev):
                gp.wait_ge(c.sems["te"], te_prev)
                for k in range(KC):
                    c.inc(gp.dma_start(
                        out=wg[:, k * 512:(k + 1) * 512],
                        in_=wqkvt[k * 128:(k + 1) * 128,
                                  g * 512:(g + 1) * 512]), "ldg", 16)
            block.gpsimd(lw)
            ld_w = c.counts["ldg"]

            for mm_ in range(4):
                mg = g * 4 + mm_
                ve_now = c.counts["ve"]
                se_now = c.counts["se"]

                def mmp(tensor, mm_=mm_, ld_w=ld_w, ve_now=ve_now, se_now=se_now):
                    tensor.wait_ge(c.sems["ldg"], ld_w)
                    tensor.wait_ge(c.sems["ve"], ve_now)
                    tensor.wait_ge(c.sems["se"], se_now)
                    for n0 in (0, 512):
                        ins = mm_tiles(c, tensor, pa[:, n0:n0 + 512],
                                       lambda k: wg[:, k * 512:(k + 1) * 512],
                                       lambda k: xbf[:, k * T:(k + 1) * T],
                                       mm_, n0, 512)
                        c.inc(ins, "te")
                block.tensor(mmp)
                te_p = c.counts["te"]

                if mg >= 32:
                    h = mg - 32

                    def evv(scalar, h=h, te_p=te_p):
                        scalar.wait_ge(c.sems["te"], te_p)
                        c.inc(scalar.copy(out=vsb[:, h * T:(h + 1) * T],
                                          in_=pa[:]), "se")
                    block.scalar(evv)
                else:
                    h = mg % KC
                    dst = qsb if mg < KC else ksb

                    def ev1(scalar, te_p=te_p):
                        scalar.wait_ge(c.sems["te"], te_p)
                        c.inc(scalar.copy(out=qb[:], in_=pa[:]), "se")
                    block.scalar(ev1)
                    se_1 = c.counts["se"]

                    def mmr(tensor, se_1=se_1):
                        tensor.wait_ge(c.sems["se"], se_1)
                        for n0 in (0, 512):
                            c.inc(tensor.matmul(pr[:, n0:n0 + 512], srt[:],
                                                qb[:, n0:n0 + 512],
                                                start=True, stop=True), "te")
                    block.tensor(mmr)
                    te_r2 = c.counts["te"]

                    def ev2(vector, te_r2=te_r2):
                        vector.wait_ge(c.sems["te"], te_r2)
                        c.inc(vector.tensor_mul(out=s1[:], in0=pa[:],
                                                in1=scos[:]), "ve")
                        c.inc(vector.tensor_mul(out=s2[:], in0=pr[:],
                                                in1=ssin[:]), "ve")
                        c.inc(vector.tensor_add(out=s1[:], in0=s1[:],
                                                in1=s2[:]), "ve")
                    block.vector(ev2)
                    ve_2 = c.counts["ve"]

                    def ev3(scalar, ve_2=ve_2):
                        c.wait(scalar, "ve", ve_2)
                        c.inc(scalar.activation(out=s2[:], in_=s1[:],
                                                func=AF.Relu), "se")
                    block.scalar(ev3)
                    se_3 = c.counts["se"]

                    def ev4(vector, se_3=se_3):
                        c.wait(vector, "se", se_3)
                        c.inc(vector.tensor_scalar_min(out=s1[:], in0=s1[:],
                                                       scalar1=0.0), "ve")
                    block.vector(ev4)
                    ve_4 = c.counts["ve"]

                    def ev5(scalar, ve_4=ve_4):
                        c.wait(scalar, "ve", ve_4)
                        c.inc(scalar.activation(out=s1[:], in_=s1[:],
                                                func=AF.Exp), "se")
                    block.scalar(ev5)
                    se_5 = c.counts["se"]

                    def ev6(vector, h=h, dst=dst, se_5=se_5):
                        c.wait(vector, "se", se_5)
                        c.inc(vector.tensor_add(out=dst[:, h * T:(h + 1) * T],
                                                in0=s1[:], in1=s2[:]), "ve")
                    block.vector(ev6)

        ve_qkv = c.counts["ve"]
        se_qkv = c.counts["se"]

        def scangate(tensor):
            tensor.wait_ge(c.sems["ve"], ve_qkv)
            tensor.wait_ge(c.sems["se"], se_qkv)
            tensor.wait_ge(c.sems["lds"], ld_c)
        block.tensor(scangate)

        for h in range(KC):
            for ci in range(NCH):
                c0 = ci * CH
                ve_a = c.counts["ve"]
                se_a = c.counts["se"]

                def trs(tensor, h=h, c0=c0, ve_a=ve_a, se_a=se_a):
                    tensor.wait_ge(c.sems["ve"], ve_a)
                    tensor.wait_ge(c.sems["se"], se_a)
                    for lb in range(2):
                        c.inc(tensor.transpose(
                            ptr[:, lb * 128:(lb + 1) * 128],
                            ksb[:, h * T + c0 + lb * 128:h * T + c0 + (lb + 1) * 128],
                            sident[:]), "te")
                        c.inc(tensor.transpose(
                            ptr[:, 256 + lb * 128:256 + (lb + 1) * 128],
                            vsb[:, h * T + c0 + lb * 128:h * T + c0 + (lb + 1) * 128],
                            sident[:]), "te")
                block.tensor(trs)
                te_tr = c.counts["te"]

                def evtr(vector, te_tr=te_tr):
                    vector.wait_ge(c.sems["te"], te_tr)
                    c.inc(vector.tensor_copy(out=ktm[:], in_=ptr[:, 0:256]),
                          "ve")
                block.vector(evtr)

                def evtr2(scalar, te_tr=te_tr):
                    scalar.wait_ge(c.sems["te"], te_tr)
                    c.inc(scalar.copy(out=vtm[:], in_=ptr[:, 256:512]), "se")
                block.scalar(evtr2)
                ve_tr = c.counts["ve"]
                se_tr = c.counts["se"]

                def mmatt(tensor, h=h, c0=c0, ve_tr=ve_tr, se_tr=se_tr):
                    tensor.wait_ge(c.sems["ve"], ve_tr)
                    tensor.wait_ge(c.sems["se"], se_tr)
                    for mb in range(2):
                        c.inc(tensor.matmul(
                            patt[:, mb * CH:(mb + 1) * CH],
                            ksb[:, h * T + c0 + mb * 128:h * T + c0 + (mb + 1) * 128],
                            qsb[:, h * T + c0:h * T + c0 + CH],
                            start=True, stop=True), "te")
                block.tensor(mmatt)
                te_att = c.counts["te"]

                def evatt(vector, te_att=te_att):
                    vector.wait_ge(c.sems["te"], te_att)
                    for mb in range(2):
                        c.inc(vector.tensor_mul(
                            out=attm[:, mb * CH:(mb + 1) * CH],
                            in0=patt[:, mb * CH:(mb + 1) * CH],
                            in1=smask[:, mb * CH:(mb + 1) * CH]), "ve")
                    c.inc(vector.reduce_sum(out=ztmp[:, 0:1],
                                            in_=ksb[:, h * T + c0:h * T + c0 + CH],
                                            axis=mybir.AxisListType.X), "ve")
                block.vector(evatt)
                ve_att = c.counts["ve"]

                def mmnum(tensor, h=h, c0=c0, ve_att=ve_att):
                    tensor.wait_ge(c.sems["ve"], ve_att)
                    for mb in range(2):
                        c.inc(tensor.matmul(
                            pnum, vtm[:, mb * 128:(mb + 1) * 128],
                            attm[:, mb * CH:(mb + 1) * CH],
                            start=(mb == 0), stop=False), "te")
                    c.inc(tensor.matmul(
                        pnum, sbf[:, h * 128:(h + 1) * 128],
                        qsb[:, h * T + c0:h * T + c0 + CH],
                        start=False, stop=True), "te")
                    for mb in range(2):
                        c.inc(tensor.matmul(
                            pden, sones[:, 0:1],
                            attm[:, mb * CH:(mb + 1) * CH],
                            start=(mb == 0), stop=False), "te")
                    c.inc(tensor.matmul(
                        pden, zbf[:, h:h + 1],
                        qsb[:, h * T + c0:h * T + c0 + CH],
                        start=False, stop=True), "te")
                    for lb in range(2):
                        c.inc(tensor.matmul(
                            pS, ktm[:, lb * 128:(lb + 1) * 128],
                            vtm[:, lb * 128:(lb + 1) * 128],
                            start=(lb == 0), stop=(lb == 1)), "te")
                block.tensor(mmnum)
                te_n = c.counts["te"]

                st_cap = c.counts["st"]

                def evn(vector, h=h, ci=ci, te_n=te_n, st_cap=st_cap):
                    vector.wait_ge(c.sems["te"], te_n)
                    vector.wait_ge(c.sems["st"], st_cap)
                    c.inc(vector.tensor_copy(
                        out=numscr[:, (ci % 2) * CH:(ci % 2 + 1) * CH],
                        in_=pnum), "ve")
                    c.inc(vector.tensor_add(out=sloc[:, h * 128:(h + 1) * 128],
                                            in0=pS,
                                            in1=sloc[:, h * 128:(h + 1) * 128]),
                          "ve")
                    c.inc(vector.tensor_add(out=zloc[:, h:h + 1],
                                            in0=ztmp[:, 0:1],
                                            in1=zloc[:, h:h + 1]), "ve")
                    c.inc(vector.tensor_copy(
                        out=denscr[:, (ci % 2) * CH:(ci % 2 + 1) * CH],
                        in_=pden), "ve")
                block.vector(evn)
                ve_n = c.counts["ve"]

                def evbf(scalar, h=h, ve_n=ve_n):
                    c.wait(scalar, "ve", ve_n)
                    c.inc(scalar.copy(out=sbf[:, h * 128:(h + 1) * 128],
                                      in_=sloc[:, h * 128:(h + 1) * 128]), "se")
                    c.inc(scalar.copy(out=zbf[:, h:h + 1],
                                      in_=zloc[:, h:h + 1]), "se")
                block.scalar(evbf)

                def stn(s, h=h, ci=ci, c0=c0, ve_n=ve_n):
                    s.wait_ge(c.sems["ve"], ve_n)
                    c.inc(s.dma_start(
                        out=num_out[h * 128:(h + 1) * 128, c0:c0 + CH],
                        in_=numscr[:, (ci % 2) * CH:(ci % 2 + 1) * CH]),
                        "st", 16)
                    c.inc(s.dma_start(
                        out=den_out[0:1, h * T + c0:h * T + c0 + CH],
                        in_=denscr[:, (ci % 2) * CH:(ci % 2 + 1) * CH]),
                        "st", 16)
                block.sync(stn)

        if "mcol" in io:
            smcol = e(nc.sbuf_tensor("smcol", [128, 1], F32))

            def scale_sz(vector):
                c.wait(vector, "lds")
                c.wait(vector, "se")
                c.inc(vector.tensor_scalar_mul(out=sloc[:, :], in0=sloc[:, :],
                                               scalar1=smcol[:, 0:1]), "ve")
                c.inc(vector.tensor_scalar_mul(out=zloc[:, :], in0=zloc[:, :],
                                               scalar1=smcol[:, 0:1]), "ve")

            def ld_m(s):
                c.inc(s.dma_start(out=smcol[:], in_=io["mcol"]), "lds", 16)
            block.sync(ld_m)
            block.vector(scale_sz)
        ve_end = c.counts["ve"]
        se_end = c.counts["se"]

        def stores(s):
            s.wait_ge(c.sems["ve"], ve_end)
            s.wait_ge(c.sems["se"], se_end)
            for h in range(KC):
                c.inc(s.dma_start(out=q_out[h * 128:(h + 1) * 128, :],
                                  in_=qsb[:, h * T:(h + 1) * T]), "st", 16)
            c.inc(s.dma_start(out=s_out, in_=sloc[:, :]), "st", 16)
            c.inc(s.dma_start(out=z_out, in_=zloc[:, :]), "st", 16)
        block.sync(stores)


build_l1a = _wrap(emit_l1a, [
    ("x2", [D, T], F32, "ExternalInput"),
    ("wqkvt", [D, 3 * D], F32, "ExternalInput"),
    ("cosf", [128, T], F32, "ExternalInput"),
    ("sinf", [128, T], F32, "ExternalInput"),
    ("rtc", [128, 128], BF16, "ExternalInput"),
    ("maskt", [128, 2 * CH], BF16, "ExternalInput"),
    ("allones", [128, 128], BF16, "ExternalInput"),
    ("ident", [128, 128], BF16, "ExternalInput"),
    ("q_out", [D, T], BF16, "ExternalOutput"),
    ("num_out", [D, T], F32, "ExternalOutput"),
    ("den_out", [1, KC * T], F32, "ExternalOutput"),
    ("s_out", [128, KC * 128], F32, "ExternalOutput"),
    ("z_out", [128, KC], F32, "ExternalOutput"),
])


def _rope_tables(t0):
    pos = np.arange(t0, t0 + T, dtype=np.float64)
    inv = 1.0 / (10000.0 ** (np.arange(0, 128, 2) / 128.0))
    fr = np.outer(pos, inv)
    emb = np.concatenate([fr, fr], axis=-1)
    return (np.ascontiguousarray(np.cos(emb).T.astype(np.float32)),
            np.ascontiguousarray(np.sin(emb).T.astype(np.float32)))


def _rt_const():
    rt = np.zeros((128, 128), np.float32)
    for i in range(64):
        rt[2 * i + 1, 2 * i] = -1.0
        rt[2 * i, 2 * i + 1] = 1.0
    return rt.astype(BF)


def _maskt_const():
    mk = np.zeros((128, 2 * CH), np.float32)
    for mb in range(2):
        for m_ in range(128):
            gm = mb * 128 + m_
            mk[m_, mb * CH + gm:(mb + 1) * CH] = 1.0
    return mk.astype(BF)


def run_l1a(x2_sh, W_qkv):
    nc = build_l1a()
    shared = {
        "wqkvt": np.ascontiguousarray(W_qkv.T),
        "rtc": _rt_const(), "maskt": _maskt_const(),
        "allones": np.ones((128, 128), BF), "ident": _IDENT_BF,
    }
    in_maps = []
    for cix in range(NCORES):
        t0 = (cix % 2) * T
        cosf, sinf = _rope_tables(t0)
        in_maps.append(dict(shared, x2=x2_sh[cix], cosf=cosf, sinf=sinf))
    res = run_bass_kernel_spmd(nc, in_maps, core_ids=list(range(NCORES)))
    return res.results



def emit_l1b(c, io, fcol=False, cc_wait=None):
    nc = c.nc
    block = c.block
    num, den, q = io["num"], io["den"], io["q"]
    scorr, zcorr = io["scorr"], io["zcorr"]
    woutt, x2 = io["woutt"], io["x2"]
    allones, onesrow_f = io["allones"], io["onesrow_f"]
    x3_out = io["x3_out"]

    N = 512
    from contextlib import ExitStack
    with ExitStack() as ctx:
        e = ctx.enter_context
        qsb = e(nc.sbuf_tensor("qsb", [128, KC * T], BF16))
        aout = e(nc.sbuf_tensor("aout", [128, KC * T], BF16))
        scb = e(nc.sbuf_tensor("scb", [128, KC * 128], BF16))
        zcb = e(nc.sbuf_tensor("zcb", [128, KC], BF16))
        numsb = e(nc.sbuf_tensor("numsb", [128, 2 * CH], F32))
        nf = e(nc.sbuf_tensor("nf", [128, CH], F32))
        densb = e(nc.sbuf_tensor("densb", [1, T], F32))
        drow = e(nc.sbuf_tensor("drow", [1, 2 * CH], F32))
        rb = e(nc.sbuf_tensor("rb", [128, CH], F32))
        wsb = e(nc.sbuf_tensor("wsb", [128, KC * 1024], BF16))
        xh = e(nc.sbuf_tensor("xh", [128, KC * N], F32))
        ressb = e(nc.sbuf_tensor("ressb", [128, KC * N], F32))
        sones = e(nc.sbuf_tensor("sones", [128, 128], BF16))
        sonesrow_f = e(nc.sbuf_tensor("sonesrow_f", [1, 128], F32))
        pcor = e(nc.psum_tensor("pcor", [128, T], F32))
        po = e(nc.psum_tensor("po", [128, 2 * N], F32))
        scr = make_ln_scratch(ctx, nc, N, psum_from=(pcor, po))
        scr["onescol"] = sones[:, 0:1]
        scr["onesrow"] = sonesrow_f[:]
        pc_ = pcor[:, 0:CH]
        pd_ = pcor[0:1, CH:2 * CH]
        pb_ = pcor[:, 512:512 + CH]

        def loads_g(g):
            if cc_wait:
                for s_ in cc_wait:
                    g.wait_ge(c.sems[s_], 1)
            c.inc(g.dma_start(out=scb[:, :], in_=scorr), "ldg", 16)
            c.inc(g.dma_start(out=zcb[:, :], in_=zcorr), "ldg", 16)
        block.gpsimd(loads_g)
        if fcol:
            sfcol = e(nc.sbuf_tensor("sfcol", [128, 1], F32))

            def ld_f(s):
                c.inc(s.dma_start(out=sfcol[:], in_=io["fcol"]), "lds", 16)
            block.sync(ld_f)

            def scale_f(vector):
                c.wait(vector, "ldg")
                c.wait(vector, "lds")
                c.inc(vector.tensor_scalar_mul(out=scb[:, :], in0=scb[:, :],
                                               scalar1=sfcol[:, 0:1]), "ve")
                c.inc(vector.tensor_scalar_mul(out=zcb[:, :], in0=zcb[:, :],
                                               scalar1=sfcol[:, 0:1]), "ve")
            block.vector(scale_f)

        def loads_s(s):
            for k in range(KC):
                c.inc(s.dma_start(out=qsb[:, k * T:(k + 1) * T],
                                  in_=q[k * 128:(k + 1) * 128, :]), "lds", 16)
            c.inc(s.dma_start(out=sones[:], in_=allones), "lds", 16)
            c.inc(s.dma_start(out=sonesrow_f[:], in_=onesrow_f), "lds", 16)
        block.sync(loads_s)
        ld_q = c.counts["lds"]

        for h in range(KC):
            ve_h = c.counts["ve"]

            def ldh(s, h=h, ve_h=ve_h):
                s.wait_ge(c.sems["ve"], ve_h)
                c.inc(s.dma_start(out=densb[:],
                                  in_=den[0:1, h * T:(h + 1) * T]), "lds", 16)
            block.sync(ldh)

            for ci in range(NCH):
                c0 = ci * CH
                ve_i = c.counts["ve"]

                def ldn(s, h=h, ci=ci, c0=c0, ve_i=ve_i):
                    s.wait_ge(c.sems["ve"], ve_i)
                    c.inc(s.dma_start(
                        out=numsb[:, (ci % 2) * CH:(ci % 2 + 1) * CH],
                        in_=num[h * 128:(h + 1) * 128, c0:c0 + CH]), "lds", 16)
                block.sync(ldn)
                ld_n = c.counts["lds"]
                ve_war = c.counts["ve"]
                se_war = c.counts["se"]

                def mmc(tensor, h=h, c0=c0, ld_n=ld_n, ve_war=ve_war,
                        se_war=se_war):
                    if h == 0 and c0 == 0:
                        tensor.wait_ge(c.sems["ldg"], 32)
                        tensor.wait_ge(c.sems["lds"], ld_q)
                    tensor.wait_ge(c.sems["ve"], ve_war)
                    tensor.wait_ge(c.sems["se"], se_war)
                    c.inc(tensor.matmul(pc_, scb[:, h * 128:(h + 1) * 128],
                                        qsb[:, h * T + c0:h * T + c0 + CH],
                                        start=True, stop=True), "te")
                    c.inc(tensor.matmul(pd_, zcb[:, h:h + 1],
                                        qsb[:, h * T + c0:h * T + c0 + CH],
                                        start=True, stop=True), "te")
                block.tensor(mmc)
                te_c = c.counts["te"]

                def evc(vector, h=h, ci=ci, c0=c0, te_c=te_c, ld_n=ld_n):
                    vector.wait_ge(c.sems["te"], te_c)
                    vector.wait_ge(c.sems["lds"], ld_n)
                    c.inc(vector.tensor_add(
                        out=nf[:], in0=pc_,
                        in1=numsb[:, (ci % 2) * CH:(ci % 2 + 1) * CH]), "ve")
                    c.inc(vector.tensor_add(out=drow[0:1, 0:CH], in0=pd_,
                                            in1=densb[0:1, c0:c0 + CH]), "ve")
                    c.inc(vector.tensor_scalar_max(out=drow[0:1, CH:2 * CH],
                                                   in0=drow[0:1, 0:CH],
                                                   scalar1=1e-6), "ve")
                    c.inc(vector.reciprocal(
                        out=drow[0:1, 0:CH], in_=drow[0:1, CH:2 * CH]), "ve")
                block.vector(evc)
                ve_c = c.counts["ve"]

                def mmb(tensor, ve_c=ve_c):
                    c.wait(tensor, "ve", ve_c)
                    c.inc(tensor.matmul(pb_, sonesrow_f[:], drow[0:1, 0:CH],
                                        start=True, stop=True), "te")
                block.tensor(mmb)
                te_b = c.counts["te"]

                def evrb(scalar, te_b=te_b):
                    scalar.wait_ge(c.sems["te"], te_b)
                    c.inc(scalar.copy(out=rb[:], in_=pb_), "se")
                block.scalar(evrb)
                se_b = c.counts["se"]

                def evo(vector, h=h, c0=c0, se_b=se_b):
                    c.wait(vector, "se", se_b)
                    c.inc(vector.tensor_mul(
                        out=aout[:, h * T + c0:h * T + c0 + CH],
                        in0=nf[:], in1=rb[:]), "ve")
                block.vector(evo)

        for nh in range(2):
            n0 = nh * N
            te_prev = c.counts["te"]
            ve_prev = c.counts["ve"]

            def ldx(s, n0=n0, te_prev=te_prev, ve_prev=ve_prev):
                s.wait_ge(c.sems["te"], te_prev)
                s.wait_ge(c.sems["ve"], ve_prev)
                for k in range(KC):
                    c.inc(s.dma_start(out=xh[:, k * N:(k + 1) * N],
                                      in_=x2[k * 128:(k + 1) * 128, n0:n0 + N]),
                          "lds", 16)
            block.sync(ldx)
            ld_x = c.counts["lds"]

            for mh in range(2):
                te_prev2 = c.counts["te"]

                def ldw(g, mh=mh, te_prev2=te_prev2):
                    g.wait_ge(c.sems["te"], te_prev2)
                    for k in range(KC):
                        c.inc(g.dma_start(
                            out=wsb[:, k * 1024:(k + 1) * 1024],
                            in_=woutt[k * 128:(k + 1) * 128,
                                      mh * 1024:(mh + 1) * 1024]), "ldg", 16)
                block.gpsimd(ldw)
                ld_w = c.counts["ldg"]

                for mm_ in range(8):
                    m = mh * 8 + mm_
                    ve0 = c.counts["ve"]

                    def mmo(tensor, m=m, mm_=mm_, n0=n0, ld_w=ld_w, ve0=ve0,
                            ld_x=ld_x):
                        tensor.wait_ge(c.sems["ldg"], ld_w)
                        tensor.wait_ge(c.sems["ve"], ve0)
                        if mm_ == 0:
                            tensor.wait_ge(c.sems["lds"], ld_x)
                        ins = mm_tiles(c, tensor,
                                       po[:, (mm_ % 2) * N:(mm_ % 2 + 1) * N],
                                       lambda k: wsb[:, k * 1024:(k + 1) * 1024],
                                       lambda k: aout[:, k * T + n0:k * T + n0 + N],
                                       mm_, 0, N)
                        c.inc(ins, "te")
                    block.tensor(mmo)
                    te_o = c.counts["te"]

                    def evo2(vector, m=m, mm_=mm_, te_o=te_o):
                        c.wait(vector, "te", te_o)
                        c.inc(vector.tensor_add(
                            out=ressb[:, m * N:(m + 1) * N],
                            in0=po[:, (mm_ % 2) * N:(mm_ % 2 + 1) * N],
                            in1=xh[:, m * N:(m + 1) * N]), "ve")
                    block.vector(evo2)

            emit_ln(c, lambda k: ressb[:, k * N:(k + 1) * N],
                    lambda k: ressb[:, k * N:(k + 1) * N], N, scr,
                    src_wait={"ve": c.counts["ve"]})
            ve_pl = c.counts["ve"]

            def store(s, n0=n0, ve_pl=ve_pl):
                s.wait_ge(c.sems["ve"], ve_pl)
                for k in range(KC):
                    c.inc(s.dma_start(out=x3_out[k * 128:(k + 1) * 128,
                                                 n0:n0 + N],
                                      in_=ressb[:, k * N:(k + 1) * N]), "st", 16)
            block.sync(store)


build_l1b = _wrap(emit_l1b, [
    ("num", [D, T], F32, "ExternalInput"),
    ("den", [1, KC * T], F32, "ExternalInput"),
    ("q", [D, T], BF16, "ExternalInput"),
    ("scorr", [128, KC * 128], F32, "ExternalInput"),
    ("zcorr", [128, KC], F32, "ExternalInput"),
    ("woutt", [D, D], F32, "ExternalInput"),
    ("x2", [D, T], F32, "ExternalInput"),
    ("allones", [128, 128], BF16, "ExternalInput"),
    ("onesrow_f", [1, 128], F32, "ExternalInput"),
    ("x3_out", [D, T], F32, "ExternalOutput"),
])


def run_l1b(num_sh, den_sh, q_sh, s_sh, z_sh, x2_sh, W_out):
    nc = build_l1b()
    zeroS = np.zeros((128, KC * 128), np.float32)
    zeroZ = np.zeros((128, KC), np.float32)
    shared = {
        "woutt": np.ascontiguousarray(W_out.T),
        "allones": np.ones((128, 128), BF),
        "onesrow_f": np.ones((1, 128), np.float32),
    }
    in_maps = []
    for cix in range(NCORES):
        odd = cix % 2 == 1
        in_maps.append(dict(
            shared, num=num_sh[cix], den=den_sh[cix], q=q_sh[cix],
            scorr=s_sh[cix - 1] if odd else zeroS,
            zcorr=z_sh[cix - 1] if odd else zeroZ, x2=x2_sh[cix]))
    res = run_bass_kernel_spmd(nc, in_maps, core_ids=list(range(NCORES)))
    return res.results



GI = 5504
GIC = GI // 128


def emit_l1g_cols(c, io):
    nc = c.nc
    block = c.block
    gidx, uidx, didx = io["gidx"], io["uidx"], io["didx"]
    lut, onesrow_f = io["lut"], io["onesrow_f"]
    wg_sh, wu_sh, wd_sh = io["wg_sh"], io["wu_sh"], io["wd_sh"]

    W1, W2 = GI // 8, D // 8
    from contextlib import ExitStack
    with ExitStack() as ctx:
        e = ctx.enter_context
        sidxf = e(nc.sbuf_tensor("sidxf", [128, KC * W1], F32))
        sidx = e(nc.sbuf_tensor("sidx", [128, KC * W1], I32))
        wacc = e(nc.sbuf_tensor("wacc", [128, KC * W1], BF16))
        mscr = e(nc.sbuf_tensor("mscr", [128, 2 * W1], F32))
        slutf = e(nc.sbuf_tensor("slutf", [1, 16], F32))
        lutcol = e(nc.sbuf_tensor("lutcol", [128, 16], F32))
        sonesrow_f = e(nc.sbuf_tensor("sonesrow_f", [1, 128], F32))
        pl = e(nc.psum_tensor("pl", [128, 16], F32))

        def loads0(s):
            c.inc(s.dma_start(out=slutf[:], in_=lut), "lds", 16)
            c.inc(s.dma_start(out=sonesrow_f[:], in_=onesrow_f), "lds", 16)
        block.sync(loads0)

        def bclut(tensor):
            tensor.wait_ge(c.sems["lds"], 32)
            c.inc(tensor.matmul(pl[:, :], sonesrow_f[:], slutf[:],
                                start=True, stop=True), "te")
        block.tensor(bclut)

        te_lut = c.counts["te"]

        def evlut(scalar):
            c.wait(scalar, "te", te_lut)
            c.inc(scalar.copy(out=lutcol[:, :], in_=pl[:, :]), "se")
        block.scalar(evlut)

        for mat, (src, dst, W, nkc) in enumerate(
                [(gidx, wg_sh, W1, KC), (uidx, wu_sh, W1, KC),
                 (didx, wd_sh, W2, GIC)]):
            ve_prev = c.counts["ve"]
            st_prev = c.counts["st"]

            def ldi(s, src=src, W=W, nkc=nkc, ve_prev=ve_prev, st_prev=st_prev):
                s.wait_ge(c.sems["ve"], ve_prev)
                s.wait_ge(c.sems["st"], st_prev)
                for k in range(nkc):
                    c.inc(s.dma_start(out=sidx[:, k * W:(k + 1) * W],
                                      in_=src[k * 128:(k + 1) * 128, :]),
                          "lds", 16)
            block.sync(ldi)
            ld_i = c.counts["lds"]

            def conv(vector, W=W, nkc=nkc, ld_i=ld_i, mat=mat):
                vector.wait_ge(c.sems["lds"], ld_i)
                if mat == 0:
                    c.wait(vector, "se")
                for k in range(nkc):
                    c.inc(vector.tensor_copy(out=sidxf[:, k * W:(k + 1) * W],
                                             in_=sidx[:, k * W:(k + 1) * W]),
                          "ve")
                vector.memset(wacc[:, 0:nkc * W], 0.0)
                for j in range(16):
                    for k in range(nkc):
                        c.inc(vector.tensor_scalar(
                            out=mscr[:, (k % 2) * W1:(k % 2) * W1 + W],
                            in0=sidxf[:, k * W:(k + 1) * W],
                            scalar1=float(j), scalar2=None,
                            op0=ALU.is_equal), "ve")
                        c.inc(vector.scalar_tensor_tensor(
                            out=wacc[:, k * W:(k + 1) * W],
                            in0=mscr[:, (k % 2) * W1:(k % 2) * W1 + W],
                            scalar=lutcol[:, j:j + 1],
                            in1=wacc[:, k * W:(k + 1) * W],
                            op0=ALU.mult, op1=ALU.add), "ve")
            block.vector(conv)
            ve_b = c.counts["ve"]

            def stw(s, dst=dst, W=W, nkc=nkc, ve_b=ve_b):
                s.wait_ge(c.sems["ve"], ve_b)
                for k in range(nkc):
                    c.inc(s.dma_start(out=dst[k * 128:(k + 1) * 128, :],
                                      in_=wacc[:, k * W:(k + 1) * W]), "st", 16)
            block.sync(stw)


build_l1g = _wrap(emit_l1g_cols, [
    ("gidx", [D, GI // 8], I32, "ExternalInput"),
    ("uidx", [D, GI // 8], I32, "ExternalInput"),
    ("didx", [GI, D // 8], I32, "ExternalInput"),
    ("lut", [1, 16], F32, "ExternalInput"),
    ("onesrow_f", [1, 128], F32, "ExternalInput"),
    ("wg_sh", [D, GI // 8], BF16, "ExternalOutput"),
    ("wu_sh", [D, GI // 8], BF16, "ExternalOutput"),
    ("wd_sh", [GI, D // 8], BF16, "ExternalOutput"),
])


def run_l1g(gate_idx, up_idx, down_idx, lut):
    nc = build_l1g()
    gT = np.ascontiguousarray(gate_idx.T)
    uT = np.ascontiguousarray(up_idx.T)
    dT = np.ascontiguousarray(down_idx.T)
    shared = {"lut": lut.reshape(1, 16).astype(np.float32),
              "onesrow_f": np.ones((1, 128), np.float32)}
    in_maps = []
    for cix in range(NCORES):
        in_maps.append(dict(
            shared,
            gidx=np.ascontiguousarray(gT[:, cix * 688:(cix + 1) * 688]),
            uidx=np.ascontiguousarray(uT[:, cix * 688:(cix + 1) * 688]),
            didx=np.ascontiguousarray(dT[:, cix * 256:(cix + 1) * 256])))
    res = run_bass_kernel_spmd(nc, in_maps, core_ids=list(range(NCORES)))
    wgt = np.concatenate([r["wg_sh"] for r in res.results], axis=1)
    wut = np.concatenate([r["wu_sh"] for r in res.results], axis=1)
    wdt = np.concatenate([r["wd_sh"] for r in res.results], axis=1)
    return wgt, wut, wdt



G_GROUPS = [(0, 11), (11, 11), (22, 11), (33, 10)]


def emit_l1f(c, io, cc_wait=None):
    nc = c.nc
    block = c.block
    x3, wgt, wut, wdt = io["x3"], io["wgt"], io["wut"], io["wdt"]
    allones, onesrow_f = io["allones"], io["onesrow_f"]
    x4_out = io["x4_out"]

    N = 512
    from contextlib import ExitStack
    with ExitStack() as ctx:
        e = ctx.enter_context
        xbf = e(nc.sbuf_tensor("xbf", [128, KC * T], BF16))
        yacc = e(nc.sbuf_tensor("yacc", [128, KC * T], F32))
        wbuf = e(nc.sbuf_tensor("wbuf", [128, KC * INTER], BF16))
        hsc = e(nc.sbuf_tensor("hsc", [128, IC * T], BF16))
        sones = e(nc.sbuf_tensor("sones", [128, 128], BF16))
        sonesrow_f = e(nc.sbuf_tensor("sonesrow_f", [1, 128], F32))
        ph = e(nc.psum_tensor("ph", [128, T], F32))
        py = e(nc.psum_tensor("py", [128, T], F32))
        scr = make_ln_scratch(ctx, nc, N, psum_from=(ph, py))
        scr["onescol"] = sones[:, 0:1]
        scr["onesrow"] = sonesrow_f[:]

        def loads_s(s):
            for k in range(KC):
                c.inc(s.dma_start(out=yacc[:, k * T:(k + 1) * T],
                                  in_=x3[k * 128:(k + 1) * 128, :]), "lds", 16)
            c.inc(s.dma_start(out=sones[:], in_=allones), "lds", 16)
            c.inc(s.dma_start(out=sonesrow_f[:], in_=onesrow_f), "lds", 16)
        block.sync(loads_s)

        def loads_g(g):
            for k in range(KC):
                c.inc(g.dma_start(out=xbf[:, k * T:(k + 1) * T],
                                  in_=x3[k * 128:(k + 1) * 128, :]), "ldg", 16)
        block.gpsimd(loads_g)

        for gs, gsz in G_GROUPS:
            gw = gsz * 128
            te_prevC = c.counts["te"]

            def lwg(gp, gs=gs, gw=gw, te_prevC=te_prevC):
                if cc_wait and gs == 0:
                    for s_ in cc_wait:
                        gp.wait_ge(c.sems[s_], 1)
                gp.wait_ge(c.sems["te"], te_prevC)
                for k in range(KC):
                    c.inc(gp.dma_start(
                        out=wbuf[:, k * INTER:k * INTER + gw],
                        in_=wgt[k * 128:(k + 1) * 128,
                                gs * 128:gs * 128 + gw]), "ldg", 16)
            block.gpsimd(lwg)
            ld_w1 = c.counts["ldg"]

            for i in range(gsz):
                se_pa = c.counts["se"]

                def mma(tensor, i=i, ld_w1=ld_w1, se_pa=se_pa):
                    tensor.wait_ge(c.sems["ldg"], ld_w1)
                    tensor.wait_ge(c.sems["se"], se_pa)
                    for n0 in (0, N):
                        ins = mm_tiles(c, tensor, ph[:, n0:n0 + N],
                                       lambda k: wbuf[:, k * INTER:(k + 1) * INTER],
                                       lambda k: xbf[:, k * T:(k + 1) * T],
                                       i, n0, N)
                        c.inc(ins, "te")
                block.tensor(mma)
                te_a = c.counts["te"]

                def eva(scalar, i=i, te_a=te_a):
                    scalar.wait_ge(c.sems["te"], te_a)
                    c.inc(scalar.activation(out=hsc[:, i * T:(i + 1) * T],
                                            in_=ph[:], func=AF.Silu), "se")
                block.scalar(eva)

            te_prevA = c.counts["te"]

            def lwu(gp, gs=gs, gw=gw, te_prevA=te_prevA):
                gp.wait_ge(c.sems["te"], te_prevA)
                for k in range(KC):
                    c.inc(gp.dma_start(
                        out=wbuf[:, k * INTER:k * INTER + gw],
                        in_=wut[k * 128:(k + 1) * 128,
                                gs * 128:gs * 128 + gw]), "ldg", 16)
            block.gpsimd(lwu)
            ld_w3 = c.counts["ldg"]
            se_afterA = c.counts["se"]

            for i in range(gsz):
                ve_pb = c.counts["ve"]

                def mmb(tensor, i=i, ld_w3=ld_w3, ve_pb=ve_pb,
                        se_afterA=se_afterA):
                    tensor.wait_ge(c.sems["ldg"], ld_w3)
                    tensor.wait_ge(c.sems["ve"], ve_pb)
                    if i == 0:
                        tensor.wait_ge(c.sems["se"], se_afterA)
                    for n0 in (0, N):
                        ins = mm_tiles(c, tensor, ph[:, n0:n0 + N],
                                       lambda k: wbuf[:, k * INTER:(k + 1) * INTER],
                                       lambda k: xbf[:, k * T:(k + 1) * T],
                                       i, n0, N)
                        c.inc(ins, "te")
                block.tensor(mmb)
                te_b = c.counts["te"]

                def evb(vector, i=i, te_b=te_b):
                    vector.wait_ge(c.sems["te"], te_b)
                    c.inc(vector.tensor_mul(out=hsc[:, i * T:(i + 1) * T],
                                            in0=ph[:],
                                            in1=hsc[:, i * T:(i + 1) * T]),
                          "ve")
                block.vector(evb)

            te_prevB = c.counts["te"]
            ve_hsc = c.counts["ve"]

            def lwd(gp, gs=gs, gsz=gsz, te_prevB=te_prevB):
                gp.wait_ge(c.sems["te"], te_prevB)
                for i in range(gsz):
                    c.inc(gp.dma_start(
                        out=wbuf[:, i * D:(i + 1) * D],
                        in_=wdt[(gs + i) * 128:(gs + i + 1) * 128, :]),
                        "ldg", 16)
            block.gpsimd(lwd)
            ld_w2 = c.counts["ldg"]

            for m in range(KC):
                ve_pc = c.counts["ve"]

                def mmc(tensor, m=m, gsz=gsz, ld_w2=ld_w2, ve_pc=ve_pc,
                        ve_hsc=ve_hsc):
                    tensor.wait_ge(c.sems["ldg"], ld_w2)
                    tensor.wait_ge(c.sems["ve"], max(ve_pc, ve_hsc))
                    for n0 in (0, N):
                        ins = mm_tiles(c, tensor, py[:, n0:n0 + N],
                                       lambda i: wbuf[:, i * D:(i + 1) * D],
                                       lambda i: hsc[:, i * T:(i + 1) * T],
                                       m, n0, N, kcs=range(gsz))
                        c.inc(ins, "te")
                block.tensor(mmc)
                te_c = c.counts["te"]

                def evc(vector, m=m, te_c=te_c):
                    vector.wait_ge(c.sems["te"], te_c)
                    if m == 0:
                        vector.wait_ge(c.sems["lds"], 16 * KC)
                    c.inc(vector.tensor_add(out=yacc[:, m * T:(m + 1) * T],
                                            in0=py[:],
                                            in1=yacc[:, m * T:(m + 1) * T]),
                          "ve")
                block.vector(evc)

        for nh in range(2):
            n0 = nh * N
            emit_ln(c, lambda k: yacc[:, k * T + n0:k * T + n0 + N],
                    lambda k: yacc[:, k * T + n0:k * T + n0 + N], N, scr,
                    src_wait={"ve": c.counts["ve"], "se": c.counts["se"]})
            ve_pl = c.counts["ve"]

            def store(s, n0=n0, ve_pl=ve_pl):
                s.wait_ge(c.sems["ve"], ve_pl)
                for k in range(KC):
                    c.inc(s.dma_start(out=x4_out[k * 128:(k + 1) * 128,
                                                 n0:n0 + N],
                                      in_=yacc[:, k * T + n0:k * T + n0 + N]),
                          "st", 16)
            block.sync(store)


build_l1f = _wrap(emit_l1f, [
    ("x3", [D, T], F32, "ExternalInput"),
    ("wgt", [D, GI], BF16, "ExternalInput"),
    ("wut", [D, GI], BF16, "ExternalInput"),
    ("wdt", [GI, D], BF16, "ExternalInput"),
    ("allones", [128, 128], BF16, "ExternalInput"),
    ("onesrow_f", [1, 128], F32, "ExternalInput"),
    ("x4_out", [D, T], F32, "ExternalOutput"),
])


def run_l1f(x3_sh, wgt, wut, wdt):
    nc = build_l1f()
    shared = {"wgt": wgt, "wut": wut, "wdt": wdt,
              "allones": np.ones((128, 128), BF),
              "onesrow_f": np.ones((1, 128), np.float32)}
    in_maps = [dict(shared, x3=x3_sh[cix]) for cix in range(NCORES)]
    res = run_bass_kernel_spmd(nc, in_maps, core_ids=list(range(NCORES)))
    return res.results



def emit_l1g_j(c, io):
    nc = c.nc
    block = c.block
    mats = [(io["gidxt"], io["wgp"], GI, KC), (io["uidxt"], io["wup"], GI, KC),
            (io["didxt"], io["wdp"], D, GIC)]
    from contextlib import ExitStack
    with ExitStack() as ctx:
        e = ctx.enter_context
        sidx = e(nc.sbuf_tensor("gsidx", [128, 2 * GI], I32))
        sidxf = e(nc.sbuf_tensor("gsidxf", [128, 2 * GI], F32))
        wch = e(nc.sbuf_tensor("gwch", [128, 2 * GI], BF16))
        msl = e(nc.sbuf_tensor("gmsl", [128, 2 * GI], F32))
        sjv = e(nc.sbuf_tensor("gsjv", [128, 2], F32))
        slut = e(nc.sbuf_tensor("gslut", [128, 2], F32))

        def loads0(s):
            c.inc(s.dma_start(out=sjv[:], in_=io["jvals"]), "lds", 16)
            c.inc(s.dma_start(out=slut[:], in_=io["lutsel"]), "lds", 16)
        block.sync(loads0)

        chunks = []
        for mi, (src, dst, W, nkc) in enumerate(mats):
            for k in range(nkc):
                chunks.append((mi, k))
        hist = []
        for ci, (mi, k) in enumerate(chunks):
            src, dst, W, nkc = mats[mi]
            sl = ci % 2
            ve_guard = hist[ci - 2][0] if ci >= 2 else 0
            st_guard = hist[ci - 2][1] if ci >= 2 else 0

            def ld(s, src=src, W=W, k=k, sl=sl, ve_guard=ve_guard):
                s.wait_ge(c.sems["ve"], ve_guard)
                c.inc(s.dma_start(out=sidx[:, sl * GI:sl * GI + W],
                                  in_=src[k * 128:(k + 1) * 128, :]), "lds", 16)
            block.sync(ld)
            ld_i = c.counts["lds"]

            def conv(scalar, W=W, sl=sl, ld_i=ld_i, ve_guard=ve_guard):
                scalar.wait_ge(c.sems["lds"], ld_i)
                scalar.wait_ge(c.sems["ve"], ve_guard)
                c.inc(scalar.copy(out=sidxf[:, sl * GI:sl * GI + W],
                                  in_=sidx[:, sl * GI:sl * GI + W]), "se")
            block.scalar(conv)
            se_i = c.counts["se"]

            def bld(vector, W=W, sl=sl, se_i=se_i, st_guard=st_guard, ci=ci):
                c.wait(vector, "se", se_i)
                vector.wait_ge(c.sems["st"], st_guard)
                if ci == 0:
                    c.wait(vector, "lds", 32)
                c.inc(vector.tensor_scalar(
                    out=wch[:, sl * GI:sl * GI + W],
                    in0=sidxf[:, sl * GI:sl * GI + W],
                    scalar1=sjv[:, 0:1], scalar2=slut[:, 0:1],
                    op0=ALU.is_equal, op1=ALU.mult), "ve")
                c.inc(vector.tensor_scalar(
                    out=msl[:, sl * GI:sl * GI + W],
                    in0=sidxf[:, sl * GI:sl * GI + W],
                    scalar1=sjv[:, 1:2], scalar2=None,
                    op0=ALU.is_equal), "ve")
                c.inc(vector.scalar_tensor_tensor(
                    out=wch[:, sl * GI:sl * GI + W],
                    in0=msl[:, sl * GI:sl * GI + W],
                    scalar=slut[:, 1:2],
                    in1=wch[:, sl * GI:sl * GI + W],
                    op0=ALU.mult, op1=ALU.add), "ve")
            block.vector(bld)
            ve_b = c.counts["ve"]

            def stw(g, dst=dst, W=W, k=k, sl=sl, ve_b=ve_b):
                g.wait_ge(c.sems["ve"], ve_b)
                c.inc(g.dma_start(out=dst[k * 128:(k + 1) * 128, :],
                                  in_=wch[:, sl * GI:sl * GI + W]), "st", 16)
            block.gpsimd(stw)
            hist.append((c.counts["ve"], c.counts["st"]))



def build_fused():
    from contextlib import ExitStack
    nc = bass.Bass("TRN2", target_bir_lowering=False)
    EI, EO = "ExternalInput", "Internal"
    io = {}

    def d(name, shape, dt, kind="ExternalInput"):
        io[name] = nc.dram_tensor(name, shape, dt, kind=kind).ap()
    d("x", [D, T], F32)
    d("wqk", [D, D], F32); d("wv", [D, D], F32); d("wu", [D, D], F32)
    d("bqk", [128, KC], F32); d("bv", [128, KC], F32); d("bu", [128, KC], F32)
    d("wo", [D, D], F32)
    d("lng", [128, KC], F32); d("lnb", [128, KC], F32); d("bo", [128, KC], F32)
    d("gwt", [D, 16], F32)
    d("w1t", [16, D, INTER], F32); d("w3t", [16, D, INTER], F32)
    d("w2t", [16, INTER, D], F32)
    d("sw1t", [D, 2 * INTER], F32); d("sw3t", [D, 2 * INTER], F32)
    d("sw2t", [2 * INTER, D], F32)
    d("ind", [16, D], BF16)
    d("wqkvt", [D, 3 * D], F32); d("woutt", [D, D], F32)
    d("cosf", [128, T], F32); d("sinf", [128, T], F32)
    d("rtc", [128, 128], BF16); d("maskt", [128, 2 * CH], BF16)
    d("gidxt", [D, GI], I32); d("uidxt", [D, GI], I32); d("didxt", [GI, D], I32)
    d("jvals", [128, 2], F32); d("lutsel", [128, 2], F32)
    d("mcol", [128, 1], F32); d("fcol", [128, 1], F32)
    d("ident", [128, 128], BF16); d("allones", [128, 128], BF16)
    d("onesrow_f", [1, 128], F32)
    d("x4_out", [D, T], F32, "ExternalOutput")
    d("q0", [D, T], BF16, EO); d("u0", [D, T], F32, EO)
    d("kvp", [128, KC * 128], F32, EO); d("kvr", [128, KC * 128], F32, EO)
    d("x1", [D, T], F32, EO); d("x2", [D, T], F32, EO); d("x3", [D, T], F32, EO)
    d("q1", [D, T], BF16, EO)
    d("num", [D, T], F32, EO); d("den", [1, KC * T], F32, EO)
    d("ssc", [128, KC * 128], F32, EO); d("zsc", [128, KC], F32, EO)
    d("sred", [128, KC * 128], F32, EO); d("zred", [128, KC], F32, EO)
    d("wgp", [D, GI], BF16, EO); d("wup", [D, GI], BF16, EO)
    d("wdp", [GI, D], BF16, EO)
    d("wgr", [D, GI], BF16, EO); d("wur", [D, GI], BF16, EO)
    d("wdr", [GI, D], BF16, EO)

    PAIRS = [[0, 1], [2, 3], [4, 5], [6, 7]]
    ALL8 = [list(range(8))]

    with ExitStack() as st0:
        c = mk_ctx(nc, st0)
        for s in ("cckv", "ccs", "ccz", "ccg", "ccu", "ccd"):
            c.ccsem(s)
        c.block = st0.enter_context(nc.Block())

        c.new_phase()
        emit_l0a(c, dict(io, q_out=io["q0"], u_out=io["u0"], kv_out=io["kvp"]))
        st_kv = c.counts["st"]

        def coll_kv(g):
            g.wait_ge(c.sems["st"], st_kv)
            c.inc(g.collective_compute(
                "AllReduce", ALU.add, PAIRS,
                ins=[io["kvp"].opt()], outs=[io["kvr"].opt()]), "cckv")
        c.block.gpsimd(coll_kv)
        emit_barrier(c)

        c.new_phase()
        emit_l0b(c, dict(io, q=io["q0"], u=io["u0"], x1_out=io["x1"]),
                 kv_pair=False, cc_wait="cckv")
        emit_barrier(c)

        c.new_phase()
        emit_l0f(c, dict(io, x2_out=io["x2"]))
        emit_barrier(c)

        c.new_phase()
        emit_l1g_j(c, io)
        st_g = c.counts["st"]

        def coll_g(g):
            g.wait_ge(c.sems["st"], st_g)
            c.inc(g.collective_compute(
                "AllReduce", ALU.add, ALL8,
                ins=[io["wgp"].opt()], outs=[io["wgr"].opt()]), "ccg")
            c.inc(g.collective_compute(
                "AllReduce", ALU.add, ALL8,
                ins=[io["wup"].opt()], outs=[io["wur"].opt()]), "ccu")
            c.inc(g.collective_compute(
                "AllReduce", ALU.add, ALL8,
                ins=[io["wdp"].opt()], outs=[io["wdr"].opt()]), "ccd")
        c.block.gpsimd(coll_g)
        emit_barrier(c)

        c.new_phase()
        emit_l1a(c, dict(io, q_out=io["q1"], num_out=io["num"],
                         den_out=io["den"], s_out=io["ssc"], z_out=io["zsc"]))
        st_sz = c.counts["st"]

        def coll_sz(g):
            g.wait_ge(c.sems["st"], st_sz)
            c.inc(g.collective_compute(
                "AllReduce", ALU.add, PAIRS,
                ins=[io["ssc"].opt()], outs=[io["sred"].opt()]), "ccs")
            c.inc(g.collective_compute(
                "AllReduce", ALU.add, PAIRS,
                ins=[io["zsc"].opt()], outs=[io["zred"].opt()]), "ccz")
        c.block.gpsimd(coll_sz)
        emit_barrier(c)

        c.new_phase()
        emit_l1b(c, dict(io, q=io["q1"], scorr=io["sred"], zcorr=io["zred"],
                         x2=io["x2"], x3_out=io["x3"]),
                 fcol=True, cc_wait=["ccs", "ccz"])
        emit_barrier(c)

        c.new_phase()
        emit_l1f(c, dict(io, wgt=io["wgr"], wut=io["wur"], wdt=io["wdr"]),
                 cc_wait=["ccg", "ccu", "ccd"])
        st_end = c.counts["st"]

        def fin(g):
            g.wait_ge(c.sems["st"], st_end)
        c.block.gpsimd(fin)
    return nc


def run_fused(ins):
    nc = build_fused()
    lut = np.asarray(ins["L1_ffn_lut"], dtype=np.float32)
    shared = {
        "wqk": np.ascontiguousarray(ins["L0_attn_Wqk"]),
        "wv": np.ascontiguousarray(ins["L0_attn_Wv"]),
        "wu": np.ascontiguousarray(ins["L0_attn_Wu"]),
        "bqk": _bias_cols(ins["L0_attn_bqk"]),
        "bv": _bias_cols(ins["L0_attn_bv"]),
        "bu": _bias_cols(ins["L0_attn_bu"]),
        "wo": np.ascontiguousarray(ins["L0_attn_Wo"]),
        "bo": _bias_cols(ins["L0_attn_bo"]),
        "lng": _bias_cols(ins["L0_attn_ln_g"]),
        "lnb": _bias_cols(ins["L0_attn_ln_b"]),
        "gwt": np.ascontiguousarray(ins["L0_ffn_gate_w"].T),
        "w1t": np.ascontiguousarray(ins["L0_ffn_w1"].transpose(0, 2, 1)),
        "w3t": np.ascontiguousarray(ins["L0_ffn_w3"].transpose(0, 2, 1)),
        "w2t": np.ascontiguousarray(ins["L0_ffn_w2"].transpose(0, 2, 1)),
        "sw1t": np.ascontiguousarray(ins["L0_ffn_sw1"].T),
        "sw3t": np.ascontiguousarray(ins["L0_ffn_sw3"].T),
        "sw2t": np.ascontiguousarray(ins["L0_ffn_sw2"].T),
        "ind": _make_ind(),
        "wqkvt": np.ascontiguousarray(ins["L1_attn_W_qkv"].T),
        "woutt": np.ascontiguousarray(ins["L1_attn_W_out"].T),
        "rtc": _rt_const(), "maskt": _maskt_const(),
        "gidxt": np.ascontiguousarray(ins["L1_ffn_gate_idx"].T),
        "uidxt": np.ascontiguousarray(ins["L1_ffn_up_idx"].T),
        "didxt": np.ascontiguousarray(ins["L1_ffn_down_idx"].T),
        "ident": _IDENT_BF,
        "allones": np.ones((128, 128), BF),
        "onesrow_f": np.ones((1, 128), np.float32),
    }
    x = np.ascontiguousarray(ins["L0_attn_x"], dtype=np.float32)
    shards = _shard_x(x)
    in_maps = []
    for cix in range(NCORES):
        t0 = (cix % 2) * T
        cosf, sinf = _rope_tables(t0)
        odd = float(cix % 2)
        jv = np.full((128, 2), 0.0, np.float32)
        jv[:, 0] = 2 * cix
        jv[:, 1] = 2 * cix + 1
        ls = np.ascontiguousarray(
            np.broadcast_to(lut[2 * cix:2 * cix + 2], (128, 2)))
        in_maps.append(dict(
            shared, x=shards[cix], cosf=cosf, sinf=sinf,
            jvals=jv, lutsel=ls,
            mcol=np.full((128, 1), 1.0 - odd, np.float32),
            fcol=np.full((128, 1), odd, np.float32)))
    res = run_bass_kernel_spmd(nc, in_maps, core_ids=list(range(NCORES)))
    out = np.empty((4, 2048, D), np.float32)
    for cix in range(NCORES):
        b, t0 = cix // 2, (cix % 2) * T
        out[b, t0:t0 + T, :] = res.results[cix]["x4_out"].T
    return out



_IDENT_BF = np.eye(128, dtype=BF)


def _shard_x(xfull):
    out = []
    for core in range(NCORES):
        b, t0 = core // 2, (core % 2) * T
        out.append(np.ascontiguousarray(xfull[b, t0:t0 + T, :].T))
    return out


def _bias_cols(b):
    return np.ascontiguousarray(b.reshape(KC, 128).T)


def run_l0a(x_shards, Wqk, bqk, Wv, bv, Wu, bu):
    nc = build_l0a()
    shared = {
        "wqk": np.ascontiguousarray(Wqk), "wv": np.ascontiguousarray(Wv),
        "wu": np.ascontiguousarray(Wu),
        "bqk": _bias_cols(bqk), "bv": _bias_cols(bv), "bu": _bias_cols(bu),
        "ident": _IDENT_BF,
    }
    in_maps = [dict(shared, x=x_shards[c]) for c in range(NCORES)]
    res = run_bass_kernel_spmd(nc, in_maps, core_ids=list(range(NCORES)))
    return res.results


def kernel(**inputs):
    return run_fused(inputs)


def kernel_multilaunch(**inputs):
    ins = inputs
    shards = _shard_x(np.ascontiguousarray(ins["L0_attn_x"], dtype=np.float32))

    r1 = run_l0a(shards, ins["L0_attn_Wqk"], ins["L0_attn_bqk"],
                 ins["L0_attn_Wv"], ins["L0_attn_bv"],
                 ins["L0_attn_Wu"], ins["L0_attn_bu"])
    r2 = run_l0b([r["q_out"] for r in r1], [r["kv_out"] for r in r1],
                 [r["u_out"] for r in r1], shards,
                 ins["L0_attn_Wo"], ins["L0_attn_bo"],
                 ins["L0_attn_ln_g"], ins["L0_attn_ln_b"])
    x1_sh = [r["x1_out"] for r in r2]
    r3 = run_l0f(x1_sh, ins["L0_ffn_gate_w"], ins["L0_ffn_w1"],
                 ins["L0_ffn_w2"], ins["L0_ffn_w3"], ins["L0_ffn_sw1"],
                 ins["L0_ffn_sw2"], ins["L0_ffn_sw3"])
    x2_sh = [r["x2_out"] for r in r3]
    ra = run_l1a(x2_sh, ins["L1_attn_W_qkv"])
    rb = run_l1b([r["num_out"] for r in ra], [r["den_out"] for r in ra],
                 [r["q_out"] for r in ra], [r["s_out"] for r in ra],
                 [r["z_out"] for r in ra], x2_sh, ins["L1_attn_W_out"])
    x3_sh = [r["x3_out"] for r in rb]
    wgt, wut, wdt = run_l1g(ins["L1_ffn_gate_idx"], ins["L1_ffn_up_idx"],
                            ins["L1_ffn_down_idx"], ins["L1_ffn_lut"])
    rf = run_l1f(x3_sh, wgt, wut, wdt)

    out = np.empty((4, 2048, D), np.float32)
    for cix in range(NCORES):
        b, t0 = cix // 2, (cix % 2) * T
        out[b, t0:t0 + T, :] = rf[cix]["x4_out"].T
    return out


# version 03
# speedup vs baseline: 1.0068x; 1.0068x over previous
import numpy as np
import ml_dtypes
import concourse.bass as bass
import concourse.mybir as mybir
from concourse.bass_utils import run_bass_kernel_spmd

F32 = mybir.dt.float32
BF16 = mybir.dt.bfloat16
I32 = mybir.dt.int32
AF = mybir.ActivationFunctionType
ALU = mybir.AluOpType
BF = ml_dtypes.bfloat16

D = 2048
T = 1024
KC = D // 128
NCORES = 8



class Ctx:

    def __init__(self, nc, block, stack=None):
        self.nc = nc
        self.block = block
        self.sems = {}
        self.counts = {}
        self._stack = stack

    def sem(self, name):
        cm = self.nc.semaphore(name)
        h = self._stack.enter_context(cm) if self._stack else cm.__enter__()
        self.sems[name] = h
        self.counts[name] = 0
        return h

    def inc(self, instr, sem, n=1):
        instr.then_inc(self.sems[sem], n)
        self.counts[sem] += n
        return self.counts[sem]

    def wait(self, eng, sem, val=None):
        eng.wait_ge(self.sems[sem], self.counts[sem] if val is None else val)

    def new_phase(self):
        if not hasattr(self, "nc0"):
            self.nc0 = self.nc
        self.phase = getattr(self, "phase", -1) + 1
        self.nc = _NcProxy(self.nc0, self.phase)
        for s in ("lds", "ldg", "st"):
            cm = self.nc0.semaphore(f"{s}{self.phase}")
            self.sems[s] = self._stack.enter_context(cm)
            self.counts[s] = 0

    def ccsem(self, name):
        h = self._stack.enter_context(self.nc.semaphore(name))
        self.sems[name] = h
        self.counts[name] = 0
        return h


class _NcProxy:
    def __init__(self, nc, tag):
        self._nc = nc
        self._tag = tag

    def sbuf_tensor(self, name, *a, **k):
        return self._nc.sbuf_tensor(f"{name}_p{self._tag}", *a, **k)

    def psum_tensor(self, name, *a, **k):
        return self._nc.psum_tensor(f"{name}_p{self._tag}", *a, **k)

    def __getattr__(self, x):
        return getattr(self._nc, x)


def mk_ctx(nc, stack):
    c = Ctx(nc, None, stack)
    for s in ("te", "ve", "se", "po", "bar"):
        c.sem(s)
    c.barn = 0
    return c


def emit_barrier(c):
    snap = {s: c.counts[s] for s in ("te", "ve", "se", "po", "lds", "ldg", "st")}
    c.barn += 1
    target = 5 * c.barn
    for name in ("tensor", "vector", "scalar", "gpsimd", "sync"):
        def body(eng, snap=snap, target=target):
            for s_, v_ in snap.items():
                if v_:
                    eng.wait_ge(c.sems[s_], v_)
            eng.sem_inc(c.sems["bar"], 1)
            eng.wait_ge(c.sems["bar"], target)
        getattr(c.block, name)(body)


def mm_tiles(ct, tensor, psum_ap, w_sb, x_sb, m, n0, nn, kcs=range(KC),
             wait=None):
    ks = list(kcs)
    for i, k in enumerate(ks):
        ins = tensor.matmul(psum_ap, w_sb(k)[:, m * 128:(m + 1) * 128],
                            x_sb(k)[:, n0:n0 + nn],
                            start=(i == 0), stop=(i == len(ks) - 1))
    return ins


def mm_tiles2(ct, tensor, psum, w_sb, x_sb, m, halves, kcs=range(KC)):
    ks = list(kcs)
    ins = None
    for i, k in enumerate(ks):
        w = w_sb(k)[:, m * 128:(m + 1) * 128]
        for off, n0, nn in halves:
            ins = tensor.matmul(psum[:, off:off + nn], w,
                                x_sb(k)[:, n0:n0 + nn],
                                start=(i == 0), stop=(i == len(ks) - 1))
    return ins



def emit_l0a(c, io):
    nc = c.nc
    block = c.block
    x, wqk, wv, wu = io["x"], io["wqk"], io["wv"], io["wu"]
    bqk, bv, bu, ident = io["bqk"], io["bv"], io["bu"], io["ident"]
    q_out, u_out, kv_out = io["q_out"], io["u_out"], io["kv_out"]

    from contextlib import ExitStack
    with ExitStack() as ctx:
        e = ctx.enter_context
        xbf = e(nc.sbuf_tensor("xbf", [128, KC * T], BF16))
        wsb = e(nc.sbuf_tensor("wsb", [128, KC * D], BF16))
        qbf = e(nc.sbuf_tensor("qbf", [128, KC * T], BF16))
        vbf = e(nc.sbuf_tensor("vbf", [128, KC * T], BF16))
        uscr = e(nc.sbuf_tensor("uscr", [128, 2 * T], F32))
        r1 = e(nc.sbuf_tensor("r1", [128, T], F32))
        r2 = e(nc.sbuf_tensor("r2", [128, T], F32))
        sbias = e(nc.sbuf_tensor("sbias", [128, 3 * KC], F32))
        sident = e(nc.sbuf_tensor("sident", [128, 128], BF16))
        ktm = e(nc.sbuf_tensor("ktm", [128, 128], BF16))
        vtm = e(nc.sbuf_tensor("vtm", [128, 128], BF16))
        kvsb = e(nc.sbuf_tensor("kvsb", [128, KC * 128], F32))
        pa = e(nc.psum_tensor("pa", [128, T], F32))
        pkv = e(nc.psum_tensor("pkv", [128, 4 * 128], F32))
        ptr = e(nc.psum_tensor("ptr", [128, 256], BF16))

        def loads(g):
            for k in range(KC):
                c.inc(g.dma_start(out=xbf[:, k * T:(k + 1) * T],
                                  in_=x[k * 128:(k + 1) * 128, :]), "ldg", 16)
            for k in range(KC):
                c.inc(g.dma_start(out=wsb[:, k * D:(k + 1) * D],
                                  in_=wqk[k * 128:(k + 1) * 128, :]), "ldg", 16)
        block.gpsimd(loads)
        ld_xw = c.counts["ldg"]

        def loads2(s):
            c.inc(s.dma_start(out=sbias[:, 0:KC], in_=bqk), "lds", 16)
            c.inc(s.dma_start(out=sbias[:, KC:2 * KC], in_=bv), "lds", 16)
            c.inc(s.dma_start(out=sbias[:, 2 * KC:3 * KC], in_=bu), "lds", 16)
            c.inc(s.dma_start(out=sident[:], in_=ident), "lds", 16)
        block.sync(loads2)

        for proj in range(3):
            wname = [wqk, wv, wu][proj]
            if proj > 0:
                te_before = c.counts["te"]

                def reload(g, wname=wname, need=te_before):
                    g.wait_ge(c.sems["te"], need)
                    for k in range(KC):
                        c.inc(g.dma_start(out=wsb[:, k * D:(k + 1) * D],
                                          in_=wname[k * 128:(k + 1) * 128, :]),
                              "ldg", 16)
                block.gpsimd(reload)
            ld_now = c.counts["ldg"]

            for m in range(KC):
                ve_now = c.counts["ve"]
                se_now = c.counts["se"]

                def mms(tensor, m=m, ld_now=ld_now, ve_now=ve_now, se_now=se_now):
                    tensor.wait_ge(c.sems["ldg"], ld_now)
                    if m == 0:
                        tensor.wait_ge(c.sems["lds"], 64)
                    tensor.wait_ge(c.sems["ve"], ve_now)
                    tensor.wait_ge(c.sems["se"], se_now)
                    ins = mm_tiles2(c, tensor, pa,
                                    lambda k: wsb[:, k * D:(k + 1) * D],
                                    lambda k: xbf[:, k * T:(k + 1) * T],
                                    m, [(0, 0, 512), (512, 512, 512)])
                    c.inc(ins, "te")
                block.tensor(mms)
                te_now = c.counts["te"]

                if proj == 0:
                    def ev_v(vector, m=m, te_now=te_now):
                        vector.wait_ge(c.sems["te"], te_now)
                        c.inc(vector.tensor_scalar(
                            out=r1[:], in0=pa[:], scalar1=sbias[:, m:m + 1],
                            scalar2=0.0, op0=ALU.add, op1=ALU.max), "ve")
                        c.inc(vector.tensor_scalar(
                            out=r2[:], in0=pa[:], scalar1=sbias[:, m:m + 1],
                            scalar2=0.0, op0=ALU.add, op1=ALU.min), "ve")
                    block.vector(ev_v)
                    ve_now2 = c.counts["ve"]

                    def ev_s2(scalar, ve_now2=ve_now2):
                        c.wait(scalar, "ve", ve_now2)
                        c.inc(scalar.activation(out=r2[:], in_=r2[:],
                                                func=AF.Exp), "se")
                    block.scalar(ev_s2)
                    se_now3 = c.counts["se"]

                    def ev_v2(vector, m=m, se_now3=se_now3):
                        vector.wait_ge(c.sems["se"], se_now3)
                        c.inc(vector.scalar_tensor_tensor(
                            out=qbf[:, m * T:(m + 1) * T], in0=r2[:],
                            scalar=-1.0, in1=r1[:],
                            op0=ALU.add, op1=ALU.add), "ve")
                    block.vector(ev_v2)
                elif proj == 1:
                    def ev_s(scalar, m=m, te_now=te_now):
                        scalar.wait_ge(c.sems["te"], te_now)
                        c.inc(scalar.activation(
                            out=vbf[:, m * T:(m + 1) * T], in_=pa[:],
                            func=AF.Silu, bias=sbias[:, KC + m:KC + m + 1]),
                            "se")
                    block.scalar(ev_s)
                else:
                    st_now = c.counts["st"]

                    def ev_s(scalar, m=m, te_now=te_now, st_now=st_now):
                        scalar.wait_ge(c.sems["te"], te_now)
                        if m >= 2:
                            scalar.wait_ge(c.sems["st"], (m - 1) * 16)
                        c.inc(scalar.activation(
                            out=uscr[:, (m % 2) * T:(m % 2 + 1) * T],
                            in_=pa[:], func=AF.Silu,
                            bias=sbias[:, 2 * KC + m:2 * KC + m + 1]), "se")
                    block.scalar(ev_s)
                    se_now2 = c.counts["se"]

                    def st_u(s, m=m, se_now2=se_now2):
                        s.wait_ge(c.sems["se"], se_now2)
                        c.inc(s.dma_start(
                            out=u_out[m * 128:(m + 1) * 128, :],
                            in_=uscr[:, (m % 2) * T:(m % 2 + 1) * T]), "st", 16)
                    block.sync(st_u)

        ve_q = c.counts["ve"]
        se_v = c.counts["se"]

        def kv_phase(tensor):
            tensor.wait_ge(c.sems["ve"], ve_q)
            tensor.wait_ge(c.sems["se"], se_v)
        block.tensor(kv_phase)

        for h in range(KC):
            for t in range(8):
                ve_now = c.counts["ve"]
                se_now = c.counts["se"]

                def trs(tensor, h=h, t=t, ve_now=ve_now, se_now=se_now):
                    tensor.wait_ge(c.sems["ve"], ve_now)
                    tensor.wait_ge(c.sems["se"], se_now)
                    c.inc(tensor.transpose(
                        ptr[:, 0:128],
                        qbf[:, h * T + t * 128:h * T + (t + 1) * 128],
                        sident[:]), "te")
                    c.inc(tensor.transpose(
                        ptr[:, 128:256],
                        vbf[:, h * T + t * 128:h * T + (t + 1) * 128],
                        sident[:]), "te")
                block.tensor(trs)
                te_now = c.counts["te"]

                def evk(vector, te_now=te_now):
                    vector.wait_ge(c.sems["te"], te_now)
                    c.inc(vector.tensor_copy(out=ktm[:], in_=ptr[:, 0:128]),
                          "ve")
                block.vector(evk)

                def evv(scalar, te_now=te_now):
                    scalar.wait_ge(c.sems["te"], te_now)
                    c.inc(scalar.copy(out=vtm[:], in_=ptr[:, 128:256]), "se")
                block.scalar(evv)
                ve_now2 = c.counts["ve"]
                se_now2 = c.counts["se"]

                def mmkv(tensor, h=h, t=t, ve_now2=ve_now2, se_now2=se_now2):
                    tensor.wait_ge(c.sems["ve"], ve_now2)
                    tensor.wait_ge(c.sems["se"], se_now2)
                    c.inc(tensor.matmul(
                        pkv[:, (h % 4) * 128:(h % 4 + 1) * 128], ktm[:], vtm[:],
                        start=(t == 0), stop=(t == 7)), "te")
                block.tensor(mmkv)

            te_now = c.counts["te"]

            def evkv(vector, h=h, te_now=te_now):
                vector.wait_ge(c.sems["te"], te_now)
                c.inc(vector.tensor_copy(
                    out=kvsb[:, h * 128:(h + 1) * 128],
                    in_=pkv[:, (h % 4) * 128:(h % 4 + 1) * 128]), "ve")
            block.vector(evkv)
            ve_kv = c.counts["ve"]

            def guard(tensor, ve_kv=ve_kv):
                tensor.wait_ge(c.sems["ve"], ve_kv)
            block.tensor(guard)

        ve_end = c.counts["ve"]
        se_end = c.counts["se"]

        def stores(s):
            s.wait_ge(c.sems["ve"], ve_end)
            s.wait_ge(c.sems["se"], se_end)
            for m in range(KC):
                c.inc(s.dma_start(out=q_out[m * 128:(m + 1) * 128, :],
                                  in_=qbf[:, m * T:(m + 1) * T]), "st", 16)
            c.inc(s.dma_start(out=kv_out, in_=kvsb[:, :]), "st", 16)
        block.sync(stores)


def _wrap(emitter, io_decl, extra=()):
    from contextlib import ExitStack

    def build():
        nc = bass.Bass("TRN2", target_bir_lowering=False)
        io = {}
        for name, shape, dt, kind in io_decl:
            io[name] = nc.dram_tensor(name, shape, dt, kind=kind).ap()
        with ExitStack() as st0:
            c = mk_ctx(nc, st0)
            c.block = st0.enter_context(nc.Block())
            c.new_phase()
            emitter(c, io, *extra)
            st_end = c.counts["st"]

            def fin(g):
                g.wait_ge(c.sems["st"], st_end)
            c.block.gpsimd(fin)
        return nc
    return build


build_l0a = _wrap(emit_l0a, [
    ("x", [D, T], F32, "ExternalInput"),
    ("wqk", [D, D], F32, "ExternalInput"),
    ("wv", [D, D], F32, "ExternalInput"),
    ("wu", [D, D], F32, "ExternalInput"),
    ("bqk", [128, KC], F32, "ExternalInput"),
    ("bv", [128, KC], F32, "ExternalInput"),
    ("bu", [128, KC], F32, "ExternalInput"),
    ("ident", [128, 128], BF16, "ExternalInput"),
    ("q_out", [D, T], BF16, "ExternalOutput"),
    ("u_out", [D, T], F32, "ExternalOutput"),
    ("kv_out", [128, KC * 128], F32, "ExternalOutput"),
])


def _dump(c, src_sb, dst_hbm, N, bf=False):
    block = c.block
    ve0, se0, te0 = c.counts["ve"], c.counts["se"], c.counts["te"]

    def st(s):
        s.wait_ge(c.sems["ve"], ve0)
        s.wait_ge(c.sems["se"], se0)
        s.wait_ge(c.sems["te"], te0)
        for k in range(KC):
            c.inc(s.dma_start(out=dst_hbm[k * 128:(k + 1) * 128, :],
                              in_=src_sb[:, k * N:(k + 1) * N]), "st", 16)
    block.sync(st)
    st1 = c.counts["st"]

    def gate(vector):
        c.wait(vector, "st", st1)
    block.vector(gate)

    def gate2(scalar):
        c.wait(scalar, "st", st1)
    block.scalar(gate2)



def emit_ln(c, src, dst, N, scr, gcol=None, bcol=None, eps=1e-5, src_wait=None):
    block = c.block
    if src_wait:
        def _gate(vector, src_wait=dict(src_wait)):
            for s_, v_ in src_wait.items():
                vector.wait_ge(c.sems[s_], v_)
        block.vector(_gate)
    for k in range(KC):
        ve0, se0, te0 = c.counts["ve"], c.counts["se"], c.counts["te"]

        def cast(vector, k=k, te0=te0):
            if k >= 2:
                vector.wait_ge(c.sems["te"], te0)
            c.inc(vector.tensor_copy(out=scr["obscr"][:, (k % 2) * N:(k % 2 + 1) * N],
                                     in_=src(k)), "ve")
        block.vector(cast)
        ve1 = c.counts["ve"]

        def sq(vector, k=k):
            c.inc(vector.tensor_mul(out=scr["sqscr"][:, (k % 2) * N:(k % 2 + 1) * N],
                                    in0=scr["obscr"][:, (k % 2) * N:(k % 2 + 1) * N],
                                    in1=scr["obscr"][:, (k % 2) * N:(k % 2 + 1) * N]),
                  "ve")
        block.vector(sq)
        ve1b = c.counts["ve"]

        def red(tensor, k=k, ve1b=ve1b):
            c.wait(tensor, "ve", ve1b)
            c.inc(tensor.matmul(scr["pstat"][0:1, 0:N], scr["onescol"],
                                scr["obscr"][:, (k % 2) * N:(k % 2 + 1) * N],
                                start=(k == 0), stop=(k == KC - 1)), "te")
            c.inc(tensor.matmul(scr["pstat"][0:1, N:2 * N], scr["onescol"],
                                scr["sqscr"][:, (k % 2) * N:(k % 2 + 1) * N],
                                start=(k == 0), stop=(k == KC - 1)), "te")
        block.tensor(red)
    te_red = c.counts["te"]

    def rows(vector):
        c.wait(vector, "te", te_red)
        c.inc(vector.tensor_copy(out=scr["rowbuf"][0:1, :], in_=scr["pstat"][0:1, :]),
              "ve")
    block.vector(rows)
    ve_rows = c.counts["ve"]

    def bc(tensor):
        c.wait(tensor, "ve", ve_rows)
        c.inc(tensor.matmul(scr["pb"][:, 0:N], scr["onesrow"],
                            scr["rowbuf"][0:1, 0:N], start=True, stop=True), "te")
        c.inc(tensor.matmul(scr["pb"][:, N:2 * N], scr["onesrow"],
                            scr["rowbuf"][0:1, N:2 * N], start=True, stop=True),
              "te")
    block.tensor(bc)
    te_bc = c.counts["te"]

    def stats_v(vector):
        c.wait(vector, "te", te_bc)
        vector.memset(scr["epscol"][:, :], float(eps))
        c.inc(vector.tensor_scalar_mul(out=scr["meanB"][:, :], in0=scr["pb"][:, 0:N],
                                       scalar1=1.0 / D), "ve")
        c.inc(vector.tensor_scalar_mul(out=scr["rstdB"][:, :], in0=scr["pb"][:, N:2 * N],
                                       scalar1=1.0 / D), "ve")
        c.inc(vector.scalar_tensor_tensor(out=scr["mrstdB"][:, :], in0=scr["meanB"][:, :],
                                          scalar=0.0, in1=scr["meanB"][:, :],
                                          op0=ALU.add, op1=ALU.mult), "ve")
        c.inc(vector.tensor_sub(out=scr["rstdB"][:, :], in0=scr["rstdB"][:, :],
                                in1=scr["mrstdB"][:, :]), "ve")
    block.vector(stats_v)
    ve_st = c.counts["ve"]

    def stats_s(scalar):
        c.wait(scalar, "ve", ve_st)
        c.inc(scalar.activation(out=scr["rstdB"][:, :], in_=scr["rstdB"][:, :], func=AF.Ln,
                                bias=scr["epscol"][:, 0:1]), "se")
        c.inc(scalar.activation(out=scr["rstdB"][:, :], in_=scr["rstdB"][:, :], func=AF.Exp,
                                scale=-0.5), "se")
    block.scalar(stats_s)
    se_st = c.counts["se"]

    def mr(vector):
        c.wait(vector, "se", se_st)
        c.inc(vector.tensor_mul(out=scr["mrstdB"][:, :], in0=scr["meanB"][:, :],
                                in1=scr["rstdB"][:, :]), "ve")
    block.vector(mr)

    for k in range(KC):
        def norm(vector, k=k):
            c.inc(vector.tensor_mul(out=dst(k), in0=src(k), in1=scr["rstdB"][:, :]),
                  "ve")
            c.inc(vector.tensor_sub(out=dst(k), in0=dst(k), in1=scr["mrstdB"][:, :]),
                  "ve")
        block.vector(norm)
        if gcol is not None:
            def aff(vector, k=k):
                c.inc(vector.tensor_scalar(out=dst(k), in0=dst(k),
                                           scalar1=gcol(k), scalar2=bcol(k),
                                           op0=ALU.mult, op1=ALU.add), "ve")
            block.vector(aff)


def make_ln_scratch(ctx, nc, N, tag="", psum_from=None):
    e = ctx.enter_context
    if psum_from is not None:
        pstat = psum_from[0][0:1, 0:2 * N]
        pb = psum_from[1][:, 0:2 * N]
    else:
        pstat = e(nc.psum_tensor(f"ln_ps{tag}", [1, 2 * N], F32))
        pb = e(nc.psum_tensor(f"ln_pb{tag}", [128, 2 * N], F32))
    return {
        "pstat": pstat,
        "pb": pb,
        "obscr": e(nc.sbuf_tensor(f"ln_ob{tag}", [128, 2 * N], BF16)),
        "sqscr": e(nc.sbuf_tensor(f"ln_sq{tag}", [128, 2 * N], BF16)),
        "rowbuf": e(nc.sbuf_tensor(f"ln_row{tag}", [1, 2 * N], F32)),
        "meanB": e(nc.sbuf_tensor(f"ln_mb{tag}", [128, N], F32)),
        "rstdB": e(nc.sbuf_tensor(f"ln_rb{tag}", [128, N], F32)),
        "mrstdB": e(nc.sbuf_tensor(f"ln_mrb{tag}", [128, N], F32)),
        "epscol": e(nc.sbuf_tensor(f"ln_eps{tag}", [128, 1], F32)),
    }



def emit_l0b(c, io, kv_pair=True, cc_wait=None):
    nc = c.nc
    block = c.block
    q, u, x, wo = io["q"], io["u"], io["x"], io["wo"]
    lng, lnb, bo = io["lng"], io["lnb"], io["bo"]
    allones, onesrow_f = io["allones"], io["onesrow_f"]
    x1_out = io["x1_out"]
    debug = False

    N = 512
    from contextlib import ExitStack
    with ExitStack() as ctx:
        e = ctx.enter_context
        qh = e(nc.sbuf_tensor("qh", [128, KC * N], BF16))
        kvbf = e(nc.sbuf_tensor("kvbf", [128, KC * 128], BF16))
        outsb = e(nc.sbuf_tensor("outsb", [128, KC * N], F32))
        kvf = outsb[:, 0:KC * 128]
        kvf2 = outsb[:, KC * 128:2 * KC * 128]
        uh = e(nc.sbuf_tensor("uh", [128, KC * N], F32))
        xh = e(nc.sbuf_tensor("xh", [128, KC * N], F32))
        zbf = e(nc.sbuf_tensor("zbf", [128, KC * N], BF16))
        wsb = e(nc.sbuf_tensor("wsb", [128, KC * 1024], BF16))
        cols = e(nc.sbuf_tensor("cols", [128, 3 * KC], F32))
        sones = e(nc.sbuf_tensor("sones", [128, 128], BF16))
        sonesrow_f = e(nc.sbuf_tensor("sonesrow_f", [1, 128], F32))
        ressb = outsb
        scr = make_ln_scratch(ctx, nc, N)
        scr["onescol"] = sones[:, 0:1]
        scr["onesrow"] = sonesrow_f[:]
        po = e(nc.psum_tensor("po", [128, 2 * N], F32))

        def loads(s):
            if cc_wait:
                s.wait_ge(c.sems[cc_wait], 1)
            if kv_pair:
                c.inc(s.dma_start(out=kvf, in_=io["kv0"]), "lds", 16)
                c.inc(s.dma_start(out=kvf2, in_=io["kv1"]), "lds", 16)
            else:
                c.inc(s.dma_start(out=kvf, in_=io["kvr"]), "lds", 16)
            c.inc(s.dma_start(out=cols[:, 0:KC], in_=lng), "lds", 16)
            c.inc(s.dma_start(out=cols[:, KC:2 * KC], in_=lnb), "lds", 16)
            c.inc(s.dma_start(out=cols[:, 2 * KC:3 * KC], in_=bo), "lds", 16)
            c.inc(s.dma_start(out=sones[:], in_=allones), "lds", 16)
            c.inc(s.dma_start(out=sonesrow_f[:], in_=onesrow_f), "lds", 16)
        block.sync(loads)
        ld_const = c.counts["lds"]

        def addkv(vector):
            c.wait(vector, "lds", 32 if kv_pair else 16)
            if kv_pair:
                c.inc(vector.tensor_add(out=kvbf[:], in0=kvf[:], in1=kvf2[:]),
                      "ve")
            else:
                c.inc(vector.tensor_copy(out=kvbf[:], in_=kvf[:]), "ve")
        block.vector(addkv)
        ve_kv = c.counts["ve"]

        for nh in range(2):
            n0 = nh * N
            ld_h = None

            def loads_h(s, n0=n0):
                for k in range(KC):
                    c.inc(s.dma_start(out=qh[:, k * N:(k + 1) * N],
                                      in_=q[k * 128:(k + 1) * 128, n0:n0 + N]),
                          "lds", 16)
                for k in range(KC):
                    c.inc(s.dma_start(out=uh[:, k * N:(k + 1) * N],
                                      in_=u[k * 128:(k + 1) * 128, n0:n0 + N]),
                          "lds", 16)
                for k in range(KC):
                    c.inc(s.dma_start(out=xh[:, k * N:(k + 1) * N],
                                      in_=x[k * 128:(k + 1) * 128, n0:n0 + N]),
                          "lds", 16)
            te_prev = c.counts["te"]
            ve_prev = c.counts["ve"]

            def loads_h_guard(s, te_prev=te_prev, ve_prev=ve_prev, n0=n0):
                s.wait_ge(c.sems["te"], te_prev)
                s.wait_ge(c.sems["ve"], ve_prev)
                loads_h(s, n0)
            block.sync(loads_h_guard)
            ld_h = c.counts["lds"]

            for h in range(KC):
                ve0, se0 = c.counts["ve"], c.counts["se"]

                def mmh(tensor, h=h, ld_h=ld_h, ve0=ve0, se0=se0):
                    if h == 0:
                        tensor.wait_ge(c.sems["lds"], ld_h)
                        tensor.wait_ge(c.sems["ve"], ve_kv)
                    else:
                        tensor.wait_ge(c.sems["ve"], ve0)
                    c.inc(tensor.matmul(po[:, (h % 2) * N:(h % 2 + 1) * N],
                                        kvbf[:, h * 128:(h + 1) * 128],
                                        qh[:, h * N:(h + 1) * N],
                                        start=True, stop=True), "te")
                block.tensor(mmh)
                te_h = c.counts["te"]

                def evh(vector, h=h, te_h=te_h):
                    c.wait(vector, "te", te_h)
                    c.inc(vector.tensor_copy(out=outsb[:, h * N:(h + 1) * N],
                                             in_=po[:, (h % 2) * N:(h % 2 + 1) * N]),
                          "ve")
                block.vector(evh)

            if debug and nh == 0:
                _dump(c, outsb, dbg[0], N)
            emit_ln(c, lambda k: outsb[:, k * N:(k + 1) * N],
                    lambda k: outsb[:, k * N:(k + 1) * N], N, scr,
                    gcol=lambda k: cols[:, k:k + 1],
                    bcol=lambda k: cols[:, KC + k:KC + k + 1],
                    src_wait={"ve": c.counts["ve"]})
            se_ln = c.counts["se"]
            ve_ln = c.counts["ve"]

            if debug and nh == 0:
                _dump(c, outsb, dbg[1], N)

            def mulu(vector, se_ln=se_ln):
                c.wait(vector, "se", se_ln)
                c.wait(vector, "lds", ld_h)
                for k in range(KC):
                    c.inc(vector.tensor_mul(out=zbf[:, k * N:(k + 1) * N],
                                            in0=outsb[:, k * N:(k + 1) * N],
                                            in1=uh[:, k * N:(k + 1) * N]), "ve")
            block.vector(mulu)
            ve_z = c.counts["ve"]

            for mh in range(2):
                te_prev2 = c.counts["te"]

                def ldw(g, mh=mh, te_prev2=te_prev2):
                    g.wait_ge(c.sems["te"], te_prev2)
                    for k in range(KC):
                        c.inc(g.dma_start(
                            out=wsb[:, k * 1024:(k + 1) * 1024],
                            in_=wo[k * 128:(k + 1) * 128,
                                   mh * 1024:(mh + 1) * 1024]), "ldg", 16)
                block.gpsimd(ldw)
                ld_w = c.counts["ldg"]

                for mm_ in range(8):
                    m = mh * 8 + mm_
                    ve0 = c.counts["ve"]

                    def mmo(tensor, m=m, mm_=mm_, ld_w=ld_w, ve0=ve0, ve_z=ve_z):
                        if mm_ == 0:
                            tensor.wait_ge(c.sems["ldg"], ld_w)
                            tensor.wait_ge(c.sems["ve"], ve_z)
                        else:
                            tensor.wait_ge(c.sems["ve"], ve0)
                        ins = mm_tiles(c, tensor, po[:, (mm_ % 2) * N:(mm_ % 2 + 1) * N],
                                       lambda k: wsb[:, k * 1024:(k + 1) * 1024],
                                       lambda k: zbf[:, k * N:(k + 1) * N],
                                       mm_, 0, N)
                        c.inc(ins, "te")
                    block.tensor(mmo)
                    te_o = c.counts["te"]

                    def evo(vector, m=m, mm_=mm_, te_o=te_o):
                        c.wait(vector, "te", te_o)
                        c.inc(vector.scalar_tensor_tensor(
                            out=ressb[:, m * N:(m + 1) * N],
                            in0=po[:, (mm_ % 2) * N:(mm_ % 2 + 1) * N],
                            scalar=cols[:, 2 * KC + m:2 * KC + m + 1],
                            in1=xh[:, m * N:(m + 1) * N],
                            op0=ALU.add, op1=ALU.add), "ve")
                    block.vector(evo)

            if debug and nh == 0:
                _dump(c, zbf, dbg[2], N, bf=True)
                _dump(c, ressb, dbg[3], N)
            emit_ln(c, lambda k: ressb[:, k * N:(k + 1) * N],
                    lambda k: ressb[:, k * N:(k + 1) * N], N, scr,
                    src_wait={"ve": c.counts["ve"]})
            ve_pl = c.counts["ve"]
            se_pl = c.counts["se"]

            def store(s, n0=n0, ve_pl=ve_pl, se_pl=se_pl):
                s.wait_ge(c.sems["ve"], ve_pl)
                s.wait_ge(c.sems["se"], se_pl)
                for k in range(KC):
                    c.inc(s.dma_start(out=x1_out[k * 128:(k + 1) * 128, n0:n0 + N],
                                      in_=ressb[:, k * N:(k + 1) * N]), "st", 16)
            block.sync(store)


build_l0b = _wrap(emit_l0b, [
    ("q", [D, T], BF16, "ExternalInput"),
    ("kv0", [128, KC * 128], F32, "ExternalInput"),
    ("kv1", [128, KC * 128], F32, "ExternalInput"),
    ("u", [D, T], F32, "ExternalInput"),
    ("x", [D, T], F32, "ExternalInput"),
    ("wo", [D, D], F32, "ExternalInput"),
    ("lng", [128, KC], F32, "ExternalInput"),
    ("lnb", [128, KC], F32, "ExternalInput"),
    ("bo", [128, KC], F32, "ExternalInput"),
    ("allones", [128, 128], BF16, "ExternalInput"),
    ("onesrow_f", [1, 128], F32, "ExternalInput"),
    ("x1_out", [D, T], F32, "ExternalOutput"),
])


def run_l0b(q_sh, kv_sh, u_sh, x_shards, Wo, bo, ln_g, ln_b):
    nc = build_l0b()
    shared = {
        "wo": np.ascontiguousarray(Wo), "bo": _bias_cols(bo),
        "lng": _bias_cols(ln_g), "lnb": _bias_cols(ln_b),
        "allones": np.ones((128, 128), BF),
        "onesrow_f": np.ones((1, 128), np.float32),
    }
    in_maps = []
    for cix in range(NCORES):
        partner = cix ^ 1
        in_maps.append(dict(shared, q=q_sh[cix], kv0=kv_sh[cix],
                            kv1=kv_sh[partner], u=u_sh[cix], x=x_shards[cix]))
    res = run_bass_kernel_spmd(nc, in_maps, core_ids=list(range(NCORES)))
    return res.results



INTER = 1408
IC = INTER // 128
SH_G = 2


def emit_l0f(c, io):
    nc = c.nc
    block = c.block
    debug = False
    x1, gwt = io["x1"], io["gwt"]
    w1t, w3t, w2t = io["w1t"], io["w3t"], io["w2t"]
    sw1t, sw3t, sw2t = io["sw1t"], io["sw3t"], io["sw2t"]
    ind, ident = io["ind"], io["ident"]
    allones, onesrow_f = io["allones"], io["onesrow_f"]
    x2_out = io["x2_out"]

    N = 512
    from contextlib import ExitStack
    with ExitStack() as ctx:
        e = ctx.enter_context
        xbf = e(nc.sbuf_tensor("xbf", [128, KC * T], BF16))
        yacc = e(nc.sbuf_tensor("yacc", [128, KC * T], F32))
        wbuf = e(nc.sbuf_tensor("wbuf", [128, KC * INTER], BF16))
        hsc = e(nc.sbuf_tensor("hsc", [128, IC * T], BF16))
        hx = e(nc.sbuf_tensor("hx", [128, T], F32))
        gbuf = e(nc.sbuf_tensor("gbuf", [128, KC * 16], BF16))
        stm = e(nc.sbuf_tensor("stm", [128, 8 * 16], F32))
        ptile = e(nc.sbuf_tensor("ptile", [128, 128], F32))
        mk = e(nc.sbuf_tensor("mk", [128, 128], F32))
        rcols = e(nc.sbuf_tensor("rcols", [128, 40], F32))
        wtmb = e(nc.sbuf_tensor("wtmb", [128, 128], BF16))
        wfm = e(nc.sbuf_tensor("wfm", [16, T], BF16))
        sind = e(nc.sbuf_tensor("sind", [16, D], BF16))
        sident = e(nc.sbuf_tensor("sident", [128, 128], BF16))
        sones = e(nc.sbuf_tensor("sones", [128, 128], BF16))
        sonesrow_f = e(nc.sbuf_tensor("sonesrow_f", [1, 128], F32))
        ph = e(nc.psum_tensor("ph", [128, T], F32))
        py = e(nc.psum_tensor("py", [128, T], F32))
        pmisc = e(nc.psum_tensor("pmisc", [128, T], F32))
        pw = e(nc.psum_tensor("pw", [16, 1024], BF16))
        scr = make_ln_scratch(ctx, nc, N, psum_from=(ph, py))
        scr["onescol"] = sones[:, 0:1]
        scr["onesrow"] = sonesrow_f[:]

        def loads_s(s):
            for k in range(KC):
                c.inc(s.dma_start(out=yacc[:, k * T:(k + 1) * T],
                                  in_=x1[k * 128:(k + 1) * 128, :]), "lds", 16)
            c.inc(s.dma_start(out=sind[:], in_=ind), "lds", 16)
            c.inc(s.dma_start(out=sident[:], in_=ident), "lds", 16)
            c.inc(s.dma_start(out=sones[:], in_=allones), "lds", 16)
            c.inc(s.dma_start(out=sonesrow_f[:], in_=onesrow_f), "lds", 16)
        block.sync(loads_s)
        ld_consts = c.counts["lds"]

        def loads_g(g):
            for k in range(KC):
                c.inc(g.dma_start(out=xbf[:, k * T:(k + 1) * T],
                                  in_=x1[k * 128:(k + 1) * 128, :]), "ldg", 16)
            for k in range(KC):
                c.inc(g.dma_start(out=gbuf[:, k * 16:(k + 1) * 16],
                                  in_=gwt[k * 128:(k + 1) * 128, :]), "ldg", 16)
        block.gpsimd(loads_g)
        ld_xg = c.counts["ldg"]

        def rmms(tensor):
            tensor.wait_ge(c.sems["ldg"], ld_xg)
            for t in range(8):
                for k in range(KC):
                    c.inc(tensor.matmul(
                        pmisc[:, t * 16:(t + 1) * 16],
                        xbf[:, k * T + t * 128:k * T + (t + 1) * 128],
                        gbuf[:, k * 16:(k + 1) * 16],
                        start=(k == 0), stop=(k == KC - 1)), "te")
        block.tensor(rmms)
        te_r = c.counts["te"]

        def rv(vector):
            c.wait(vector, "te", te_r)
            c.inc(vector.tensor_copy(out=stm[:], in_=pmisc[:, 0:128]), "ve")
        block.vector(rv)
        ve_r = c.counts["ve"]

        def sl(t):
            return stm[:, t * 16:(t + 1) * 16]

        def rtop(vector):
            for t in range(8):
                c.inc(vector.reduce_max(out=rcols[:, t:t + 1], in_=sl(t),
                                        axis=mybir.AxisListType.X), "ve")
            for t in range(8):
                c.inc(vector.tensor_scalar_mul(out=rcols[:, 8 + t:9 + t],
                                               in0=rcols[:, t:t + 1],
                                               scalar1=-1.0), "ve")
            for t in range(8):
                c.inc(vector.tensor_scalar(out=mk[:, t * 16:(t + 1) * 16],
                                           in0=sl(t),
                                           scalar1=rcols[:, t:t + 1],
                                           scalar2=None, op0=ALU.is_ge), "ve")
            for t in range(8):
                c.inc(vector.scalar_tensor_tensor(
                    out=mk[:, t * 16:(t + 1) * 16],
                    in0=mk[:, t * 16:(t + 1) * 16], scalar=-1e30, in1=sl(t),
                    op0=ALU.mult, op1=ALU.add), "ve")
            for t in range(8):
                c.inc(vector.reduce_max(out=rcols[:, 16 + t:17 + t],
                                        in_=mk[:, t * 16:(t + 1) * 16],
                                        axis=mybir.AxisListType.X), "ve")
        block.vector(rtop)
        ve_t = c.counts["ve"]

        def rexp(scalar):
            c.wait(scalar, "ve", ve_t)
            for t in range(8):
                c.inc(scalar.activation(out=ptile[:, t * 16:(t + 1) * 16],
                                        in_=sl(t), func=AF.Exp,
                                        bias=rcols[:, 8 + t:9 + t]), "se")
        block.scalar(rexp)
        se_t = c.counts["se"]

        def rw(vector):
            c.wait(vector, "se", se_t)
            for t in range(8):
                c.inc(vector.reduce_sum(out=rcols[:, 24 + t:25 + t],
                                        in_=ptile[:, t * 16:(t + 1) * 16],
                                        axis=mybir.AxisListType.X), "ve")
            for t in range(8):
                c.inc(vector.reciprocal(out=rcols[:, 32 + t:33 + t],
                                        in_=rcols[:, 24 + t:25 + t]), "ve")
            for t in range(8):
                c.inc(vector.tensor_scalar(out=mk[:, t * 16:(t + 1) * 16],
                                           in0=sl(t),
                                           scalar1=rcols[:, 16 + t:17 + t],
                                           scalar2=None, op0=ALU.is_ge), "ve")
            for t in range(8):
                c.inc(vector.tensor_mul(out=ptile[:, t * 16:(t + 1) * 16],
                                        in0=ptile[:, t * 16:(t + 1) * 16],
                                        in1=mk[:, t * 16:(t + 1) * 16]), "ve")
            for t in range(8):
                c.inc(vector.tensor_scalar(out=wtmb[:, t * 16:(t + 1) * 16],
                                           in0=ptile[:, t * 16:(t + 1) * 16],
                                           scalar1=rcols[:, 32 + t:33 + t],
                                           scalar2=None, op0=ALU.mult), "ve")
        block.vector(rw)
        ve_t2 = c.counts["ve"]

        def rtr(tensor):
            c.wait(tensor, "ve", ve_t2)
            for t in range(8):
                c.inc(tensor.transpose(pw[:, t * 128:(t + 1) * 128],
                                       wtmb[:, t * 16:(t + 1) * 16],
                                       sident[:]), "te")
        block.tensor(rtr)
        te_t = c.counts["te"]

        def rev(vector):
            c.wait(vector, "te", te_t)
            c.inc(vector.tensor_copy(out=wfm[:], in_=pw[:]), "ve")
        block.vector(rev)

        se_wfm = c.counts["ve"]
        if debug:
            ve_d, se_d = c.counts["ve"], c.counts["se"]

            def dmp0(s, ve_d=ve_d, se_d=se_d):
                s.wait_ge(c.sems["ve"], ve_d)
                s.wait_ge(c.sems["se"], se_d)
                c.inc(s.dma_start(out=dstm, in_=stm[:]), "st", 16)
                c.inc(s.dma_start(out=dwfm, in_=wfm[:]), "st", 16)
            block.sync(dmp0)

        for ep in range(18):
            gated = ep < 16
            eix = ep if gated else None
            g = None if gated else ep - 16
            nic = IC

            if gated:
                ve_pm = c.counts["ve"]
                se_pm = c.counts["se"]

                def bca(tensor, eix=eix, ve_pm=ve_pm, se_pm=se_pm):
                    tensor.wait_ge(c.sems["ve"], se_wfm)
                    tensor.wait_ge(c.sems["ve"], ve_pm)
                    tensor.wait_ge(c.sems["se"], se_pm)
                    for n0 in (0, N):
                        c.inc(tensor.matmul(
                            pmisc[:, n0:n0 + N],
                            sind[:, eix * 128:(eix + 1) * 128],
                            wfm[:, n0:n0 + N], start=True, stop=True), "te")
                block.tensor(bca)

            te_prevC = c.counts["te"]

            def lw1(gp, ep=ep, g=g, te_prevC=te_prevC):
                gp.wait_ge(c.sems["te"], te_prevC)
                for k in range(KC):
                    if ep < 16:
                        src = w1t[ep, k * 128:(k + 1) * 128, :]
                    else:
                        src = sw1t[k * 128:(k + 1) * 128,
                                   g * INTER:(g + 1) * INTER]
                    c.inc(gp.dma_start(out=wbuf[:, k * INTER:(k + 1) * INTER],
                                       in_=src), "ldg", 16)
            block.gpsimd(lw1)
            ld_w1 = c.counts["ldg"]

            for i in range(nic):
                se_pa = c.counts["se"]

                def mma(tensor, i=i, ld_w1=ld_w1, se_pa=se_pa):
                    tensor.wait_ge(c.sems["ldg"], ld_w1)
                    tensor.wait_ge(c.sems["se"], se_pa)
                    ins = mm_tiles2(c, tensor, ph,
                                    lambda k: wbuf[:, k * INTER:(k + 1) * INTER],
                                    lambda k: xbf[:, k * T:(k + 1) * T],
                                    i, [(0, 0, N), (N, N, N)])
                    c.inc(ins, "te")
                block.tensor(mma)
                te_a = c.counts["te"]

                def eva(scalar, i=i, te_a=te_a):
                    scalar.wait_ge(c.sems["te"], te_a)
                    c.inc(scalar.activation(out=hsc[:, i * T:(i + 1) * T],
                                            in_=ph[:], func=AF.Silu), "se")
                block.scalar(eva)

            te_prevA = c.counts["te"]

            def lw3(gp, ep=ep, g=g, te_prevA=te_prevA):
                gp.wait_ge(c.sems["te"], te_prevA)
                for k in range(KC):
                    if ep < 16:
                        src = w3t[ep, k * 128:(k + 1) * 128, :]
                    else:
                        src = sw3t[k * 128:(k + 1) * 128,
                                   g * INTER:(g + 1) * INTER]
                    c.inc(gp.dma_start(out=wbuf[:, k * INTER:(k + 1) * INTER],
                                       in_=src), "ldg", 16)
            block.gpsimd(lw3)
            ld_w3 = c.counts["ldg"]
            se_afterA = c.counts["se"]

            for i in range(nic):
                ve_pb = c.counts["ve"]

                def mmb(tensor, i=i, ld_w3=ld_w3, ve_pb=ve_pb,
                        se_afterA=se_afterA):
                    tensor.wait_ge(c.sems["ldg"], ld_w3)
                    tensor.wait_ge(c.sems["ve"], ve_pb)
                    if i == 0:
                        tensor.wait_ge(c.sems["se"], se_afterA)
                    ins = mm_tiles2(c, tensor, ph,
                                    lambda k: wbuf[:, k * INTER:(k + 1) * INTER],
                                    lambda k: xbf[:, k * T:(k + 1) * T],
                                    i, [(0, 0, N), (N, N, N)])
                    c.inc(ins, "te")
                block.tensor(mmb)
                te_b = c.counts["te"]

                def evb(vector, i=i, te_b=te_b, gated=gated):
                    vector.wait_ge(c.sems["te"], te_b)
                    if gated:
                        c.inc(vector.tensor_mul(out=hx[:], in0=ph[:],
                                                in1=hsc[:, i * T:(i + 1) * T]),
                              "ve")
                        c.inc(vector.tensor_mul(out=hsc[:, i * T:(i + 1) * T],
                                                in0=hx[:], in1=pmisc[:]), "ve")
                    else:
                        c.inc(vector.tensor_mul(out=hsc[:, i * T:(i + 1) * T],
                                                in0=ph[:],
                                                in1=hsc[:, i * T:(i + 1) * T]),
                              "ve")
                block.vector(evb)

            te_prevB = c.counts["te"]
            ve_hsc = c.counts["ve"]

            def lw2(gp, ep=ep, g=g, te_prevB=te_prevB):
                gp.wait_ge(c.sems["te"], te_prevB)
                for i in range(IC):
                    if ep < 16:
                        src = w2t[ep, i * 128:(i + 1) * 128, :]
                    else:
                        src = sw2t[(g * IC + i) * 128:(g * IC + i + 1) * 128, :]
                    c.inc(gp.dma_start(out=wbuf[:, i * D:(i + 1) * D],
                                       in_=src), "ldg", 16)
            block.gpsimd(lw2)
            ld_w2 = c.counts["ldg"]

            for m in range(KC):
                ve_pc = c.counts["ve"]

                def mmc(tensor, m=m, ld_w2=ld_w2, ve_pc=ve_pc, ve_hsc=ve_hsc):
                    tensor.wait_ge(c.sems["ldg"], ld_w2)
                    tensor.wait_ge(c.sems["ve"], max(ve_pc, ve_hsc))
                    ins = mm_tiles2(c, tensor, py,
                                    lambda i: wbuf[:, i * D:(i + 1) * D],
                                    lambda i: hsc[:, i * T:(i + 1) * T],
                                    m, [(0, 0, N), (N, N, N)], kcs=range(IC))
                    c.inc(ins, "te")
                block.tensor(mmc)
                te_c = c.counts["te"]

                def evc(vector, m=m, te_c=te_c):
                    vector.wait_ge(c.sems["te"], te_c)
                    if m == 0:
                        vector.wait_ge(c.sems["lds"], 16 * KC)
                    c.inc(vector.tensor_add(out=yacc[:, m * T:(m + 1) * T],
                                            in0=py[:],
                                            in1=yacc[:, m * T:(m + 1) * T]),
                          "ve")
                block.vector(evc)

            if debug and ep == 0:
                ve_d, se_d, te_d, st_d0 = (c.counts["ve"], c.counts["se"],
                                           c.counts["te"], c.counts["st"])

                def dmp1(s, ve_d=ve_d, se_d=se_d, te_d=te_d):
                    s.wait_ge(c.sems["ve"], ve_d)
                    s.wait_ge(c.sems["se"], se_d)
                    s.wait_ge(c.sems["te"], te_d)
                    for i in range(IC):
                        c.inc(s.dma_start(out=dhsc[:, i * T:(i + 1) * T],
                                          in_=hsc[:, i * T:(i + 1) * T]),
                              "st", 16)
                    for k in range(KC):
                        c.inc(s.dma_start(out=dyac[k * 128:(k + 1) * 128, :],
                                          in_=yacc[:, k * T:(k + 1) * T]),
                              "st", 16)
                block.sync(dmp1)
                st_d1 = c.counts["st"]

                def dgate_v(vector, st_d1=st_d1):
                    vector.wait_ge(c.sems["st"], st_d1)
                block.vector(dgate_v)

                def dgate_t(tensor, st_d1=st_d1):
                    tensor.wait_ge(c.sems["st"], st_d1)
                block.tensor(dgate_t)

                def dgate_s(scalar, st_d1=st_d1):
                    scalar.wait_ge(c.sems["st"], st_d1)
                block.scalar(dgate_s)

        for nh in range(2):
            n0 = nh * N
            emit_ln(c, lambda k: yacc[:, k * T + n0:k * T + n0 + N],
                    lambda k: yacc[:, k * T + n0:k * T + n0 + N], N, scr,
                    src_wait={"ve": c.counts["ve"]})
            ve_pl = c.counts["ve"]

            def store(s, n0=n0, ve_pl=ve_pl):
                s.wait_ge(c.sems["ve"], ve_pl)
                for k in range(KC):
                    c.inc(s.dma_start(out=x2_out[k * 128:(k + 1) * 128,
                                                 n0:n0 + N],
                                      in_=yacc[:, k * T + n0:k * T + n0 + N]),
                          "st", 16)
            block.sync(store)


build_l0f = _wrap(emit_l0f, [
    ("x1", [D, T], F32, "ExternalInput"),
    ("gwt", [D, 16], F32, "ExternalInput"),
    ("w1t", [16, D, INTER], F32, "ExternalInput"),
    ("w3t", [16, D, INTER], F32, "ExternalInput"),
    ("w2t", [16, INTER, D], F32, "ExternalInput"),
    ("sw1t", [D, 2 * INTER], F32, "ExternalInput"),
    ("sw3t", [D, 2 * INTER], F32, "ExternalInput"),
    ("sw2t", [2 * INTER, D], F32, "ExternalInput"),
    ("ind", [16, D], BF16, "ExternalInput"),
    ("ident", [128, 128], BF16, "ExternalInput"),
    ("allones", [128, 128], BF16, "ExternalInput"),
    ("onesrow_f", [1, 128], F32, "ExternalInput"),
    ("x2_out", [D, T], F32, "ExternalOutput"),
])


def _make_ind():
    ind = np.zeros((16, D), BF)
    for e_ in range(16):
        ind[e_, e_ * 128:(e_ + 1) * 128] = 1
    return ind


def run_l0f(x1_sh, gate_w, w1, w2, w3, sw1, sw2, sw3):
    nc = build_l0f()
    shared = {
        "gwt": np.ascontiguousarray(gate_w.T),
        "w1t": np.ascontiguousarray(w1.transpose(0, 2, 1)),
        "w3t": np.ascontiguousarray(w3.transpose(0, 2, 1)),
        "w2t": np.ascontiguousarray(w2.transpose(0, 2, 1)),
        "sw1t": np.ascontiguousarray(sw1.T),
        "sw3t": np.ascontiguousarray(sw3.T),
        "sw2t": np.ascontiguousarray(sw2.T),
        "ind": _make_ind(),
        "ident": _IDENT_BF,
        "allones": np.ones((128, 128), BF),
        "onesrow_f": np.ones((1, 128), np.float32),
    }
    in_maps = [dict(shared, x1=x1_sh[cix]) for cix in range(NCORES)]
    res = run_bass_kernel_spmd(nc, in_maps, core_ids=list(range(NCORES)))
    return res.results



CH = 256
NCH = T // CH


def emit_l1a(c, io):
    nc = c.nc
    block = c.block
    x2, wqkvt = io["x2"], io["wqkvt"]
    cosf, sinf, rtc, maskt = io["cosf"], io["sinf"], io["rtc"], io["maskt"]
    allones, ident = io["allones"], io["ident"]
    q_out, num_out, den_out = io["q_out"], io["num_out"], io["den_out"]
    s_out, z_out = io["s_out"], io["z_out"]

    from contextlib import ExitStack
    with ExitStack() as ctx:
        e = ctx.enter_context
        xbf = e(nc.sbuf_tensor("xbf", [128, KC * T], BF16))
        wg = e(nc.sbuf_tensor("wg", [128, KC * 512], BF16))
        qsb = e(nc.sbuf_tensor("qsb", [128, KC * T], BF16))
        ksb = e(nc.sbuf_tensor("ksb", [128, KC * T], BF16))
        vsb = e(nc.sbuf_tensor("vsb", [128, KC * T], BF16))
        scos = e(nc.sbuf_tensor("scos", [128, T], F32))
        ssin = e(nc.sbuf_tensor("ssin", [128, T], F32))
        srt = e(nc.sbuf_tensor("srt", [128, 128], BF16))
        smask = e(nc.sbuf_tensor("smaskt", [128, 2 * CH], BF16))
        sident = e(nc.sbuf_tensor("sident", [128, 128], BF16))
        sones = e(nc.sbuf_tensor("sones", [128, 128], BF16))
        qb = e(nc.sbuf_tensor("qb", [128, T], BF16))
        s1 = e(nc.sbuf_tensor("s1", [128, T], F32))
        s2 = e(nc.sbuf_tensor("s2", [128, T], F32))
        ktm = e(nc.sbuf_tensor("ktm", [128, 2 * 128], BF16))
        vtm = e(nc.sbuf_tensor("vtm", [128, 2 * 128], BF16))
        attm = e(nc.sbuf_tensor("attm", [128, 2 * CH], BF16))
        sloc = e(nc.sbuf_tensor("sloc", [128, KC * 128], F32))
        sbf = e(nc.sbuf_tensor("sbf", [128, KC * 128], BF16))
        zloc = e(nc.sbuf_tensor("zloc", [128, KC], F32))
        zbf = e(nc.sbuf_tensor("zbf", [128, KC], BF16))
        ztmp = e(nc.sbuf_tensor("ztmp", [128, KC], F32))
        numscr = e(nc.sbuf_tensor("numscr", [128, 2 * CH], F32))
        denscr = e(nc.sbuf_tensor("denscr", [1, 2 * CH], F32))
        pa = e(nc.psum_tensor("pa", [128, T], F32))
        pr = e(nc.psum_tensor("pr", [128, T], F32))
        patt = e(nc.psum_tensor("patt", [128, 2 * CH], F32))
        pmix = e(nc.psum_tensor("pmix", [128, T], F32))
        ptr = e(nc.psum_tensor("ptr", [128, 4 * 128], BF16))
        pnum = pmix[:, 0:CH]
        pS = pmix[:, CH:CH + 128]
        pden = pmix[0:1, 512:512 + CH]

        def loads_s(s):
            c.inc(s.dma_start(out=scos[:], in_=cosf), "lds", 16)
            c.inc(s.dma_start(out=ssin[:], in_=sinf), "lds", 16)
            c.inc(s.dma_start(out=srt[:], in_=rtc), "lds", 16)
            c.inc(s.dma_start(out=smask[:], in_=maskt), "lds", 16)
            c.inc(s.dma_start(out=sident[:], in_=ident), "lds", 16)
            c.inc(s.dma_start(out=sones[:], in_=allones), "lds", 16)
        block.sync(loads_s)
        ld_c = c.counts["lds"]

        def loads_g(g):
            for k in range(KC):
                c.inc(g.dma_start(out=xbf[:, k * T:(k + 1) * T],
                                  in_=x2[k * 128:(k + 1) * 128, :]), "ldg", 16)
        block.gpsimd(loads_g)

        def init_v(vector):
            vector.memset(sloc[:, :], 0.0)
            vector.memset(zloc[:, :], 0.0)
            c.inc(vector.tensor_copy(out=sbf[:, :], in_=sloc[:, :]), "ve")
            c.inc(vector.tensor_copy(out=zbf[:, :], in_=zloc[:, :]), "ve")
        block.vector(init_v)

        for g in range(12):
            te_prev = c.counts["te"]

            def lw(gp, g=g, te_prev=te_prev):
                gp.wait_ge(c.sems["te"], te_prev)
                for k in range(KC):
                    c.inc(gp.dma_start(
                        out=wg[:, k * 512:(k + 1) * 512],
                        in_=wqkvt[k * 128:(k + 1) * 128,
                                  g * 512:(g + 1) * 512]), "ldg", 16)
            block.gpsimd(lw)
            ld_w = c.counts["ldg"]

            for mm_ in range(4):
                mg = g * 4 + mm_
                ve_now = c.counts["ve"]
                se_now = c.counts["se"]

                def mmp(tensor, mm_=mm_, ld_w=ld_w, ve_now=ve_now, se_now=se_now):
                    tensor.wait_ge(c.sems["ldg"], ld_w)
                    tensor.wait_ge(c.sems["ve"], ve_now)
                    tensor.wait_ge(c.sems["se"], se_now)
                    ins = mm_tiles2(c, tensor, pa,
                                    lambda k: wg[:, k * 512:(k + 1) * 512],
                                    lambda k: xbf[:, k * T:(k + 1) * T],
                                    mm_, [(0, 0, 512), (512, 512, 512)])
                    c.inc(ins, "te")
                block.tensor(mmp)
                te_p = c.counts["te"]

                if mg >= 32:
                    h = mg - 32

                    def evv(vector, h=h, te_p=te_p):
                        vector.wait_ge(c.sems["te"], te_p)
                        c.inc(vector.tensor_copy(out=vsb[:, h * T:(h + 1) * T],
                                                 in_=pa[:]), "ve")
                    block.vector(evv)
                else:
                    h = mg % KC
                    dst = qsb if mg < KC else ksb

                    def ev1(vector, te_p=te_p):
                        vector.wait_ge(c.sems["te"], te_p)
                        c.inc(vector.tensor_copy(out=qb[:], in_=pa[:]), "ve")
                    block.vector(ev1)
                    se_1 = c.counts["ve"]

                    def mmr(tensor, se_1=se_1):
                        tensor.wait_ge(c.sems["ve"], se_1)
                        for n0 in (0, 512):
                            c.inc(tensor.matmul(pr[:, n0:n0 + 512], srt[:],
                                                qb[:, n0:n0 + 512],
                                                start=True, stop=True), "te")
                    block.tensor(mmr)
                    te_r2 = c.counts["te"]

                    def ev2(vector, te_r2=te_r2):
                        vector.wait_ge(c.sems["te"], te_r2)
                        c.inc(vector.tensor_mul(out=s1[:], in0=pa[:],
                                                in1=scos[:]), "ve")
                        c.inc(vector.tensor_mul(out=s2[:], in0=pr[:],
                                                in1=ssin[:]), "ve")
                        c.inc(vector.tensor_add(out=s1[:], in0=s1[:],
                                                in1=s2[:]), "ve")
                    block.vector(ev2)
                    ve_2 = c.counts["ve"]

                    def ev3(vector, ve_2=ve_2):
                        c.inc(vector.tensor_relu(out=s2[:], in_=s1[:]), "ve")
                        c.inc(vector.tensor_scalar_min(out=s1[:], in0=s1[:],
                                                       scalar1=0.0), "ve")
                    block.vector(ev3)
                    ve_4 = c.counts["ve"]

                    def ev5(scalar, ve_4=ve_4):
                        c.wait(scalar, "ve", ve_4)
                        c.inc(scalar.activation(out=s1[:], in_=s1[:],
                                                func=AF.Exp), "se")
                    block.scalar(ev5)
                    se_5 = c.counts["se"]

                    def ev6(vector, h=h, dst=dst, se_5=se_5):
                        c.wait(vector, "se", se_5)
                        c.inc(vector.tensor_add(out=dst[:, h * T:(h + 1) * T],
                                                in0=s1[:], in1=s2[:]), "ve")
                    block.vector(ev6)

        ve_qkv = c.counts["ve"]
        se_qkv = c.counts["se"]

        def scangate(tensor):
            tensor.wait_ge(c.sems["ve"], ve_qkv)
            tensor.wait_ge(c.sems["se"], se_qkv)
            tensor.wait_ge(c.sems["lds"], ld_c)
        block.tensor(scangate)

        for h in range(KC):
            for ci in range(NCH):
                c0 = ci * CH
                ve_a = c.counts["ve"]
                se_a = c.counts["se"]

                def trs(tensor, h=h, c0=c0, ve_a=ve_a, se_a=se_a):
                    tensor.wait_ge(c.sems["ve"], ve_a)
                    tensor.wait_ge(c.sems["se"], se_a)
                    for lb in range(2):
                        c.inc(tensor.transpose(
                            ptr[:, lb * 128:(lb + 1) * 128],
                            ksb[:, h * T + c0 + lb * 128:h * T + c0 + (lb + 1) * 128],
                            sident[:]), "te")
                        c.inc(tensor.transpose(
                            ptr[:, 256 + lb * 128:256 + (lb + 1) * 128],
                            vsb[:, h * T + c0 + lb * 128:h * T + c0 + (lb + 1) * 128],
                            sident[:]), "te")
                block.tensor(trs)
                te_tr = c.counts["te"]

                def evtr(vector, te_tr=te_tr):
                    vector.wait_ge(c.sems["te"], te_tr)
                    c.inc(vector.tensor_copy(out=ktm[:], in_=ptr[:, 0:256]),
                          "ve")
                    c.inc(vector.tensor_copy(out=vtm[:], in_=ptr[:, 256:512]),
                          "ve")
                block.vector(evtr)
                ve_tr = c.counts["ve"]
                se_tr = c.counts["se"]

                def mmatt(tensor, h=h, c0=c0, ve_tr=ve_tr, se_tr=se_tr):
                    tensor.wait_ge(c.sems["ve"], ve_tr)
                    for mb in range(2):
                        c.inc(tensor.matmul(
                            patt[:, mb * CH:(mb + 1) * CH],
                            ksb[:, h * T + c0 + mb * 128:h * T + c0 + (mb + 1) * 128],
                            qsb[:, h * T + c0:h * T + c0 + CH],
                            start=True, stop=True), "te")
                block.tensor(mmatt)
                te_att = c.counts["te"]

                def evatt(vector, te_att=te_att):
                    vector.wait_ge(c.sems["te"], te_att)
                    for mb in range(2):
                        c.inc(vector.tensor_mul(
                            out=attm[:, mb * CH:(mb + 1) * CH],
                            in0=patt[:, mb * CH:(mb + 1) * CH],
                            in1=smask[:, mb * CH:(mb + 1) * CH]), "ve")
                    c.inc(vector.reduce_sum(out=ztmp[:, 0:1],
                                            in_=ksb[:, h * T + c0:h * T + c0 + CH],
                                            axis=mybir.AxisListType.X), "ve")
                block.vector(evatt)
                ve_att = c.counts["ve"]

                po_bf = c.counts["po"]

                def mmnum(tensor, h=h, c0=c0, ve_att=ve_att, po_bf=po_bf):
                    tensor.wait_ge(c.sems["ve"], ve_att)
                    tensor.wait_ge(c.sems["po"], po_bf)
                    for mb in range(2):
                        c.inc(tensor.matmul(
                            pnum, vtm[:, mb * 128:(mb + 1) * 128],
                            attm[:, mb * CH:(mb + 1) * CH],
                            start=(mb == 0), stop=False), "te")
                    c.inc(tensor.matmul(
                        pnum, sbf[:, h * 128:(h + 1) * 128],
                        qsb[:, h * T + c0:h * T + c0 + CH],
                        start=False, stop=True), "te")
                    for mb in range(2):
                        c.inc(tensor.matmul(
                            pden, sones[:, 0:1],
                            attm[:, mb * CH:(mb + 1) * CH],
                            start=(mb == 0), stop=False), "te")
                    c.inc(tensor.matmul(
                        pden, zbf[:, h:h + 1],
                        qsb[:, h * T + c0:h * T + c0 + CH],
                        start=False, stop=True), "te")
                    for lb in range(2):
                        c.inc(tensor.matmul(
                            pS, ktm[:, lb * 128:(lb + 1) * 128],
                            vtm[:, lb * 128:(lb + 1) * 128],
                            start=(lb == 0), stop=(lb == 1)), "te")
                block.tensor(mmnum)
                te_n = c.counts["te"]

                st_cap = c.counts["st"]

                def evn(vector, h=h, ci=ci, te_n=te_n, st_cap=st_cap):
                    vector.wait_ge(c.sems["te"], te_n)
                    vector.wait_ge(c.sems["st"], st_cap)
                    c.inc(vector.tensor_copy(
                        out=numscr[:, (ci % 2) * CH:(ci % 2 + 1) * CH],
                        in_=pnum), "ve")
                    c.inc(vector.tensor_add(out=sloc[:, h * 128:(h + 1) * 128],
                                            in0=pS,
                                            in1=sloc[:, h * 128:(h + 1) * 128]),
                          "ve")
                    c.inc(vector.tensor_add(out=zloc[:, h:h + 1],
                                            in0=ztmp[:, 0:1],
                                            in1=zloc[:, h:h + 1]), "ve")
                    c.inc(vector.tensor_copy(
                        out=denscr[:, (ci % 2) * CH:(ci % 2 + 1) * CH],
                        in_=pden), "ve")
                block.vector(evn)
                ve_n = c.counts["ve"]

                def evbf(gp, h=h, ve_n=ve_n):
                    c.wait(gp, "ve", ve_n)
                    c.inc(gp.tensor_copy(out=sbf[:, h * 128:(h + 1) * 128],
                                         in_=sloc[:, h * 128:(h + 1) * 128]),
                          "po")
                    c.inc(gp.tensor_copy(out=zbf[:, h:h + 1],
                                         in_=zloc[:, h:h + 1]), "po")
                block.gpsimd(evbf)

                def stn(s, h=h, ci=ci, c0=c0, ve_n=ve_n):
                    s.wait_ge(c.sems["ve"], ve_n)
                    c.inc(s.dma_start(
                        out=num_out[h * 128:(h + 1) * 128, c0:c0 + CH],
                        in_=numscr[:, (ci % 2) * CH:(ci % 2 + 1) * CH]),
                        "st", 16)
                    c.inc(s.dma_start(
                        out=den_out[0:1, h * T + c0:h * T + c0 + CH],
                        in_=denscr[:, (ci % 2) * CH:(ci % 2 + 1) * CH]),
                        "st", 16)
                block.sync(stn)

        if "mcol" in io:
            smcol = e(nc.sbuf_tensor("smcol", [128, 1], F32))

            def scale_sz(vector):
                c.wait(vector, "lds")
                c.wait(vector, "se")
                c.inc(vector.tensor_scalar_mul(out=sloc[:, :], in0=sloc[:, :],
                                               scalar1=smcol[:, 0:1]), "ve")
                c.inc(vector.tensor_scalar_mul(out=zloc[:, :], in0=zloc[:, :],
                                               scalar1=smcol[:, 0:1]), "ve")

            def ld_m(s):
                c.inc(s.dma_start(out=smcol[:], in_=io["mcol"]), "lds", 16)
            block.sync(ld_m)
            block.vector(scale_sz)
        ve_end = c.counts["ve"]
        se_end = c.counts["se"]

        def stores(s):
            s.wait_ge(c.sems["ve"], ve_end)
            s.wait_ge(c.sems["se"], se_end)
            for h in range(KC):
                c.inc(s.dma_start(out=q_out[h * 128:(h + 1) * 128, :],
                                  in_=qsb[:, h * T:(h + 1) * T]), "st", 16)
            c.inc(s.dma_start(out=s_out, in_=sloc[:, :]), "st", 16)
            c.inc(s.dma_start(out=z_out, in_=zloc[:, :]), "st", 16)
        block.sync(stores)


build_l1a = _wrap(emit_l1a, [
    ("x2", [D, T], F32, "ExternalInput"),
    ("wqkvt", [D, 3 * D], F32, "ExternalInput"),
    ("cosf", [128, T], F32, "ExternalInput"),
    ("sinf", [128, T], F32, "ExternalInput"),
    ("rtc", [128, 128], BF16, "ExternalInput"),
    ("maskt", [128, 2 * CH], BF16, "ExternalInput"),
    ("allones", [128, 128], BF16, "ExternalInput"),
    ("ident", [128, 128], BF16, "ExternalInput"),
    ("q_out", [D, T], BF16, "ExternalOutput"),
    ("num_out", [D, T], F32, "ExternalOutput"),
    ("den_out", [1, KC * T], F32, "ExternalOutput"),
    ("s_out", [128, KC * 128], F32, "ExternalOutput"),
    ("z_out", [128, KC], F32, "ExternalOutput"),
])


def _rope_tables(t0):
    pos = np.arange(t0, t0 + T, dtype=np.float64)
    inv = 1.0 / (10000.0 ** (np.arange(0, 128, 2) / 128.0))
    fr = np.outer(pos, inv)
    emb = np.concatenate([fr, fr], axis=-1)
    return (np.ascontiguousarray(np.cos(emb).T.astype(np.float32)),
            np.ascontiguousarray(np.sin(emb).T.astype(np.float32)))


def _rt_const():
    rt = np.zeros((128, 128), np.float32)
    for i in range(64):
        rt[2 * i + 1, 2 * i] = -1.0
        rt[2 * i, 2 * i + 1] = 1.0
    return rt.astype(BF)


def _maskt_const():
    mk = np.zeros((128, 2 * CH), np.float32)
    for mb in range(2):
        for m_ in range(128):
            gm = mb * 128 + m_
            mk[m_, mb * CH + gm:(mb + 1) * CH] = 1.0
    return mk.astype(BF)


def run_l1a(x2_sh, W_qkv):
    nc = build_l1a()
    shared = {
        "wqkvt": np.ascontiguousarray(W_qkv.T),
        "rtc": _rt_const(), "maskt": _maskt_const(),
        "allones": np.ones((128, 128), BF), "ident": _IDENT_BF,
    }
    in_maps = []
    for cix in range(NCORES):
        t0 = (cix % 2) * T
        cosf, sinf = _rope_tables(t0)
        in_maps.append(dict(shared, x2=x2_sh[cix], cosf=cosf, sinf=sinf))
    res = run_bass_kernel_spmd(nc, in_maps, core_ids=list(range(NCORES)))
    return res.results



def emit_l1b(c, io, fcol=False, cc_wait=None):
    nc = c.nc
    block = c.block
    num, den, q = io["num"], io["den"], io["q"]
    scorr, zcorr = io["scorr"], io["zcorr"]
    woutt, x2 = io["woutt"], io["x2"]
    allones, onesrow_f = io["allones"], io["onesrow_f"]
    x3_out = io["x3_out"]

    N = 512
    from contextlib import ExitStack
    with ExitStack() as ctx:
        e = ctx.enter_context
        qsb = e(nc.sbuf_tensor("qsb", [128, KC * T], BF16))
        aout = e(nc.sbuf_tensor("aout", [128, KC * T], BF16))
        scb = e(nc.sbuf_tensor("scb", [128, KC * 128], BF16))
        zcb = e(nc.sbuf_tensor("zcb", [128, KC], BF16))
        numsb = e(nc.sbuf_tensor("numsb", [128, 2 * CH], F32))
        nf = e(nc.sbuf_tensor("nf", [128, CH], F32))
        densb = e(nc.sbuf_tensor("densb", [1, T], F32))
        drow = e(nc.sbuf_tensor("drow", [1, 2 * CH], F32))
        rb = e(nc.sbuf_tensor("rb", [128, CH], F32))
        wsb = e(nc.sbuf_tensor("wsb", [128, KC * 1024], BF16))
        xh = e(nc.sbuf_tensor("xh", [128, KC * N], F32))
        ressb = e(nc.sbuf_tensor("ressb", [128, KC * N], F32))
        sones = e(nc.sbuf_tensor("sones", [128, 128], BF16))
        sonesrow_f = e(nc.sbuf_tensor("sonesrow_f", [1, 128], F32))
        pcor = e(nc.psum_tensor("pcor", [128, T], F32))
        po = e(nc.psum_tensor("po", [128, 2 * N], F32))
        scr = make_ln_scratch(ctx, nc, N, psum_from=(pcor, po))
        scr["onescol"] = sones[:, 0:1]
        scr["onesrow"] = sonesrow_f[:]
        pc_ = pcor[:, 0:CH]
        pd_ = pcor[0:1, CH:2 * CH]
        pb_ = pcor[:, 512:512 + CH]

        def loads_g(g):
            if cc_wait:
                for s_ in cc_wait:
                    g.wait_ge(c.sems[s_], 1)
            c.inc(g.dma_start(out=scb[:, :], in_=scorr), "ldg", 16)
            c.inc(g.dma_start(out=zcb[:, :], in_=zcorr), "ldg", 16)
        block.gpsimd(loads_g)
        if fcol:
            sfcol = e(nc.sbuf_tensor("sfcol", [128, 1], F32))

            def ld_f(s):
                c.inc(s.dma_start(out=sfcol[:], in_=io["fcol"]), "lds", 16)
            block.sync(ld_f)

            def scale_f(vector):
                c.wait(vector, "ldg")
                c.wait(vector, "lds")
                c.inc(vector.tensor_scalar_mul(out=scb[:, :], in0=scb[:, :],
                                               scalar1=sfcol[:, 0:1]), "ve")
                c.inc(vector.tensor_scalar_mul(out=zcb[:, :], in0=zcb[:, :],
                                               scalar1=sfcol[:, 0:1]), "ve")
            block.vector(scale_f)

        def loads_s(s):
            for k in range(KC):
                c.inc(s.dma_start(out=qsb[:, k * T:(k + 1) * T],
                                  in_=q[k * 128:(k + 1) * 128, :]), "lds", 16)
            c.inc(s.dma_start(out=sones[:], in_=allones), "lds", 16)
            c.inc(s.dma_start(out=sonesrow_f[:], in_=onesrow_f), "lds", 16)
        block.sync(loads_s)
        ld_q = c.counts["lds"]

        for h in range(KC):
            ve_h = c.counts["ve"]

            def ldh(s, h=h, ve_h=ve_h):
                s.wait_ge(c.sems["ve"], ve_h)
                c.inc(s.dma_start(out=densb[:],
                                  in_=den[0:1, h * T:(h + 1) * T]), "lds", 16)
            block.sync(ldh)

            for ci in range(NCH):
                c0 = ci * CH
                ve_i = c.counts["ve"]

                def ldn(s, h=h, ci=ci, c0=c0, ve_i=ve_i):
                    s.wait_ge(c.sems["ve"], ve_i)
                    c.inc(s.dma_start(
                        out=numsb[:, (ci % 2) * CH:(ci % 2 + 1) * CH],
                        in_=num[h * 128:(h + 1) * 128, c0:c0 + CH]), "lds", 16)
                block.sync(ldn)
                ld_n = c.counts["lds"]
                ve_war = c.counts["ve"]
                se_war = c.counts["se"]

                def mmc(tensor, h=h, c0=c0, ld_n=ld_n, ve_war=ve_war,
                        se_war=se_war):
                    if h == 0 and c0 == 0:
                        tensor.wait_ge(c.sems["ldg"], 32)
                        tensor.wait_ge(c.sems["lds"], ld_q)
                    tensor.wait_ge(c.sems["ve"], ve_war)
                    tensor.wait_ge(c.sems["se"], se_war)
                    c.inc(tensor.matmul(pc_, scb[:, h * 128:(h + 1) * 128],
                                        qsb[:, h * T + c0:h * T + c0 + CH],
                                        start=True, stop=True), "te")
                    c.inc(tensor.matmul(pd_, zcb[:, h:h + 1],
                                        qsb[:, h * T + c0:h * T + c0 + CH],
                                        start=True, stop=True), "te")
                block.tensor(mmc)
                te_c = c.counts["te"]

                def evc(vector, h=h, ci=ci, c0=c0, te_c=te_c, ld_n=ld_n):
                    vector.wait_ge(c.sems["te"], te_c)
                    vector.wait_ge(c.sems["lds"], ld_n)
                    c.inc(vector.tensor_add(
                        out=nf[:], in0=pc_,
                        in1=numsb[:, (ci % 2) * CH:(ci % 2 + 1) * CH]), "ve")
                    c.inc(vector.tensor_add(out=drow[0:1, 0:CH], in0=pd_,
                                            in1=densb[0:1, c0:c0 + CH]), "ve")
                    c.inc(vector.tensor_scalar_max(out=drow[0:1, CH:2 * CH],
                                                   in0=drow[0:1, 0:CH],
                                                   scalar1=1e-6), "ve")
                    c.inc(vector.reciprocal(
                        out=drow[0:1, 0:CH], in_=drow[0:1, CH:2 * CH]), "ve")
                block.vector(evc)
                ve_c = c.counts["ve"]

                def mmb(tensor, ve_c=ve_c):
                    c.wait(tensor, "ve", ve_c)
                    c.inc(tensor.matmul(pb_, sonesrow_f[:], drow[0:1, 0:CH],
                                        start=True, stop=True), "te")
                block.tensor(mmb)
                te_b = c.counts["te"]

                def evo(vector, h=h, c0=c0, te_b=te_b):
                    vector.wait_ge(c.sems["te"], te_b)
                    c.inc(vector.tensor_copy(out=rb[:], in_=pb_), "ve")
                    c.inc(vector.tensor_mul(
                        out=aout[:, h * T + c0:h * T + c0 + CH],
                        in0=nf[:], in1=rb[:]), "ve")
                block.vector(evo)

        for nh in range(2):
            n0 = nh * N
            te_prev = c.counts["te"]
            ve_prev = c.counts["ve"]

            def ldx(s, n0=n0, te_prev=te_prev, ve_prev=ve_prev):
                s.wait_ge(c.sems["te"], te_prev)
                s.wait_ge(c.sems["ve"], ve_prev)
                for k in range(KC):
                    c.inc(s.dma_start(out=xh[:, k * N:(k + 1) * N],
                                      in_=x2[k * 128:(k + 1) * 128, n0:n0 + N]),
                          "lds", 16)
            block.sync(ldx)
            ld_x = c.counts["lds"]

            for mh in range(2):
                te_prev2 = c.counts["te"]

                def ldw(g, mh=mh, te_prev2=te_prev2):
                    g.wait_ge(c.sems["te"], te_prev2)
                    for k in range(KC):
                        c.inc(g.dma_start(
                            out=wsb[:, k * 1024:(k + 1) * 1024],
                            in_=woutt[k * 128:(k + 1) * 128,
                                      mh * 1024:(mh + 1) * 1024]), "ldg", 16)
                block.gpsimd(ldw)
                ld_w = c.counts["ldg"]

                for mm_ in range(8):
                    m = mh * 8 + mm_
                    ve0 = c.counts["ve"]

                    def mmo(tensor, m=m, mm_=mm_, n0=n0, ld_w=ld_w, ve0=ve0,
                            ld_x=ld_x):
                        tensor.wait_ge(c.sems["ldg"], ld_w)
                        tensor.wait_ge(c.sems["ve"], ve0)
                        if mm_ == 0:
                            tensor.wait_ge(c.sems["lds"], ld_x)
                        ins = mm_tiles(c, tensor,
                                       po[:, (mm_ % 2) * N:(mm_ % 2 + 1) * N],
                                       lambda k: wsb[:, k * 1024:(k + 1) * 1024],
                                       lambda k: aout[:, k * T + n0:k * T + n0 + N],
                                       mm_, 0, N)
                        c.inc(ins, "te")
                    block.tensor(mmo)
                    te_o = c.counts["te"]

                    def evo2(vector, m=m, mm_=mm_, te_o=te_o):
                        c.wait(vector, "te", te_o)
                        c.inc(vector.tensor_add(
                            out=ressb[:, m * N:(m + 1) * N],
                            in0=po[:, (mm_ % 2) * N:(mm_ % 2 + 1) * N],
                            in1=xh[:, m * N:(m + 1) * N]), "ve")
                    block.vector(evo2)

            emit_ln(c, lambda k: ressb[:, k * N:(k + 1) * N],
                    lambda k: ressb[:, k * N:(k + 1) * N], N, scr,
                    src_wait={"ve": c.counts["ve"]})
            ve_pl = c.counts["ve"]

            def store(s, n0=n0, ve_pl=ve_pl):
                s.wait_ge(c.sems["ve"], ve_pl)
                for k in range(KC):
                    c.inc(s.dma_start(out=x3_out[k * 128:(k + 1) * 128,
                                                 n0:n0 + N],
                                      in_=ressb[:, k * N:(k + 1) * N]), "st", 16)
            block.sync(store)


build_l1b = _wrap(emit_l1b, [
    ("num", [D, T], F32, "ExternalInput"),
    ("den", [1, KC * T], F32, "ExternalInput"),
    ("q", [D, T], BF16, "ExternalInput"),
    ("scorr", [128, KC * 128], F32, "ExternalInput"),
    ("zcorr", [128, KC], F32, "ExternalInput"),
    ("woutt", [D, D], F32, "ExternalInput"),
    ("x2", [D, T], F32, "ExternalInput"),
    ("allones", [128, 128], BF16, "ExternalInput"),
    ("onesrow_f", [1, 128], F32, "ExternalInput"),
    ("x3_out", [D, T], F32, "ExternalOutput"),
])


def run_l1b(num_sh, den_sh, q_sh, s_sh, z_sh, x2_sh, W_out):
    nc = build_l1b()
    zeroS = np.zeros((128, KC * 128), np.float32)
    zeroZ = np.zeros((128, KC), np.float32)
    shared = {
        "woutt": np.ascontiguousarray(W_out.T),
        "allones": np.ones((128, 128), BF),
        "onesrow_f": np.ones((1, 128), np.float32),
    }
    in_maps = []
    for cix in range(NCORES):
        odd = cix % 2 == 1
        in_maps.append(dict(
            shared, num=num_sh[cix], den=den_sh[cix], q=q_sh[cix],
            scorr=s_sh[cix - 1] if odd else zeroS,
            zcorr=z_sh[cix - 1] if odd else zeroZ, x2=x2_sh[cix]))
    res = run_bass_kernel_spmd(nc, in_maps, core_ids=list(range(NCORES)))
    return res.results



GI = 5504
GIC = GI // 128


def emit_l1g_cols(c, io):
    nc = c.nc
    block = c.block
    gidx, uidx, didx = io["gidx"], io["uidx"], io["didx"]
    lut, onesrow_f = io["lut"], io["onesrow_f"]
    wg_sh, wu_sh, wd_sh = io["wg_sh"], io["wu_sh"], io["wd_sh"]

    W1, W2 = GI // 8, D // 8
    from contextlib import ExitStack
    with ExitStack() as ctx:
        e = ctx.enter_context
        sidxf = e(nc.sbuf_tensor("sidxf", [128, KC * W1], F32))
        sidx = e(nc.sbuf_tensor("sidx", [128, KC * W1], I32))
        wacc = e(nc.sbuf_tensor("wacc", [128, KC * W1], BF16))
        mscr = e(nc.sbuf_tensor("mscr", [128, 2 * W1], F32))
        slutf = e(nc.sbuf_tensor("slutf", [1, 16], F32))
        lutcol = e(nc.sbuf_tensor("lutcol", [128, 16], F32))
        sonesrow_f = e(nc.sbuf_tensor("sonesrow_f", [1, 128], F32))
        pl = e(nc.psum_tensor("pl", [128, 16], F32))

        def loads0(s):
            c.inc(s.dma_start(out=slutf[:], in_=lut), "lds", 16)
            c.inc(s.dma_start(out=sonesrow_f[:], in_=onesrow_f), "lds", 16)
        block.sync(loads0)

        def bclut(tensor):
            tensor.wait_ge(c.sems["lds"], 32)
            c.inc(tensor.matmul(pl[:, :], sonesrow_f[:], slutf[:],
                                start=True, stop=True), "te")
        block.tensor(bclut)

        te_lut = c.counts["te"]

        def evlut(vector):
            c.wait(vector, "te", te_lut)
            c.inc(vector.tensor_copy(out=lutcol[:, :], in_=pl[:, :]), "ve")
        block.vector(evlut)

        for mat, (src, dst, W, nkc) in enumerate(
                [(gidx, wg_sh, W1, KC), (uidx, wu_sh, W1, KC),
                 (didx, wd_sh, W2, GIC)]):
            ve_prev = c.counts["ve"]
            st_prev = c.counts["st"]

            def ldi(s, src=src, W=W, nkc=nkc, ve_prev=ve_prev, st_prev=st_prev):
                s.wait_ge(c.sems["ve"], ve_prev)
                s.wait_ge(c.sems["st"], st_prev)
                for k in range(nkc):
                    c.inc(s.dma_start(out=sidx[:, k * W:(k + 1) * W],
                                      in_=src[k * 128:(k + 1) * 128, :]),
                          "lds", 16)
            block.sync(ldi)
            ld_i = c.counts["lds"]

            def conv(vector, W=W, nkc=nkc, ld_i=ld_i, mat=mat):
                vector.wait_ge(c.sems["lds"], ld_i)
                if mat == 0:
                    c.wait(vector, "se")
                for k in range(nkc):
                    c.inc(vector.tensor_copy(out=sidxf[:, k * W:(k + 1) * W],
                                             in_=sidx[:, k * W:(k + 1) * W]),
                          "ve")
                vector.memset(wacc[:, 0:nkc * W], 0.0)
                for j in range(16):
                    for k in range(nkc):
                        c.inc(vector.tensor_scalar(
                            out=mscr[:, (k % 2) * W1:(k % 2) * W1 + W],
                            in0=sidxf[:, k * W:(k + 1) * W],
                            scalar1=float(j), scalar2=None,
                            op0=ALU.is_equal), "ve")
                        c.inc(vector.scalar_tensor_tensor(
                            out=wacc[:, k * W:(k + 1) * W],
                            in0=mscr[:, (k % 2) * W1:(k % 2) * W1 + W],
                            scalar=lutcol[:, j:j + 1],
                            in1=wacc[:, k * W:(k + 1) * W],
                            op0=ALU.mult, op1=ALU.add), "ve")
            block.vector(conv)
            ve_b = c.counts["ve"]

            def stw(s, dst=dst, W=W, nkc=nkc, ve_b=ve_b):
                s.wait_ge(c.sems["ve"], ve_b)
                for k in range(nkc):
                    c.inc(s.dma_start(out=dst[k * 128:(k + 1) * 128, :],
                                      in_=wacc[:, k * W:(k + 1) * W]), "st", 16)
            block.sync(stw)


build_l1g = _wrap(emit_l1g_cols, [
    ("gidx", [D, GI // 8], I32, "ExternalInput"),
    ("uidx", [D, GI // 8], I32, "ExternalInput"),
    ("didx", [GI, D // 8], I32, "ExternalInput"),
    ("lut", [1, 16], F32, "ExternalInput"),
    ("onesrow_f", [1, 128], F32, "ExternalInput"),
    ("wg_sh", [D, GI // 8], BF16, "ExternalOutput"),
    ("wu_sh", [D, GI // 8], BF16, "ExternalOutput"),
    ("wd_sh", [GI, D // 8], BF16, "ExternalOutput"),
])


def run_l1g(gate_idx, up_idx, down_idx, lut):
    nc = build_l1g()
    gT = np.ascontiguousarray(gate_idx.T)
    uT = np.ascontiguousarray(up_idx.T)
    dT = np.ascontiguousarray(down_idx.T)
    shared = {"lut": lut.reshape(1, 16).astype(np.float32),
              "onesrow_f": np.ones((1, 128), np.float32)}
    in_maps = []
    for cix in range(NCORES):
        in_maps.append(dict(
            shared,
            gidx=np.ascontiguousarray(gT[:, cix * 688:(cix + 1) * 688]),
            uidx=np.ascontiguousarray(uT[:, cix * 688:(cix + 1) * 688]),
            didx=np.ascontiguousarray(dT[:, cix * 256:(cix + 1) * 256])))
    res = run_bass_kernel_spmd(nc, in_maps, core_ids=list(range(NCORES)))
    wgt = np.concatenate([r["wg_sh"] for r in res.results], axis=1)
    wut = np.concatenate([r["wu_sh"] for r in res.results], axis=1)
    wdt = np.concatenate([r["wd_sh"] for r in res.results], axis=1)
    return wgt, wut, wdt



G_GROUPS = [(0, 11), (11, 11), (22, 11), (33, 10)]


def emit_l1f(c, io, cc_wait=None):
    nc = c.nc
    block = c.block
    x3, wgt, wut, wdt = io["x3"], io["wgt"], io["wut"], io["wdt"]
    allones, onesrow_f = io["allones"], io["onesrow_f"]
    x4_out = io["x4_out"]

    N = 512
    from contextlib import ExitStack
    with ExitStack() as ctx:
        e = ctx.enter_context
        xbf = e(nc.sbuf_tensor("xbf", [128, KC * T], BF16))
        yacc = e(nc.sbuf_tensor("yacc", [128, KC * T], F32))
        wbuf = e(nc.sbuf_tensor("wbuf", [128, KC * INTER], BF16))
        hsc = e(nc.sbuf_tensor("hsc", [128, IC * T], BF16))
        sones = e(nc.sbuf_tensor("sones", [128, 128], BF16))
        sonesrow_f = e(nc.sbuf_tensor("sonesrow_f", [1, 128], F32))
        ph = e(nc.psum_tensor("ph", [128, T], F32))
        py = e(nc.psum_tensor("py", [128, T], F32))
        scr = make_ln_scratch(ctx, nc, N, psum_from=(ph, py))
        scr["onescol"] = sones[:, 0:1]
        scr["onesrow"] = sonesrow_f[:]

        def loads_s(s):
            for k in range(KC):
                c.inc(s.dma_start(out=yacc[:, k * T:(k + 1) * T],
                                  in_=x3[k * 128:(k + 1) * 128, :]), "lds", 16)
            c.inc(s.dma_start(out=sones[:], in_=allones), "lds", 16)
            c.inc(s.dma_start(out=sonesrow_f[:], in_=onesrow_f), "lds", 16)
        block.sync(loads_s)

        def loads_g(g):
            for k in range(KC):
                c.inc(g.dma_start(out=xbf[:, k * T:(k + 1) * T],
                                  in_=x3[k * 128:(k + 1) * 128, :]), "ldg", 16)
        block.gpsimd(loads_g)

        for gs, gsz in G_GROUPS:
            gw = gsz * 128
            te_prevC = c.counts["te"]

            def lwg(gp, gs=gs, gw=gw, te_prevC=te_prevC):
                if cc_wait and gs == 0:
                    for s_ in cc_wait:
                        gp.wait_ge(c.sems[s_], 1)
                gp.wait_ge(c.sems["te"], te_prevC)
                for k in range(KC):
                    c.inc(gp.dma_start(
                        out=wbuf[:, k * INTER:k * INTER + gw],
                        in_=wgt[k * 128:(k + 1) * 128,
                                gs * 128:gs * 128 + gw]), "ldg", 16)
            block.gpsimd(lwg)
            ld_w1 = c.counts["ldg"]

            for i in range(gsz):
                se_pa = c.counts["se"]

                def mma(tensor, i=i, ld_w1=ld_w1, se_pa=se_pa):
                    tensor.wait_ge(c.sems["ldg"], ld_w1)
                    tensor.wait_ge(c.sems["se"], se_pa)
                    ins = mm_tiles2(c, tensor, ph,
                                    lambda k: wbuf[:, k * INTER:(k + 1) * INTER],
                                    lambda k: xbf[:, k * T:(k + 1) * T],
                                    i, [(0, 0, N), (N, N, N)])
                    c.inc(ins, "te")
                block.tensor(mma)
                te_a = c.counts["te"]

                def eva(scalar, i=i, te_a=te_a):
                    scalar.wait_ge(c.sems["te"], te_a)
                    c.inc(scalar.activation(out=hsc[:, i * T:(i + 1) * T],
                                            in_=ph[:], func=AF.Silu), "se")
                block.scalar(eva)

            te_prevA = c.counts["te"]

            def lwu(gp, gs=gs, gw=gw, te_prevA=te_prevA):
                gp.wait_ge(c.sems["te"], te_prevA)
                for k in range(KC):
                    c.inc(gp.dma_start(
                        out=wbuf[:, k * INTER:k * INTER + gw],
                        in_=wut[k * 128:(k + 1) * 128,
                                gs * 128:gs * 128 + gw]), "ldg", 16)
            block.gpsimd(lwu)
            ld_w3 = c.counts["ldg"]
            se_afterA = c.counts["se"]

            for i in range(gsz):
                ve_pb = c.counts["ve"]

                def mmb(tensor, i=i, ld_w3=ld_w3, ve_pb=ve_pb,
                        se_afterA=se_afterA):
                    tensor.wait_ge(c.sems["ldg"], ld_w3)
                    tensor.wait_ge(c.sems["ve"], ve_pb)
                    if i == 0:
                        tensor.wait_ge(c.sems["se"], se_afterA)
                    ins = mm_tiles2(c, tensor, ph,
                                    lambda k: wbuf[:, k * INTER:(k + 1) * INTER],
                                    lambda k: xbf[:, k * T:(k + 1) * T],
                                    i, [(0, 0, N), (N, N, N)])
                    c.inc(ins, "te")
                block.tensor(mmb)
                te_b = c.counts["te"]

                def evb(vector, i=i, te_b=te_b):
                    vector.wait_ge(c.sems["te"], te_b)
                    c.inc(vector.tensor_mul(out=hsc[:, i * T:(i + 1) * T],
                                            in0=ph[:],
                                            in1=hsc[:, i * T:(i + 1) * T]),
                          "ve")
                block.vector(evb)

            te_prevB = c.counts["te"]
            ve_hsc = c.counts["ve"]

            def lwd(gp, gs=gs, gsz=gsz, te_prevB=te_prevB):
                gp.wait_ge(c.sems["te"], te_prevB)
                for i in range(gsz):
                    c.inc(gp.dma_start(
                        out=wbuf[:, i * D:(i + 1) * D],
                        in_=wdt[(gs + i) * 128:(gs + i + 1) * 128, :]),
                        "ldg", 16)
            block.gpsimd(lwd)
            ld_w2 = c.counts["ldg"]

            for m in range(KC):
                ve_pc = c.counts["ve"]

                def mmc(tensor, m=m, gsz=gsz, ld_w2=ld_w2, ve_pc=ve_pc,
                        ve_hsc=ve_hsc):
                    tensor.wait_ge(c.sems["ldg"], ld_w2)
                    tensor.wait_ge(c.sems["ve"], max(ve_pc, ve_hsc))
                    ins = mm_tiles2(c, tensor, py,
                                    lambda i: wbuf[:, i * D:(i + 1) * D],
                                    lambda i: hsc[:, i * T:(i + 1) * T],
                                    m, [(0, 0, N), (N, N, N)], kcs=range(gsz))
                    c.inc(ins, "te")
                block.tensor(mmc)
                te_c = c.counts["te"]

                def evc(vector, m=m, te_c=te_c):
                    vector.wait_ge(c.sems["te"], te_c)
                    if m == 0:
                        vector.wait_ge(c.sems["lds"], 16 * KC)
                    c.inc(vector.tensor_add(out=yacc[:, m * T:(m + 1) * T],
                                            in0=py[:],
                                            in1=yacc[:, m * T:(m + 1) * T]),
                          "ve")
                block.vector(evc)

        for nh in range(2):
            n0 = nh * N
            emit_ln(c, lambda k: yacc[:, k * T + n0:k * T + n0 + N],
                    lambda k: yacc[:, k * T + n0:k * T + n0 + N], N, scr,
                    src_wait={"ve": c.counts["ve"], "se": c.counts["se"]})
            ve_pl = c.counts["ve"]

            def store(s, n0=n0, ve_pl=ve_pl):
                s.wait_ge(c.sems["ve"], ve_pl)
                for k in range(KC):
                    c.inc(s.dma_start(out=x4_out[k * 128:(k + 1) * 128,
                                                 n0:n0 + N],
                                      in_=yacc[:, k * T + n0:k * T + n0 + N]),
                          "st", 16)
            block.sync(store)


build_l1f = _wrap(emit_l1f, [
    ("x3", [D, T], F32, "ExternalInput"),
    ("wgt", [D, GI], BF16, "ExternalInput"),
    ("wut", [D, GI], BF16, "ExternalInput"),
    ("wdt", [GI, D], BF16, "ExternalInput"),
    ("allones", [128, 128], BF16, "ExternalInput"),
    ("onesrow_f", [1, 128], F32, "ExternalInput"),
    ("x4_out", [D, T], F32, "ExternalOutput"),
])


def run_l1f(x3_sh, wgt, wut, wdt):
    nc = build_l1f()
    shared = {"wgt": wgt, "wut": wut, "wdt": wdt,
              "allones": np.ones((128, 128), BF),
              "onesrow_f": np.ones((1, 128), np.float32)}
    in_maps = [dict(shared, x3=x3_sh[cix]) for cix in range(NCORES)]
    res = run_bass_kernel_spmd(nc, in_maps, core_ids=list(range(NCORES)))
    return res.results



def emit_l1g_j(c, io):
    nc = c.nc
    block = c.block
    mats = [(io["gidxt"], io["wgp"], GI, KC), (io["uidxt"], io["wup"], GI, KC),
            (io["didxt"], io["wdp"], D, GIC)]
    from contextlib import ExitStack
    with ExitStack() as ctx:
        e = ctx.enter_context
        sidx = e(nc.sbuf_tensor("gsidx", [128, 2 * GI], I32))
        sidxf = e(nc.sbuf_tensor("gsidxf", [128, 2 * GI], F32))
        wch = e(nc.sbuf_tensor("gwch", [128, 2 * GI], BF16))
        msl = e(nc.sbuf_tensor("gmsl", [128, 2 * GI], F32))
        sjv = e(nc.sbuf_tensor("gsjv", [128, 2], F32))
        slut = e(nc.sbuf_tensor("gslut", [128, 2], F32))

        def loads0(s):
            c.inc(s.dma_start(out=sjv[:], in_=io["jvals"]), "lds", 16)
            c.inc(s.dma_start(out=slut[:], in_=io["lutsel"]), "lds", 16)
        block.sync(loads0)

        chunks = []
        for mi, (src, dst, W, nkc) in enumerate(mats):
            for k in range(nkc):
                chunks.append((mi, k))
        hist = []
        for ci, (mi, k) in enumerate(chunks):
            src, dst, W, nkc = mats[mi]
            sl = ci % 2
            ve_guard = hist[ci - 2][0] if ci >= 2 else 0
            st_guard = hist[ci - 2][1] if ci >= 2 else 0

            def ld(s, src=src, W=W, k=k, sl=sl, ve_guard=ve_guard):
                s.wait_ge(c.sems["ve"], ve_guard)
                c.inc(s.dma_start(out=sidx[:, sl * GI:sl * GI + W],
                                  in_=src[k * 128:(k + 1) * 128, :]), "lds", 16)
            block.sync(ld)
            ld_i = c.counts["lds"]

            def conv(gp, W=W, sl=sl, ld_i=ld_i, ve_guard=ve_guard):
                gp.wait_ge(c.sems["lds"], ld_i)
                gp.wait_ge(c.sems["ve"], ve_guard)
                c.inc(gp.tensor_copy(out=sidxf[:, sl * GI:sl * GI + W],
                                     in_=sidx[:, sl * GI:sl * GI + W]), "po")
            block.gpsimd(conv)
            se_i = c.counts["po"]

            def bld(vector, W=W, sl=sl, se_i=se_i, st_guard=st_guard, ci=ci):
                c.wait(vector, "po", se_i)
                vector.wait_ge(c.sems["st"], st_guard)
                if ci == 0:
                    c.wait(vector, "lds", 32)
                c.inc(vector.tensor_scalar(
                    out=wch[:, sl * GI:sl * GI + W],
                    in0=sidxf[:, sl * GI:sl * GI + W],
                    scalar1=sjv[:, 0:1], scalar2=slut[:, 0:1],
                    op0=ALU.is_equal, op1=ALU.mult), "ve")
                c.inc(vector.tensor_scalar(
                    out=msl[:, sl * GI:sl * GI + W],
                    in0=sidxf[:, sl * GI:sl * GI + W],
                    scalar1=sjv[:, 1:2], scalar2=None,
                    op0=ALU.is_equal), "ve")
                c.inc(vector.scalar_tensor_tensor(
                    out=wch[:, sl * GI:sl * GI + W],
                    in0=msl[:, sl * GI:sl * GI + W],
                    scalar=slut[:, 1:2],
                    in1=wch[:, sl * GI:sl * GI + W],
                    op0=ALU.mult, op1=ALU.add), "ve")
            block.vector(bld)
            ve_b = c.counts["ve"]

            def stw(g, dst=dst, W=W, k=k, sl=sl, ve_b=ve_b):
                g.wait_ge(c.sems["ve"], ve_b)
                c.inc(g.dma_start(out=dst[k * 128:(k + 1) * 128, :],
                                  in_=wch[:, sl * GI:sl * GI + W]), "st", 16)
            block.gpsimd(stw)
            hist.append((c.counts["ve"], c.counts["st"]))



def build_fused():
    from contextlib import ExitStack
    nc = bass.Bass("TRN2", target_bir_lowering=False)
    EI, EO = "ExternalInput", "Internal"
    io = {}

    def d(name, shape, dt, kind="ExternalInput"):
        io[name] = nc.dram_tensor(name, shape, dt, kind=kind).ap()
    d("x", [D, T], F32)
    d("wqk", [D, D], F32); d("wv", [D, D], F32); d("wu", [D, D], F32)
    d("bqk", [128, KC], F32); d("bv", [128, KC], F32); d("bu", [128, KC], F32)
    d("wo", [D, D], F32)
    d("lng", [128, KC], F32); d("lnb", [128, KC], F32); d("bo", [128, KC], F32)
    d("gwt", [D, 16], F32)
    d("w1t", [16, D, INTER], F32); d("w3t", [16, D, INTER], F32)
    d("w2t", [16, INTER, D], F32)
    d("sw1t", [D, 2 * INTER], F32); d("sw3t", [D, 2 * INTER], F32)
    d("sw2t", [2 * INTER, D], F32)
    d("ind", [16, D], BF16)
    d("wqkvt", [D, 3 * D], F32); d("woutt", [D, D], F32)
    d("cosf", [128, T], F32); d("sinf", [128, T], F32)
    d("rtc", [128, 128], BF16); d("maskt", [128, 2 * CH], BF16)
    d("gidxt", [D, GI], I32); d("uidxt", [D, GI], I32); d("didxt", [GI, D], I32)
    d("jvals", [128, 2], F32); d("lutsel", [128, 2], F32)
    d("mcol", [128, 1], F32); d("fcol", [128, 1], F32)
    d("ident", [128, 128], BF16); d("allones", [128, 128], BF16)
    d("onesrow_f", [1, 128], F32)
    d("x4_out", [D, T], F32, "ExternalOutput")
    d("q0", [D, T], BF16, EO); d("u0", [D, T], F32, EO)
    d("kvp", [128, KC * 128], F32, EO); d("kvr", [128, KC * 128], F32, EO)
    d("x1", [D, T], F32, EO); d("x2", [D, T], F32, EO); d("x3", [D, T], F32, EO)
    d("q1", [D, T], BF16, EO)
    d("num", [D, T], F32, EO); d("den", [1, KC * T], F32, EO)
    d("ssc", [128, KC * 128], F32, EO); d("zsc", [128, KC], F32, EO)
    d("sred", [128, KC * 128], F32, EO); d("zred", [128, KC], F32, EO)
    d("wgp", [D, GI], BF16, EO); d("wup", [D, GI], BF16, EO)
    d("wdp", [GI, D], BF16, EO)
    d("wgr", [D, GI], BF16, EO); d("wur", [D, GI], BF16, EO)
    d("wdr", [GI, D], BF16, EO)

    PAIRS = [[0, 1], [2, 3], [4, 5], [6, 7]]
    ALL8 = [list(range(8))]

    with ExitStack() as st0:
        c = mk_ctx(nc, st0)
        for s in ("cckv", "ccs", "ccz", "ccg", "ccu", "ccd"):
            c.ccsem(s)
        c.block = st0.enter_context(nc.Block())

        c.new_phase()
        emit_l0a(c, dict(io, q_out=io["q0"], u_out=io["u0"], kv_out=io["kvp"]))
        st_kv = c.counts["st"]

        def coll_kv(g):
            g.wait_ge(c.sems["st"], st_kv)
            c.inc(g.collective_compute(
                "AllReduce", ALU.add, PAIRS,
                ins=[io["kvp"].opt()], outs=[io["kvr"].opt()]), "cckv")
        c.block.gpsimd(coll_kv)
        emit_barrier(c)

        c.new_phase()
        emit_l0b(c, dict(io, q=io["q0"], u=io["u0"], x1_out=io["x1"]),
                 kv_pair=False, cc_wait="cckv")
        emit_barrier(c)

        c.new_phase()
        emit_l0f(c, dict(io, x2_out=io["x2"]))
        emit_barrier(c)

        c.new_phase()
        emit_l1g_j(c, io)
        st_g = c.counts["st"]

        def coll_g(g):
            g.wait_ge(c.sems["st"], st_g)
            c.inc(g.collective_compute(
                "AllReduce", ALU.add, ALL8,
                ins=[io["wgp"].opt()], outs=[io["wgr"].opt()]), "ccg")
            c.inc(g.collective_compute(
                "AllReduce", ALU.add, ALL8,
                ins=[io["wup"].opt()], outs=[io["wur"].opt()]), "ccu")
            c.inc(g.collective_compute(
                "AllReduce", ALU.add, ALL8,
                ins=[io["wdp"].opt()], outs=[io["wdr"].opt()]), "ccd")
        c.block.gpsimd(coll_g)
        emit_barrier(c)

        c.new_phase()
        emit_l1a(c, dict(io, q_out=io["q1"], num_out=io["num"],
                         den_out=io["den"], s_out=io["ssc"], z_out=io["zsc"]))
        st_sz = c.counts["st"]

        def coll_sz(g):
            g.wait_ge(c.sems["st"], st_sz)
            c.inc(g.collective_compute(
                "AllReduce", ALU.add, PAIRS,
                ins=[io["ssc"].opt()], outs=[io["sred"].opt()]), "ccs")
            c.inc(g.collective_compute(
                "AllReduce", ALU.add, PAIRS,
                ins=[io["zsc"].opt()], outs=[io["zred"].opt()]), "ccz")
        c.block.gpsimd(coll_sz)
        emit_barrier(c)

        c.new_phase()
        emit_l1b(c, dict(io, q=io["q1"], scorr=io["sred"], zcorr=io["zred"],
                         x2=io["x2"], x3_out=io["x3"]),
                 fcol=True, cc_wait=["ccs", "ccz"])
        emit_barrier(c)

        c.new_phase()
        emit_l1f(c, dict(io, wgt=io["wgr"], wut=io["wur"], wdt=io["wdr"]),
                 cc_wait=["ccg", "ccu", "ccd"])
        st_end = c.counts["st"]

        def fin(g):
            g.wait_ge(c.sems["st"], st_end)
        c.block.gpsimd(fin)
    return nc


def run_fused(ins):
    nc = build_fused()
    lut = np.asarray(ins["L1_ffn_lut"], dtype=np.float32)
    shared = {
        "wqk": np.ascontiguousarray(ins["L0_attn_Wqk"]),
        "wv": np.ascontiguousarray(ins["L0_attn_Wv"]),
        "wu": np.ascontiguousarray(ins["L0_attn_Wu"]),
        "bqk": _bias_cols(ins["L0_attn_bqk"]),
        "bv": _bias_cols(ins["L0_attn_bv"]),
        "bu": _bias_cols(ins["L0_attn_bu"]),
        "wo": np.ascontiguousarray(ins["L0_attn_Wo"]),
        "bo": _bias_cols(ins["L0_attn_bo"]),
        "lng": _bias_cols(ins["L0_attn_ln_g"]),
        "lnb": _bias_cols(ins["L0_attn_ln_b"]),
        "gwt": np.ascontiguousarray(ins["L0_ffn_gate_w"].T),
        "w1t": np.ascontiguousarray(ins["L0_ffn_w1"].transpose(0, 2, 1)),
        "w3t": np.ascontiguousarray(ins["L0_ffn_w3"].transpose(0, 2, 1)),
        "w2t": np.ascontiguousarray(ins["L0_ffn_w2"].transpose(0, 2, 1)),
        "sw1t": np.ascontiguousarray(ins["L0_ffn_sw1"].T),
        "sw3t": np.ascontiguousarray(ins["L0_ffn_sw3"].T),
        "sw2t": np.ascontiguousarray(ins["L0_ffn_sw2"].T),
        "ind": _make_ind(),
        "wqkvt": np.ascontiguousarray(ins["L1_attn_W_qkv"].T),
        "woutt": np.ascontiguousarray(ins["L1_attn_W_out"].T),
        "rtc": _rt_const(), "maskt": _maskt_const(),
        "gidxt": np.ascontiguousarray(ins["L1_ffn_gate_idx"].T),
        "uidxt": np.ascontiguousarray(ins["L1_ffn_up_idx"].T),
        "didxt": np.ascontiguousarray(ins["L1_ffn_down_idx"].T),
        "ident": _IDENT_BF,
        "allones": np.ones((128, 128), BF),
        "onesrow_f": np.ones((1, 128), np.float32),
    }
    x = np.ascontiguousarray(ins["L0_attn_x"], dtype=np.float32)
    shards = _shard_x(x)
    in_maps = []
    for cix in range(NCORES):
        t0 = (cix % 2) * T
        cosf, sinf = _rope_tables(t0)
        odd = float(cix % 2)
        jv = np.full((128, 2), 0.0, np.float32)
        jv[:, 0] = 2 * cix
        jv[:, 1] = 2 * cix + 1
        ls = np.ascontiguousarray(
            np.broadcast_to(lut[2 * cix:2 * cix + 2], (128, 2)))
        in_maps.append(dict(
            shared, x=shards[cix], cosf=cosf, sinf=sinf,
            jvals=jv, lutsel=ls,
            mcol=np.full((128, 1), 1.0 - odd, np.float32),
            fcol=np.full((128, 1), odd, np.float32)))
    res = run_bass_kernel_spmd(nc, in_maps, core_ids=list(range(NCORES)))
    out = np.empty((4, 2048, D), np.float32)
    for cix in range(NCORES):
        b, t0 = cix // 2, (cix % 2) * T
        out[b, t0:t0 + T, :] = res.results[cix]["x4_out"].T
    return out



_IDENT_BF = np.eye(128, dtype=BF)


def _shard_x(xfull):
    out = []
    for core in range(NCORES):
        b, t0 = core // 2, (core % 2) * T
        out.append(np.ascontiguousarray(xfull[b, t0:t0 + T, :].T))
    return out


def _bias_cols(b):
    return np.ascontiguousarray(b.reshape(KC, 128).T)


def run_l0a(x_shards, Wqk, bqk, Wv, bv, Wu, bu):
    nc = build_l0a()
    shared = {
        "wqk": np.ascontiguousarray(Wqk), "wv": np.ascontiguousarray(Wv),
        "wu": np.ascontiguousarray(Wu),
        "bqk": _bias_cols(bqk), "bv": _bias_cols(bv), "bu": _bias_cols(bu),
        "ident": _IDENT_BF,
    }
    in_maps = [dict(shared, x=x_shards[c]) for c in range(NCORES)]
    res = run_bass_kernel_spmd(nc, in_maps, core_ids=list(range(NCORES)))
    return res.results


def kernel(**inputs):
    return run_fused(inputs)


def kernel_multilaunch(**inputs):
    ins = inputs
    shards = _shard_x(np.ascontiguousarray(ins["L0_attn_x"], dtype=np.float32))

    r1 = run_l0a(shards, ins["L0_attn_Wqk"], ins["L0_attn_bqk"],
                 ins["L0_attn_Wv"], ins["L0_attn_bv"],
                 ins["L0_attn_Wu"], ins["L0_attn_bu"])
    r2 = run_l0b([r["q_out"] for r in r1], [r["kv_out"] for r in r1],
                 [r["u_out"] for r in r1], shards,
                 ins["L0_attn_Wo"], ins["L0_attn_bo"],
                 ins["L0_attn_ln_g"], ins["L0_attn_ln_b"])
    x1_sh = [r["x1_out"] for r in r2]
    r3 = run_l0f(x1_sh, ins["L0_ffn_gate_w"], ins["L0_ffn_w1"],
                 ins["L0_ffn_w2"], ins["L0_ffn_w3"], ins["L0_ffn_sw1"],
                 ins["L0_ffn_sw2"], ins["L0_ffn_sw3"])
    x2_sh = [r["x2_out"] for r in r3]
    ra = run_l1a(x2_sh, ins["L1_attn_W_qkv"])
    rb = run_l1b([r["num_out"] for r in ra], [r["den_out"] for r in ra],
                 [r["q_out"] for r in ra], [r["s_out"] for r in ra],
                 [r["z_out"] for r in ra], x2_sh, ins["L1_attn_W_out"])
    x3_sh = [r["x3_out"] for r in rb]
    wgt, wut, wdt = run_l1g(ins["L1_ffn_gate_idx"], ins["L1_ffn_up_idx"],
                            ins["L1_ffn_down_idx"], ins["L1_ffn_lut"])
    rf = run_l1f(x3_sh, wgt, wut, wdt)

    out = np.empty((4, 2048, D), np.float32)
    for cix in range(NCORES):
        b, t0 = cix // 2, (cix % 2) * T
        out[b, t0:t0 + T, :] = rf[cix]["x4_out"].T
    return out


# version 04
# speedup vs baseline: 1.1836x; 1.1755x over previous
import numpy as np
import ml_dtypes
import concourse.bass as bass
import concourse.mybir as mybir
from concourse.bass_utils import run_bass_kernel_spmd

F32 = mybir.dt.float32
BF16 = mybir.dt.bfloat16
I32 = mybir.dt.int32
AF = mybir.ActivationFunctionType
ALU = mybir.AluOpType
BF = ml_dtypes.bfloat16

D = 2048
T = 1024
KC = D // 128
NCORES = 8



class Ctx:

    def __init__(self, nc, block, stack=None):
        self.nc = nc
        self.block = block
        self.sems = {}
        self.counts = {}
        self._stack = stack

    def sem(self, name):
        cm = self.nc.semaphore(name)
        h = self._stack.enter_context(cm) if self._stack else cm.__enter__()
        self.sems[name] = h
        self.counts[name] = 0
        return h

    def inc(self, instr, sem, n=1):
        instr.then_inc(self.sems[sem], n)
        self.counts[sem] += n
        return self.counts[sem]

    def wait(self, eng, sem, val=None):
        eng.wait_ge(self.sems[sem], self.counts[sem] if val is None else val)

    def new_phase(self):
        if not hasattr(self, "nc0"):
            self.nc0 = self.nc
        self.phase = getattr(self, "phase", -1) + 1
        self.nc = _NcProxy(self.nc0, self.phase)
        for s in ("lds", "ldg", "st"):
            cm = self.nc0.semaphore(f"{s}{self.phase}")
            self.sems[s] = self._stack.enter_context(cm)
            self.counts[s] = 0

    def ccsem(self, name):
        h = self._stack.enter_context(self.nc.semaphore(name))
        self.sems[name] = h
        self.counts[name] = 0
        return h


class _NcProxy:
    def __init__(self, nc, tag):
        self._nc = nc
        self._tag = tag

    def sbuf_tensor(self, name, *a, **k):
        return self._nc.sbuf_tensor(f"{name}_p{self._tag}", *a, **k)

    def psum_tensor(self, name, *a, **k):
        return self._nc.psum_tensor(f"{name}_p{self._tag}", *a, **k)

    def __getattr__(self, x):
        return getattr(self._nc, x)


def mk_ctx(nc, stack):
    c = Ctx(nc, None, stack)
    for s in ("te", "ve", "se", "po", "bar"):
        c.sem(s)
    c.barn = 0
    return c


def emit_barrier(c):
    snap = {s: c.counts[s] for s in ("te", "ve", "se", "po", "lds", "ldg", "st")}
    c.barn += 1
    target = 5 * c.barn
    for name in ("tensor", "vector", "scalar", "gpsimd", "sync"):
        def body(eng, snap=snap, target=target):
            for s_, v_ in snap.items():
                if v_:
                    eng.wait_ge(c.sems[s_], v_)
            eng.sem_inc(c.sems["bar"], 1)
            eng.wait_ge(c.sems["bar"], target)
        getattr(c.block, name)(body)


def mm_tiles(ct, tensor, psum_ap, w_sb, x_sb, m, n0, nn, kcs=range(KC),
             wait=None):
    ks = list(kcs)
    for i, k in enumerate(ks):
        ins = tensor.matmul(psum_ap, w_sb(k)[:, m * 128:(m + 1) * 128],
                            x_sb(k)[:, n0:n0 + nn],
                            start=(i == 0), stop=(i == len(ks) - 1))
    return ins


def mm_tiles2(ct, tensor, psum, w_sb, x_sb, m, halves, kcs=range(KC)):
    ks = list(kcs)
    ins = None
    for i, k in enumerate(ks):
        w = w_sb(k)[:, m * 128:(m + 1) * 128]
        for off, n0, nn in halves:
            ins = tensor.matmul(psum[:, off:off + nn], w,
                                x_sb(k)[:, n0:n0 + nn],
                                start=(i == 0), stop=(i == len(ks) - 1))
    return ins



def emit_l0a(c, io):
    nc = c.nc
    block = c.block
    x, wqk, wv, wu = io["x"], io["wqk"], io["wv"], io["wu"]
    bqk, bv, bu, ident = io["bqk"], io["bv"], io["bu"], io["ident"]
    q_out, u_out, kv_out = io["q_out"], io["u_out"], io["kv_out"]

    from contextlib import ExitStack
    with ExitStack() as ctx:
        e = ctx.enter_context
        xbf = e(nc.sbuf_tensor("xbf", [128, KC * T], BF16))
        wsb = e(nc.sbuf_tensor("wsb", [128, KC * D], BF16))
        qbf = e(nc.sbuf_tensor("qbf", [128, KC * T], BF16))
        vbf = e(nc.sbuf_tensor("vbf", [128, KC * T], BF16))
        uscr = e(nc.sbuf_tensor("uscr", [128, 2 * T], F32))
        r1 = e(nc.sbuf_tensor("r1", [128, T], F32))
        r2 = e(nc.sbuf_tensor("r2", [128, T], F32))
        sbias = e(nc.sbuf_tensor("sbias", [128, 3 * KC], F32))
        sident = e(nc.sbuf_tensor("sident", [128, 128], BF16))
        ktm = e(nc.sbuf_tensor("ktm", [128, 128], BF16))
        vtm = e(nc.sbuf_tensor("vtm", [128, 128], BF16))
        kvsb = e(nc.sbuf_tensor("kvsb", [128, KC * 128], F32))
        pa = e(nc.psum_tensor("pa", [128, T], F32))
        pkv = e(nc.psum_tensor("pkv", [128, 4 * 128], F32))
        ptr = e(nc.psum_tensor("ptr", [128, 256], BF16))

        def loads(g):
            for k in range(KC):
                c.inc(g.dma_start(out=xbf[:, k * T:(k + 1) * T],
                                  in_=x[k * 128:(k + 1) * 128, :]), "ldg", 16)
            for k in range(KC):
                c.inc(g.dma_start(out=wsb[:, k * D:(k + 1) * D],
                                  in_=wqk[k * 128:(k + 1) * 128, :]), "ldg", 16)
        block.gpsimd(loads)
        ld_xw = c.counts["ldg"]

        def loads2(s):
            c.inc(s.dma_start(out=sbias[:, 0:KC], in_=bqk), "lds", 16)
            c.inc(s.dma_start(out=sbias[:, KC:2 * KC], in_=bv), "lds", 16)
            c.inc(s.dma_start(out=sbias[:, 2 * KC:3 * KC], in_=bu), "lds", 16)
            c.inc(s.dma_start(out=sident[:], in_=ident), "lds", 16)
        block.sync(loads2)

        for proj in range(3):
            wname = [wqk, wv, wu][proj]
            if proj > 0:
                te_before = c.counts["te"]

                def reload(g, wname=wname, need=te_before):
                    g.wait_ge(c.sems["te"], need)
                    for k in range(KC):
                        c.inc(g.dma_start(out=wsb[:, k * D:(k + 1) * D],
                                          in_=wname[k * 128:(k + 1) * 128, :]),
                              "ldg", 16)
                block.gpsimd(reload)
            ld_now = c.counts["ldg"]

            for m in range(KC):
                ve_now = c.counts["ve"]
                se_now = c.counts["se"]

                def mms(tensor, m=m, ld_now=ld_now, ve_now=ve_now, se_now=se_now):
                    tensor.wait_ge(c.sems["ldg"], ld_now)
                    if m == 0:
                        tensor.wait_ge(c.sems["lds"], 64)
                    tensor.wait_ge(c.sems["ve"], ve_now)
                    tensor.wait_ge(c.sems["se"], se_now)
                    ins = mm_tiles2(c, tensor, pa,
                                    lambda k: wsb[:, k * D:(k + 1) * D],
                                    lambda k: xbf[:, k * T:(k + 1) * T],
                                    m, [(0, 0, 512), (512, 512, 512)])
                    c.inc(ins, "te")
                block.tensor(mms)
                te_now = c.counts["te"]

                if proj == 0:
                    def ev_v(vector, m=m, te_now=te_now):
                        vector.wait_ge(c.sems["te"], te_now)
                        c.inc(vector.tensor_scalar(
                            out=r1[:], in0=pa[:], scalar1=sbias[:, m:m + 1],
                            scalar2=0.0, op0=ALU.add, op1=ALU.max), "ve")
                        c.inc(vector.tensor_scalar(
                            out=r2[:], in0=pa[:], scalar1=sbias[:, m:m + 1],
                            scalar2=0.0, op0=ALU.add, op1=ALU.min), "ve")
                    block.vector(ev_v)
                    ve_now2 = c.counts["ve"]

                    def ev_s2(scalar, ve_now2=ve_now2):
                        c.wait(scalar, "ve", ve_now2)
                        c.inc(scalar.activation(out=r2[:], in_=r2[:],
                                                func=AF.Exp), "se")
                    block.scalar(ev_s2)
                    se_now3 = c.counts["se"]

                    def ev_v2(vector, m=m, se_now3=se_now3):
                        vector.wait_ge(c.sems["se"], se_now3)
                        c.inc(vector.scalar_tensor_tensor(
                            out=qbf[:, m * T:(m + 1) * T], in0=r2[:],
                            scalar=-1.0, in1=r1[:],
                            op0=ALU.add, op1=ALU.add), "ve")
                    block.vector(ev_v2)
                elif proj == 1:
                    def ev_s(scalar, m=m, te_now=te_now):
                        scalar.wait_ge(c.sems["te"], te_now)
                        c.inc(scalar.activation(
                            out=vbf[:, m * T:(m + 1) * T], in_=pa[:],
                            func=AF.Silu, bias=sbias[:, KC + m:KC + m + 1]),
                            "se")
                    block.scalar(ev_s)
                else:
                    st_now = c.counts["st"]

                    def ev_s(scalar, m=m, te_now=te_now, st_now=st_now):
                        scalar.wait_ge(c.sems["te"], te_now)
                        if m >= 2:
                            scalar.wait_ge(c.sems["st"], (m - 1) * 16)
                        c.inc(scalar.activation(
                            out=uscr[:, (m % 2) * T:(m % 2 + 1) * T],
                            in_=pa[:], func=AF.Silu,
                            bias=sbias[:, 2 * KC + m:2 * KC + m + 1]), "se")
                    block.scalar(ev_s)
                    se_now2 = c.counts["se"]

                    def st_u(s, m=m, se_now2=se_now2):
                        s.wait_ge(c.sems["se"], se_now2)
                        c.inc(s.dma_start(
                            out=u_out[m * 128:(m + 1) * 128, :],
                            in_=uscr[:, (m % 2) * T:(m % 2 + 1) * T]), "st", 16)
                    block.sync(st_u)

        ve_q = c.counts["ve"]
        se_v = c.counts["se"]

        def kv_phase(tensor):
            tensor.wait_ge(c.sems["ve"], ve_q)
            tensor.wait_ge(c.sems["se"], se_v)
        block.tensor(kv_phase)

        for h in range(KC):
            for t in range(8):
                ve_now = c.counts["ve"]
                se_now = c.counts["se"]

                def trs(tensor, h=h, t=t, ve_now=ve_now, se_now=se_now):
                    tensor.wait_ge(c.sems["ve"], ve_now)
                    tensor.wait_ge(c.sems["se"], se_now)
                    c.inc(tensor.transpose(
                        ptr[:, 0:128],
                        qbf[:, h * T + t * 128:h * T + (t + 1) * 128],
                        sident[:]), "te")
                    c.inc(tensor.transpose(
                        ptr[:, 128:256],
                        vbf[:, h * T + t * 128:h * T + (t + 1) * 128],
                        sident[:]), "te")
                block.tensor(trs)
                te_now = c.counts["te"]

                def evk(vector, te_now=te_now):
                    vector.wait_ge(c.sems["te"], te_now)
                    c.inc(vector.tensor_copy(out=ktm[:], in_=ptr[:, 0:128]),
                          "ve")
                block.vector(evk)

                def evv(scalar, te_now=te_now):
                    scalar.wait_ge(c.sems["te"], te_now)
                    c.inc(scalar.copy(out=vtm[:], in_=ptr[:, 128:256]), "se")
                block.scalar(evv)
                ve_now2 = c.counts["ve"]
                se_now2 = c.counts["se"]

                def mmkv(tensor, h=h, t=t, ve_now2=ve_now2, se_now2=se_now2):
                    tensor.wait_ge(c.sems["ve"], ve_now2)
                    tensor.wait_ge(c.sems["se"], se_now2)
                    c.inc(tensor.matmul(
                        pkv[:, (h % 4) * 128:(h % 4 + 1) * 128], ktm[:], vtm[:],
                        start=(t == 0), stop=(t == 7)), "te")
                block.tensor(mmkv)

            te_now = c.counts["te"]

            def evkv(vector, h=h, te_now=te_now):
                vector.wait_ge(c.sems["te"], te_now)
                c.inc(vector.tensor_copy(
                    out=kvsb[:, h * 128:(h + 1) * 128],
                    in_=pkv[:, (h % 4) * 128:(h % 4 + 1) * 128]), "ve")
            block.vector(evkv)
            ve_kv = c.counts["ve"]

            def guard(tensor, ve_kv=ve_kv):
                tensor.wait_ge(c.sems["ve"], ve_kv)
            block.tensor(guard)

        ve_end = c.counts["ve"]
        se_end = c.counts["se"]

        def stores(s):
            s.wait_ge(c.sems["ve"], ve_end)
            s.wait_ge(c.sems["se"], se_end)
            for m in range(KC):
                c.inc(s.dma_start(out=q_out[m * 128:(m + 1) * 128, :],
                                  in_=qbf[:, m * T:(m + 1) * T]), "st", 16)
            c.inc(s.dma_start(out=kv_out, in_=kvsb[:, :]), "st", 16)
        block.sync(stores)


def _wrap(emitter, io_decl, extra=()):
    from contextlib import ExitStack

    def build():
        nc = bass.Bass("TRN2", target_bir_lowering=False)
        io = {}
        for name, shape, dt, kind in io_decl:
            io[name] = nc.dram_tensor(name, shape, dt, kind=kind).ap()
        with ExitStack() as st0:
            c = mk_ctx(nc, st0)
            c.block = st0.enter_context(nc.Block())
            c.new_phase()
            emitter(c, io, *extra)
            st_end = c.counts["st"]

            def fin(g):
                g.wait_ge(c.sems["st"], st_end)
            c.block.gpsimd(fin)
        return nc
    return build


build_l0a = _wrap(emit_l0a, [
    ("x", [D, T], F32, "ExternalInput"),
    ("wqk", [D, D], F32, "ExternalInput"),
    ("wv", [D, D], F32, "ExternalInput"),
    ("wu", [D, D], F32, "ExternalInput"),
    ("bqk", [128, KC], F32, "ExternalInput"),
    ("bv", [128, KC], F32, "ExternalInput"),
    ("bu", [128, KC], F32, "ExternalInput"),
    ("ident", [128, 128], BF16, "ExternalInput"),
    ("q_out", [D, T], BF16, "ExternalOutput"),
    ("u_out", [D, T], F32, "ExternalOutput"),
    ("kv_out", [128, KC * 128], F32, "ExternalOutput"),
])


def _dump(c, src_sb, dst_hbm, N, bf=False):
    block = c.block
    ve0, se0, te0 = c.counts["ve"], c.counts["se"], c.counts["te"]

    def st(s):
        s.wait_ge(c.sems["ve"], ve0)
        s.wait_ge(c.sems["se"], se0)
        s.wait_ge(c.sems["te"], te0)
        for k in range(KC):
            c.inc(s.dma_start(out=dst_hbm[k * 128:(k + 1) * 128, :],
                              in_=src_sb[:, k * N:(k + 1) * N]), "st", 16)
    block.sync(st)
    st1 = c.counts["st"]

    def gate(vector):
        c.wait(vector, "st", st1)
    block.vector(gate)

    def gate2(scalar):
        c.wait(scalar, "st", st1)
    block.scalar(gate2)



def emit_ln(c, src, dst, N, scr, gcol=None, bcol=None, eps=1e-5, src_wait=None):
    block = c.block
    if src_wait:
        def _gate(vector, src_wait=dict(src_wait)):
            for s_, v_ in src_wait.items():
                vector.wait_ge(c.sems[s_], v_)
        block.vector(_gate)
    for k in range(KC):
        ve0, se0, te0 = c.counts["ve"], c.counts["se"], c.counts["te"]

        def cast(vector, k=k, te0=te0):
            if k >= 2:
                vector.wait_ge(c.sems["te"], te0)
            c.inc(vector.tensor_copy(out=scr["obscr"][:, (k % 2) * N:(k % 2 + 1) * N],
                                     in_=src(k)), "ve")
        block.vector(cast)
        ve1 = c.counts["ve"]

        def sq(vector, k=k):
            c.inc(vector.tensor_mul(out=scr["sqscr"][:, (k % 2) * N:(k % 2 + 1) * N],
                                    in0=scr["obscr"][:, (k % 2) * N:(k % 2 + 1) * N],
                                    in1=scr["obscr"][:, (k % 2) * N:(k % 2 + 1) * N]),
                  "ve")
        block.vector(sq)
        ve1b = c.counts["ve"]

        def red(tensor, k=k, ve1b=ve1b):
            c.wait(tensor, "ve", ve1b)
            c.inc(tensor.matmul(scr["pstat"][0:1, 0:N], scr["onescol"],
                                scr["obscr"][:, (k % 2) * N:(k % 2 + 1) * N],
                                start=(k == 0), stop=(k == KC - 1)), "te")
            c.inc(tensor.matmul(scr["pstat"][0:1, N:2 * N], scr["onescol"],
                                scr["sqscr"][:, (k % 2) * N:(k % 2 + 1) * N],
                                start=(k == 0), stop=(k == KC - 1)), "te")
        block.tensor(red)
    te_red = c.counts["te"]

    def rows(vector):
        c.wait(vector, "te", te_red)
        c.inc(vector.tensor_copy(out=scr["rowbuf"][0:1, :], in_=scr["pstat"][0:1, :]),
              "ve")
    block.vector(rows)
    ve_rows = c.counts["ve"]

    def bc(tensor):
        c.wait(tensor, "ve", ve_rows)
        c.inc(tensor.matmul(scr["pb"][:, 0:N], scr["onesrow"],
                            scr["rowbuf"][0:1, 0:N], start=True, stop=True), "te")
        c.inc(tensor.matmul(scr["pb"][:, N:2 * N], scr["onesrow"],
                            scr["rowbuf"][0:1, N:2 * N], start=True, stop=True),
              "te")
    block.tensor(bc)
    te_bc = c.counts["te"]

    def stats_v(vector):
        c.wait(vector, "te", te_bc)
        vector.memset(scr["epscol"][:, :], float(eps))
        c.inc(vector.tensor_scalar_mul(out=scr["meanB"][:, :], in0=scr["pb"][:, 0:N],
                                       scalar1=1.0 / D), "ve")
        c.inc(vector.tensor_scalar_mul(out=scr["rstdB"][:, :], in0=scr["pb"][:, N:2 * N],
                                       scalar1=1.0 / D), "ve")
        c.inc(vector.scalar_tensor_tensor(out=scr["mrstdB"][:, :], in0=scr["meanB"][:, :],
                                          scalar=0.0, in1=scr["meanB"][:, :],
                                          op0=ALU.add, op1=ALU.mult), "ve")
        c.inc(vector.tensor_sub(out=scr["rstdB"][:, :], in0=scr["rstdB"][:, :],
                                in1=scr["mrstdB"][:, :]), "ve")
    block.vector(stats_v)
    ve_st = c.counts["ve"]

    def stats_s(scalar):
        c.wait(scalar, "ve", ve_st)
        c.inc(scalar.activation(out=scr["rstdB"][:, :], in_=scr["rstdB"][:, :], func=AF.Ln,
                                bias=scr["epscol"][:, 0:1]), "se")
        c.inc(scalar.activation(out=scr["rstdB"][:, :], in_=scr["rstdB"][:, :], func=AF.Exp,
                                scale=-0.5), "se")
    block.scalar(stats_s)
    se_st = c.counts["se"]

    def mr(vector):
        c.wait(vector, "se", se_st)
        c.inc(vector.tensor_mul(out=scr["mrstdB"][:, :], in0=scr["meanB"][:, :],
                                in1=scr["rstdB"][:, :]), "ve")
    block.vector(mr)

    for k in range(KC):
        def norm(vector, k=k):
            c.inc(vector.tensor_mul(out=dst(k), in0=src(k), in1=scr["rstdB"][:, :]),
                  "ve")
            c.inc(vector.tensor_sub(out=dst(k), in0=dst(k), in1=scr["mrstdB"][:, :]),
                  "ve")
        block.vector(norm)
        if gcol is not None:
            def aff(vector, k=k):
                c.inc(vector.tensor_scalar(out=dst(k), in0=dst(k),
                                           scalar1=gcol(k), scalar2=bcol(k),
                                           op0=ALU.mult, op1=ALU.add), "ve")
            block.vector(aff)


def make_ln_scratch(ctx, nc, N, tag="", psum_from=None):
    e = ctx.enter_context
    if psum_from is not None:
        pstat = psum_from[0][0:1, 0:2 * N]
        pb = psum_from[1][:, 0:2 * N]
    else:
        pstat = e(nc.psum_tensor(f"ln_ps{tag}", [1, 2 * N], F32))
        pb = e(nc.psum_tensor(f"ln_pb{tag}", [128, 2 * N], F32))
    return {
        "pstat": pstat,
        "pb": pb,
        "obscr": e(nc.sbuf_tensor(f"ln_ob{tag}", [128, 2 * N], BF16)),
        "sqscr": e(nc.sbuf_tensor(f"ln_sq{tag}", [128, 2 * N], BF16)),
        "rowbuf": e(nc.sbuf_tensor(f"ln_row{tag}", [1, 2 * N], F32)),
        "meanB": e(nc.sbuf_tensor(f"ln_mb{tag}", [128, N], F32)),
        "rstdB": e(nc.sbuf_tensor(f"ln_rb{tag}", [128, N], F32)),
        "mrstdB": e(nc.sbuf_tensor(f"ln_mrb{tag}", [128, N], F32)),
        "epscol": e(nc.sbuf_tensor(f"ln_eps{tag}", [128, 1], F32)),
    }



def emit_l0b(c, io, kv_pair=True, cc_wait=None):
    nc = c.nc
    block = c.block
    q, u, x, wo = io["q"], io["u"], io["x"], io["wo"]
    lng, lnb, bo = io["lng"], io["lnb"], io["bo"]
    allones, onesrow_f = io["allones"], io["onesrow_f"]
    x1_out = io["x1_out"]
    debug = False

    N = 512
    from contextlib import ExitStack
    with ExitStack() as ctx:
        e = ctx.enter_context
        qh = e(nc.sbuf_tensor("qh", [128, KC * N], BF16))
        kvbf = e(nc.sbuf_tensor("kvbf", [128, KC * 128], BF16))
        outsb = e(nc.sbuf_tensor("outsb", [128, KC * N], F32))
        kvf = outsb[:, 0:KC * 128]
        kvf2 = outsb[:, KC * 128:2 * KC * 128]
        uh = e(nc.sbuf_tensor("uh", [128, KC * N], F32))
        xh = e(nc.sbuf_tensor("xh", [128, KC * N], F32))
        zbf = e(nc.sbuf_tensor("zbf", [128, KC * N], BF16))
        wsb = e(nc.sbuf_tensor("wsb", [128, KC * 1024], BF16))
        cols = e(nc.sbuf_tensor("cols", [128, 3 * KC], F32))
        sones = e(nc.sbuf_tensor("sones", [128, 128], BF16))
        sonesrow_f = e(nc.sbuf_tensor("sonesrow_f", [1, 128], F32))
        ressb = outsb
        scr = make_ln_scratch(ctx, nc, N)
        scr["onescol"] = sones[:, 0:1]
        scr["onesrow"] = sonesrow_f[:]
        po = e(nc.psum_tensor("po", [128, 2 * N], F32))

        def loads(s):
            if cc_wait:
                s.wait_ge(c.sems[cc_wait], 1)
            if kv_pair:
                c.inc(s.dma_start(out=kvf, in_=io["kv0"]), "lds", 16)
                c.inc(s.dma_start(out=kvf2, in_=io["kv1"]), "lds", 16)
            else:
                c.inc(s.dma_start(out=kvf, in_=io["kvr"]), "lds", 16)
            c.inc(s.dma_start(out=cols[:, 0:KC], in_=lng), "lds", 16)
            c.inc(s.dma_start(out=cols[:, KC:2 * KC], in_=lnb), "lds", 16)
            c.inc(s.dma_start(out=cols[:, 2 * KC:3 * KC], in_=bo), "lds", 16)
            c.inc(s.dma_start(out=sones[:], in_=allones), "lds", 16)
            c.inc(s.dma_start(out=sonesrow_f[:], in_=onesrow_f), "lds", 16)
        block.sync(loads)
        ld_const = c.counts["lds"]

        def addkv(vector):
            c.wait(vector, "lds", 32 if kv_pair else 16)
            if kv_pair:
                c.inc(vector.tensor_add(out=kvbf[:], in0=kvf[:], in1=kvf2[:]),
                      "ve")
            else:
                c.inc(vector.tensor_copy(out=kvbf[:], in_=kvf[:]), "ve")
        block.vector(addkv)
        ve_kv = c.counts["ve"]

        for nh in range(2):
            n0 = nh * N
            ld_h = None

            def loads_h(s, n0=n0):
                for k in range(KC):
                    c.inc(s.dma_start(out=qh[:, k * N:(k + 1) * N],
                                      in_=q[k * 128:(k + 1) * 128, n0:n0 + N]),
                          "lds", 16)
                for k in range(KC):
                    c.inc(s.dma_start(out=uh[:, k * N:(k + 1) * N],
                                      in_=u[k * 128:(k + 1) * 128, n0:n0 + N]),
                          "lds", 16)
                for k in range(KC):
                    c.inc(s.dma_start(out=xh[:, k * N:(k + 1) * N],
                                      in_=x[k * 128:(k + 1) * 128, n0:n0 + N]),
                          "lds", 16)
            te_prev = c.counts["te"]
            ve_prev = c.counts["ve"]

            def loads_h_guard(s, te_prev=te_prev, ve_prev=ve_prev, n0=n0):
                s.wait_ge(c.sems["te"], te_prev)
                s.wait_ge(c.sems["ve"], ve_prev)
                loads_h(s, n0)
            block.sync(loads_h_guard)
            ld_h = c.counts["lds"]

            for h in range(KC):
                ve0, se0 = c.counts["ve"], c.counts["se"]

                def mmh(tensor, h=h, ld_h=ld_h, ve0=ve0, se0=se0):
                    if h == 0:
                        tensor.wait_ge(c.sems["lds"], ld_h)
                        tensor.wait_ge(c.sems["ve"], ve_kv)
                    else:
                        tensor.wait_ge(c.sems["ve"], ve0)
                    c.inc(tensor.matmul(po[:, (h % 2) * N:(h % 2 + 1) * N],
                                        kvbf[:, h * 128:(h + 1) * 128],
                                        qh[:, h * N:(h + 1) * N],
                                        start=True, stop=True), "te")
                block.tensor(mmh)
                te_h = c.counts["te"]

                def evh(vector, h=h, te_h=te_h):
                    c.wait(vector, "te", te_h)
                    c.inc(vector.tensor_copy(out=outsb[:, h * N:(h + 1) * N],
                                             in_=po[:, (h % 2) * N:(h % 2 + 1) * N]),
                          "ve")
                block.vector(evh)

            if debug and nh == 0:
                _dump(c, outsb, dbg[0], N)
            emit_ln(c, lambda k: outsb[:, k * N:(k + 1) * N],
                    lambda k: outsb[:, k * N:(k + 1) * N], N, scr,
                    gcol=lambda k: cols[:, k:k + 1],
                    bcol=lambda k: cols[:, KC + k:KC + k + 1],
                    src_wait={"ve": c.counts["ve"]})
            se_ln = c.counts["se"]
            ve_ln = c.counts["ve"]

            if debug and nh == 0:
                _dump(c, outsb, dbg[1], N)

            def mulu(vector, se_ln=se_ln):
                c.wait(vector, "se", se_ln)
                c.wait(vector, "lds", ld_h)
                for k in range(KC):
                    c.inc(vector.tensor_mul(out=zbf[:, k * N:(k + 1) * N],
                                            in0=outsb[:, k * N:(k + 1) * N],
                                            in1=uh[:, k * N:(k + 1) * N]), "ve")
            block.vector(mulu)
            ve_z = c.counts["ve"]

            for mh in range(2):
                te_prev2 = c.counts["te"]

                def ldw(g, mh=mh, te_prev2=te_prev2):
                    g.wait_ge(c.sems["te"], te_prev2)
                    for k in range(KC):
                        c.inc(g.dma_start(
                            out=wsb[:, k * 1024:(k + 1) * 1024],
                            in_=wo[k * 128:(k + 1) * 128,
                                   mh * 1024:(mh + 1) * 1024]), "ldg", 16)
                block.gpsimd(ldw)
                ld_w = c.counts["ldg"]

                for mm_ in range(8):
                    m = mh * 8 + mm_
                    ve0 = c.counts["ve"]

                    def mmo(tensor, m=m, mm_=mm_, ld_w=ld_w, ve0=ve0, ve_z=ve_z):
                        if mm_ == 0:
                            tensor.wait_ge(c.sems["ldg"], ld_w)
                            tensor.wait_ge(c.sems["ve"], ve_z)
                        else:
                            tensor.wait_ge(c.sems["ve"], ve0)
                        ins = mm_tiles(c, tensor, po[:, (mm_ % 2) * N:(mm_ % 2 + 1) * N],
                                       lambda k: wsb[:, k * 1024:(k + 1) * 1024],
                                       lambda k: zbf[:, k * N:(k + 1) * N],
                                       mm_, 0, N)
                        c.inc(ins, "te")
                    block.tensor(mmo)
                    te_o = c.counts["te"]

                    def evo(vector, m=m, mm_=mm_, te_o=te_o):
                        c.wait(vector, "te", te_o)
                        c.inc(vector.scalar_tensor_tensor(
                            out=ressb[:, m * N:(m + 1) * N],
                            in0=po[:, (mm_ % 2) * N:(mm_ % 2 + 1) * N],
                            scalar=cols[:, 2 * KC + m:2 * KC + m + 1],
                            in1=xh[:, m * N:(m + 1) * N],
                            op0=ALU.add, op1=ALU.add), "ve")
                    block.vector(evo)

            if debug and nh == 0:
                _dump(c, zbf, dbg[2], N, bf=True)
                _dump(c, ressb, dbg[3], N)
            emit_ln(c, lambda k: ressb[:, k * N:(k + 1) * N],
                    lambda k: ressb[:, k * N:(k + 1) * N], N, scr,
                    src_wait={"ve": c.counts["ve"]})
            ve_pl = c.counts["ve"]
            se_pl = c.counts["se"]

            def store(s, n0=n0, ve_pl=ve_pl, se_pl=se_pl):
                s.wait_ge(c.sems["ve"], ve_pl)
                s.wait_ge(c.sems["se"], se_pl)
                for k in range(KC):
                    c.inc(s.dma_start(out=x1_out[k * 128:(k + 1) * 128, n0:n0 + N],
                                      in_=ressb[:, k * N:(k + 1) * N]), "st", 16)
            block.sync(store)


build_l0b = _wrap(emit_l0b, [
    ("q", [D, T], BF16, "ExternalInput"),
    ("kv0", [128, KC * 128], F32, "ExternalInput"),
    ("kv1", [128, KC * 128], F32, "ExternalInput"),
    ("u", [D, T], F32, "ExternalInput"),
    ("x", [D, T], F32, "ExternalInput"),
    ("wo", [D, D], F32, "ExternalInput"),
    ("lng", [128, KC], F32, "ExternalInput"),
    ("lnb", [128, KC], F32, "ExternalInput"),
    ("bo", [128, KC], F32, "ExternalInput"),
    ("allones", [128, 128], BF16, "ExternalInput"),
    ("onesrow_f", [1, 128], F32, "ExternalInput"),
    ("x1_out", [D, T], F32, "ExternalOutput"),
])


def run_l0b(q_sh, kv_sh, u_sh, x_shards, Wo, bo, ln_g, ln_b):
    nc = build_l0b()
    shared = {
        "wo": np.ascontiguousarray(Wo), "bo": _bias_cols(bo),
        "lng": _bias_cols(ln_g), "lnb": _bias_cols(ln_b),
        "allones": np.ones((128, 128), BF),
        "onesrow_f": np.ones((1, 128), np.float32),
    }
    in_maps = []
    for cix in range(NCORES):
        partner = cix ^ 1
        in_maps.append(dict(shared, q=q_sh[cix], kv0=kv_sh[cix],
                            kv1=kv_sh[partner], u=u_sh[cix], x=x_shards[cix]))
    res = run_bass_kernel_spmd(nc, in_maps, core_ids=list(range(NCORES)))
    return res.results



INTER = 1408
IC = INTER // 128
SH_G = 2


def emit_l0f(c, io):
    nc = c.nc
    block = c.block
    debug = False
    x1, gwt = io["x1"], io["gwt"]
    w1t, w3t, w2t = io["w1t"], io["w3t"], io["w2t"]
    sw1t, sw3t, sw2t = io["sw1t"], io["sw3t"], io["sw2t"]
    ind, ident = io["ind"], io["ident"]
    allones, onesrow_f = io["allones"], io["onesrow_f"]
    x2_out = io["x2_out"]

    N = 512
    from contextlib import ExitStack
    with ExitStack() as ctx:
        e = ctx.enter_context
        xbf = e(nc.sbuf_tensor("xbf", [128, KC * T], BF16))
        yacc = e(nc.sbuf_tensor("yacc", [128, KC * T], F32))
        wbuf = e(nc.sbuf_tensor("wbuf", [128, KC * INTER], BF16))
        hsc = e(nc.sbuf_tensor("hsc", [128, IC * T], BF16))
        hx = e(nc.sbuf_tensor("hx", [128, T], F32))
        gbuf = e(nc.sbuf_tensor("gbuf", [128, KC * 16], BF16))
        stm = e(nc.sbuf_tensor("stm", [128, 8 * 16], F32))
        ptile = e(nc.sbuf_tensor("ptile", [128, 128], F32))
        mk = e(nc.sbuf_tensor("mk", [128, 128], F32))
        rcols = e(nc.sbuf_tensor("rcols", [128, 40], F32))
        wtmb = e(nc.sbuf_tensor("wtmb", [128, 128], BF16))
        wfm = e(nc.sbuf_tensor("wfm", [16, T], BF16))
        sind = e(nc.sbuf_tensor("sind", [16, D], BF16))
        sident = e(nc.sbuf_tensor("sident", [128, 128], BF16))
        sones = e(nc.sbuf_tensor("sones", [128, 128], BF16))
        sonesrow_f = e(nc.sbuf_tensor("sonesrow_f", [1, 128], F32))
        ph = e(nc.psum_tensor("ph", [128, T], F32))
        py = e(nc.psum_tensor("py", [128, T], F32))
        pmisc = e(nc.psum_tensor("pmisc", [128, T], F32))
        pw = e(nc.psum_tensor("pw", [16, 1024], BF16))
        scr = make_ln_scratch(ctx, nc, N, psum_from=(ph, py))
        scr["onescol"] = sones[:, 0:1]
        scr["onesrow"] = sonesrow_f[:]

        def loads_s(s):
            for k in range(KC):
                c.inc(s.dma_start(out=yacc[:, k * T:(k + 1) * T],
                                  in_=x1[k * 128:(k + 1) * 128, :]), "lds", 16)
            c.inc(s.dma_start(out=sind[:], in_=ind), "lds", 16)
            c.inc(s.dma_start(out=sident[:], in_=ident), "lds", 16)
            c.inc(s.dma_start(out=sones[:], in_=allones), "lds", 16)
            c.inc(s.dma_start(out=sonesrow_f[:], in_=onesrow_f), "lds", 16)
        block.sync(loads_s)
        ld_consts = c.counts["lds"]

        def loads_g(g):
            for k in range(KC):
                c.inc(g.dma_start(out=xbf[:, k * T:(k + 1) * T],
                                  in_=x1[k * 128:(k + 1) * 128, :]), "ldg", 16)
            for k in range(KC):
                c.inc(g.dma_start(out=gbuf[:, k * 16:(k + 1) * 16],
                                  in_=gwt[k * 128:(k + 1) * 128, :]), "ldg", 16)
        block.gpsimd(loads_g)
        ld_xg = c.counts["ldg"]

        def rmms(tensor):
            tensor.wait_ge(c.sems["ldg"], ld_xg)
            for t in range(8):
                for k in range(KC):
                    c.inc(tensor.matmul(
                        pmisc[:, t * 16:(t + 1) * 16],
                        xbf[:, k * T + t * 128:k * T + (t + 1) * 128],
                        gbuf[:, k * 16:(k + 1) * 16],
                        start=(k == 0), stop=(k == KC - 1)), "te")
        block.tensor(rmms)
        te_r = c.counts["te"]

        def rv(vector):
            c.wait(vector, "te", te_r)
            c.inc(vector.tensor_copy(out=stm[:], in_=pmisc[:, 0:128]), "ve")
        block.vector(rv)
        ve_r = c.counts["ve"]

        def sl(t):
            return stm[:, t * 16:(t + 1) * 16]

        def rtop(vector):
            for t in range(8):
                c.inc(vector.reduce_max(out=rcols[:, t:t + 1], in_=sl(t),
                                        axis=mybir.AxisListType.X), "ve")
            for t in range(8):
                c.inc(vector.tensor_scalar_mul(out=rcols[:, 8 + t:9 + t],
                                               in0=rcols[:, t:t + 1],
                                               scalar1=-1.0), "ve")
            for t in range(8):
                c.inc(vector.tensor_scalar(out=mk[:, t * 16:(t + 1) * 16],
                                           in0=sl(t),
                                           scalar1=rcols[:, t:t + 1],
                                           scalar2=None, op0=ALU.is_ge), "ve")
            for t in range(8):
                c.inc(vector.scalar_tensor_tensor(
                    out=mk[:, t * 16:(t + 1) * 16],
                    in0=mk[:, t * 16:(t + 1) * 16], scalar=-1e30, in1=sl(t),
                    op0=ALU.mult, op1=ALU.add), "ve")
            for t in range(8):
                c.inc(vector.reduce_max(out=rcols[:, 16 + t:17 + t],
                                        in_=mk[:, t * 16:(t + 1) * 16],
                                        axis=mybir.AxisListType.X), "ve")
        block.vector(rtop)
        ve_t = c.counts["ve"]

        def rexp(scalar):
            c.wait(scalar, "ve", ve_t)
            for t in range(8):
                c.inc(scalar.activation(out=ptile[:, t * 16:(t + 1) * 16],
                                        in_=sl(t), func=AF.Exp,
                                        bias=rcols[:, 8 + t:9 + t]), "se")
        block.scalar(rexp)
        se_t = c.counts["se"]

        def rw(vector):
            c.wait(vector, "se", se_t)
            for t in range(8):
                c.inc(vector.reduce_sum(out=rcols[:, 24 + t:25 + t],
                                        in_=ptile[:, t * 16:(t + 1) * 16],
                                        axis=mybir.AxisListType.X), "ve")
            for t in range(8):
                c.inc(vector.reciprocal(out=rcols[:, 32 + t:33 + t],
                                        in_=rcols[:, 24 + t:25 + t]), "ve")
            for t in range(8):
                c.inc(vector.tensor_scalar(out=mk[:, t * 16:(t + 1) * 16],
                                           in0=sl(t),
                                           scalar1=rcols[:, 16 + t:17 + t],
                                           scalar2=None, op0=ALU.is_ge), "ve")
            for t in range(8):
                c.inc(vector.tensor_mul(out=ptile[:, t * 16:(t + 1) * 16],
                                        in0=ptile[:, t * 16:(t + 1) * 16],
                                        in1=mk[:, t * 16:(t + 1) * 16]), "ve")
            for t in range(8):
                c.inc(vector.tensor_scalar(out=wtmb[:, t * 16:(t + 1) * 16],
                                           in0=ptile[:, t * 16:(t + 1) * 16],
                                           scalar1=rcols[:, 32 + t:33 + t],
                                           scalar2=None, op0=ALU.mult), "ve")
        block.vector(rw)
        ve_t2 = c.counts["ve"]

        def rtr(tensor):
            c.wait(tensor, "ve", ve_t2)
            for t in range(8):
                c.inc(tensor.transpose(pw[:, t * 128:(t + 1) * 128],
                                       wtmb[:, t * 16:(t + 1) * 16],
                                       sident[:]), "te")
        block.tensor(rtr)
        te_t = c.counts["te"]

        def rev(vector):
            c.wait(vector, "te", te_t)
            c.inc(vector.tensor_copy(out=wfm[:], in_=pw[:]), "ve")
        block.vector(rev)

        se_wfm = c.counts["ve"]
        if debug:
            ve_d, se_d = c.counts["ve"], c.counts["se"]

            def dmp0(s, ve_d=ve_d, se_d=se_d):
                s.wait_ge(c.sems["ve"], ve_d)
                s.wait_ge(c.sems["se"], se_d)
                c.inc(s.dma_start(out=dstm, in_=stm[:]), "st", 16)
                c.inc(s.dma_start(out=dwfm, in_=wfm[:]), "st", 16)
            block.sync(dmp0)

        sewA = [0, 0]
        vewB = [0, 0]
        vewC = [0, 0]
        for ep in range(18):
            gated = ep < 16
            eix = ep if gated else None
            g = None if gated else ep - 16
            nic = IC

            if gated:
                ve_pm = c.counts["ve"]
                se_pm = c.counts["se"]

                def bca(tensor, eix=eix, ve_pm=ve_pm, se_pm=se_pm):
                    tensor.wait_ge(c.sems["ve"], se_wfm)
                    tensor.wait_ge(c.sems["ve"], ve_pm)
                    tensor.wait_ge(c.sems["se"], se_pm)
                    for n0 in (0, N):
                        c.inc(tensor.matmul(
                            pmisc[:, n0:n0 + N],
                            sind[:, eix * 128:(eix + 1) * 128],
                            wfm[:, n0:n0 + N], start=True, stop=True), "te")
                block.tensor(bca)

            te_prevC = c.counts["te"]

            def lw1(gp, ep=ep, g=g, te_prevC=te_prevC):
                gp.wait_ge(c.sems["te"], te_prevC)
                for k in range(KC):
                    if ep < 16:
                        src = w1t[ep, k * 128:(k + 1) * 128, :]
                    else:
                        src = sw1t[k * 128:(k + 1) * 128,
                                   g * INTER:(g + 1) * INTER]
                    c.inc(gp.dma_start(out=wbuf[:, k * INTER:(k + 1) * INTER],
                                       in_=src), "ldg", 16)
            block.gpsimd(lw1)
            ld_w1 = c.counts["ldg"]

            for i in range(nic):
                for hf in (0, 1):
                    need = sewA[hf]

                    def mmah(tensor, i=i, hf=hf, need=need, ld_w1=ld_w1):
                        tensor.wait_ge(c.sems["ldg"], ld_w1)
                        tensor.wait_ge(c.sems["se"], need)
                        ins = mm_tiles(c, tensor, ph[:, hf * N:(hf + 1) * N],
                                       lambda k: wbuf[:, k * INTER:(k + 1) * INTER],
                                       lambda k: xbf[:, k * T:(k + 1) * T],
                                       i, hf * N, N)
                        c.inc(ins, "te")
                    block.tensor(mmah)
                    te_h = c.counts["te"]

                    def evah(scalar, i=i, hf=hf, te_h=te_h):
                        scalar.wait_ge(c.sems["te"], te_h)
                        c.inc(scalar.activation(
                            out=hsc[:, i * T + hf * N:i * T + (hf + 1) * N],
                            in_=ph[:, hf * N:(hf + 1) * N], func=AF.Silu),
                            "se")
                    block.scalar(evah)
                    sewA[hf] = c.counts["se"]

            te_prevA = c.counts["te"]

            def lw3(gp, ep=ep, g=g, te_prevA=te_prevA):
                gp.wait_ge(c.sems["te"], te_prevA)
                for k in range(KC):
                    if ep < 16:
                        src = w3t[ep, k * 128:(k + 1) * 128, :]
                    else:
                        src = sw3t[k * 128:(k + 1) * 128,
                                   g * INTER:(g + 1) * INTER]
                    c.inc(gp.dma_start(out=wbuf[:, k * INTER:(k + 1) * INTER],
                                       in_=src), "ldg", 16)
            block.gpsimd(lw3)
            ld_w3 = c.counts["ldg"]
            se_afterA = c.counts["se"]

            for i in range(nic):
                for hf in (0, 1):
                    need = vewB[hf]

                    def mmbh(tensor, i=i, hf=hf, need=need, ld_w3=ld_w3,
                             se_afterA=se_afterA):
                        tensor.wait_ge(c.sems["ldg"], ld_w3)
                        tensor.wait_ge(c.sems["ve"], need)
                        if i == 0 and hf == 0:
                            tensor.wait_ge(c.sems["se"], se_afterA)
                        ins = mm_tiles(c, tensor, ph[:, hf * N:(hf + 1) * N],
                                       lambda k: wbuf[:, k * INTER:(k + 1) * INTER],
                                       lambda k: xbf[:, k * T:(k + 1) * T],
                                       i, hf * N, N)
                        c.inc(ins, "te")
                    block.tensor(mmbh)
                    te_h = c.counts["te"]

                    def evbh(vector, i=i, hf=hf, te_h=te_h, gated=gated):
                        vector.wait_ge(c.sems["te"], te_h)
                        lo = i * T + hf * N
                        if gated:
                            c.inc(vector.tensor_mul(
                                out=hx[:, hf * N:(hf + 1) * N],
                                in0=ph[:, hf * N:(hf + 1) * N],
                                in1=hsc[:, lo:lo + N]), "ve")
                            c.inc(vector.tensor_mul(
                                out=hsc[:, lo:lo + N],
                                in0=hx[:, hf * N:(hf + 1) * N],
                                in1=pmisc[:, hf * N:(hf + 1) * N]), "ve")
                        else:
                            c.inc(vector.tensor_mul(
                                out=hsc[:, lo:lo + N],
                                in0=ph[:, hf * N:(hf + 1) * N],
                                in1=hsc[:, lo:lo + N]), "ve")
                    block.vector(evbh)
                    vewB[hf] = c.counts["ve"]

            te_prevB = c.counts["te"]
            ve_hsc = c.counts["ve"]

            def lw2(gp, ep=ep, g=g, te_prevB=te_prevB):
                gp.wait_ge(c.sems["te"], te_prevB)
                for i in range(IC):
                    if ep < 16:
                        src = w2t[ep, i * 128:(i + 1) * 128, :]
                    else:
                        src = sw2t[(g * IC + i) * 128:(g * IC + i + 1) * 128, :]
                    c.inc(gp.dma_start(out=wbuf[:, i * D:(i + 1) * D],
                                       in_=src), "ldg", 16)
            block.gpsimd(lw2)
            ld_w2 = c.counts["ldg"]

            for m in range(KC):
                for hf in (0, 1):
                    need = vewC[hf]

                    def mmch(tensor, m=m, hf=hf, need=need, ld_w2=ld_w2,
                             ve_hsc=ve_hsc):
                        tensor.wait_ge(c.sems["ldg"], ld_w2)
                        tensor.wait_ge(c.sems["ve"], max(need, ve_hsc))
                        ins = mm_tiles(c, tensor, py[:, hf * N:(hf + 1) * N],
                                       lambda i: wbuf[:, i * D:(i + 1) * D],
                                       lambda i: hsc[:, i * T:(i + 1) * T],
                                       m, hf * N, N, kcs=range(IC))
                        c.inc(ins, "te")
                    block.tensor(mmch)
                    te_h = c.counts["te"]

                    def evch(vector, m=m, hf=hf, te_h=te_h):
                        vector.wait_ge(c.sems["te"], te_h)
                        if m == 0 and hf == 0:
                            vector.wait_ge(c.sems["lds"], 16 * KC)
                        lo = m * T + hf * N
                        c.inc(vector.tensor_add(
                            out=yacc[:, lo:lo + N],
                            in0=py[:, hf * N:(hf + 1) * N],
                            in1=yacc[:, lo:lo + N]), "ve")
                    block.vector(evch)
                    vewC[hf] = c.counts["ve"]


        for nh in range(2):
            n0 = nh * N
            emit_ln(c, lambda k: yacc[:, k * T + n0:k * T + n0 + N],
                    lambda k: yacc[:, k * T + n0:k * T + n0 + N], N, scr,
                    src_wait={"ve": c.counts["ve"]})
            ve_pl = c.counts["ve"]

            def store(s, n0=n0, ve_pl=ve_pl):
                s.wait_ge(c.sems["ve"], ve_pl)
                for k in range(KC):
                    c.inc(s.dma_start(out=x2_out[k * 128:(k + 1) * 128,
                                                 n0:n0 + N],
                                      in_=yacc[:, k * T + n0:k * T + n0 + N]),
                          "st", 16)
            block.sync(store)


build_l0f = _wrap(emit_l0f, [
    ("x1", [D, T], F32, "ExternalInput"),
    ("gwt", [D, 16], F32, "ExternalInput"),
    ("w1t", [16, D, INTER], F32, "ExternalInput"),
    ("w3t", [16, D, INTER], F32, "ExternalInput"),
    ("w2t", [16, INTER, D], F32, "ExternalInput"),
    ("sw1t", [D, 2 * INTER], F32, "ExternalInput"),
    ("sw3t", [D, 2 * INTER], F32, "ExternalInput"),
    ("sw2t", [2 * INTER, D], F32, "ExternalInput"),
    ("ind", [16, D], BF16, "ExternalInput"),
    ("ident", [128, 128], BF16, "ExternalInput"),
    ("allones", [128, 128], BF16, "ExternalInput"),
    ("onesrow_f", [1, 128], F32, "ExternalInput"),
    ("x2_out", [D, T], F32, "ExternalOutput"),
])


def _make_ind():
    ind = np.zeros((16, D), BF)
    for e_ in range(16):
        ind[e_, e_ * 128:(e_ + 1) * 128] = 1
    return ind


def run_l0f(x1_sh, gate_w, w1, w2, w3, sw1, sw2, sw3):
    nc = build_l0f()
    shared = {
        "gwt": np.ascontiguousarray(gate_w.T),
        "w1t": np.ascontiguousarray(w1.transpose(0, 2, 1)),
        "w3t": np.ascontiguousarray(w3.transpose(0, 2, 1)),
        "w2t": np.ascontiguousarray(w2.transpose(0, 2, 1)),
        "sw1t": np.ascontiguousarray(sw1.T),
        "sw3t": np.ascontiguousarray(sw3.T),
        "sw2t": np.ascontiguousarray(sw2.T),
        "ind": _make_ind(),
        "ident": _IDENT_BF,
        "allones": np.ones((128, 128), BF),
        "onesrow_f": np.ones((1, 128), np.float32),
    }
    in_maps = [dict(shared, x1=x1_sh[cix]) for cix in range(NCORES)]
    res = run_bass_kernel_spmd(nc, in_maps, core_ids=list(range(NCORES)))
    return res.results



CH = 256
NCH = T // CH


def emit_l1a(c, io):
    nc = c.nc
    block = c.block
    x2, wqkvt = io["x2"], io["wqkvt"]
    cosf, sinf, rtc, maskt = io["cosf"], io["sinf"], io["rtc"], io["maskt"]
    allones, ident = io["allones"], io["ident"]
    q_out, num_out, den_out = io["q_out"], io["num_out"], io["den_out"]
    s_out, z_out = io["s_out"], io["z_out"]

    from contextlib import ExitStack
    with ExitStack() as ctx:
        e = ctx.enter_context
        xbf = e(nc.sbuf_tensor("xbf", [128, KC * T], BF16))
        wg = e(nc.sbuf_tensor("wg", [128, KC * 512], BF16))
        qsb = e(nc.sbuf_tensor("qsb", [128, KC * T], BF16))
        ksb = e(nc.sbuf_tensor("ksb", [128, KC * T], BF16))
        vsb = e(nc.sbuf_tensor("vsb", [128, KC * T], BF16))
        scos = e(nc.sbuf_tensor("scos", [128, T], F32))
        ssin = e(nc.sbuf_tensor("ssin", [128, T], F32))
        srt = e(nc.sbuf_tensor("srt", [128, 128], BF16))
        smask = e(nc.sbuf_tensor("smaskt", [128, 2 * CH], BF16))
        sident = e(nc.sbuf_tensor("sident", [128, 128], BF16))
        sones = e(nc.sbuf_tensor("sones", [128, 128], BF16))
        qb = e(nc.sbuf_tensor("qb", [128, T], BF16))
        s1 = e(nc.sbuf_tensor("s1", [128, T], F32))
        s2 = e(nc.sbuf_tensor("s2", [128, T], F32))
        ktm = e(nc.sbuf_tensor("ktm", [128, 2 * 128], BF16))
        vtm = e(nc.sbuf_tensor("vtm", [128, 2 * 128], BF16))
        attm = e(nc.sbuf_tensor("attm", [128, 2 * CH], BF16))
        sloc = e(nc.sbuf_tensor("sloc", [128, KC * 128], F32))
        sbf = e(nc.sbuf_tensor("sbf", [128, KC * 128], BF16))
        zloc = e(nc.sbuf_tensor("zloc", [128, KC], F32))
        zbf = e(nc.sbuf_tensor("zbf", [128, KC], BF16))
        ztmp = e(nc.sbuf_tensor("ztmp", [128, KC], F32))
        numscr = e(nc.sbuf_tensor("numscr", [128, 2 * CH], F32))
        denscr = e(nc.sbuf_tensor("denscr", [1, 2 * CH], F32))
        pa = e(nc.psum_tensor("pa", [128, T], F32))
        pr = e(nc.psum_tensor("pr", [128, T], F32))
        patt = e(nc.psum_tensor("patt", [128, 2 * CH], F32))
        pmix = e(nc.psum_tensor("pmix", [128, T], F32))
        ptr = e(nc.psum_tensor("ptr", [128, 4 * 128], BF16))
        pnum = pmix[:, 0:CH]
        pS = pmix[:, CH:CH + 128]
        pden = pmix[0:1, 512:512 + CH]

        def loads_s(s):
            c.inc(s.dma_start(out=scos[:], in_=cosf), "lds", 16)
            c.inc(s.dma_start(out=ssin[:], in_=sinf), "lds", 16)
            c.inc(s.dma_start(out=srt[:], in_=rtc), "lds", 16)
            c.inc(s.dma_start(out=smask[:], in_=maskt), "lds", 16)
            c.inc(s.dma_start(out=sident[:], in_=ident), "lds", 16)
            c.inc(s.dma_start(out=sones[:], in_=allones), "lds", 16)
        block.sync(loads_s)
        ld_c = c.counts["lds"]

        def loads_g(g):
            for k in range(KC):
                c.inc(g.dma_start(out=xbf[:, k * T:(k + 1) * T],
                                  in_=x2[k * 128:(k + 1) * 128, :]), "ldg", 16)
        block.gpsimd(loads_g)

        def init_v(vector):
            vector.memset(sloc[:, :], 0.0)
            vector.memset(zloc[:, :], 0.0)
            c.inc(vector.tensor_copy(out=sbf[:, :], in_=sloc[:, :]), "ve")
            c.inc(vector.tensor_copy(out=zbf[:, :], in_=zloc[:, :]), "ve")
        block.vector(init_v)

        for g in range(12):
            te_prev = c.counts["te"]

            def lw(gp, g=g, te_prev=te_prev):
                gp.wait_ge(c.sems["te"], te_prev)
                for k in range(KC):
                    c.inc(gp.dma_start(
                        out=wg[:, k * 512:(k + 1) * 512],
                        in_=wqkvt[k * 128:(k + 1) * 128,
                                  g * 512:(g + 1) * 512]), "ldg", 16)
            block.gpsimd(lw)
            ld_w = c.counts["ldg"]

            for mm_ in range(4):
                mg = g * 4 + mm_
                ve_now = c.counts["ve"]
                se_now = c.counts["se"]

                def mmp(tensor, mm_=mm_, ld_w=ld_w, ve_now=ve_now, se_now=se_now):
                    tensor.wait_ge(c.sems["ldg"], ld_w)
                    tensor.wait_ge(c.sems["ve"], ve_now)
                    tensor.wait_ge(c.sems["se"], se_now)
                    ins = mm_tiles2(c, tensor, pa,
                                    lambda k: wg[:, k * 512:(k + 1) * 512],
                                    lambda k: xbf[:, k * T:(k + 1) * T],
                                    mm_, [(0, 0, 512), (512, 512, 512)])
                    c.inc(ins, "te")
                block.tensor(mmp)
                te_p = c.counts["te"]

                if mg >= 32:
                    h = mg - 32

                    def evv(vector, h=h, te_p=te_p):
                        vector.wait_ge(c.sems["te"], te_p)
                        c.inc(vector.tensor_copy(out=vsb[:, h * T:(h + 1) * T],
                                                 in_=pa[:]), "ve")
                    block.vector(evv)
                else:
                    h = mg % KC
                    dst = qsb if mg < KC else ksb

                    def ev1(vector, te_p=te_p):
                        vector.wait_ge(c.sems["te"], te_p)
                        c.inc(vector.tensor_copy(out=qb[:], in_=pa[:]), "ve")
                    block.vector(ev1)
                    se_1 = c.counts["ve"]

                    def mmr(tensor, se_1=se_1):
                        tensor.wait_ge(c.sems["ve"], se_1)
                        for n0 in (0, 512):
                            c.inc(tensor.matmul(pr[:, n0:n0 + 512], srt[:],
                                                qb[:, n0:n0 + 512],
                                                start=True, stop=True), "te")
                    block.tensor(mmr)
                    te_r2 = c.counts["te"]

                    def ev2(vector, te_r2=te_r2):
                        vector.wait_ge(c.sems["te"], te_r2)
                        c.inc(vector.tensor_mul(out=s1[:], in0=pa[:],
                                                in1=scos[:]), "ve")
                        c.inc(vector.tensor_mul(out=s2[:], in0=pr[:],
                                                in1=ssin[:]), "ve")
                        c.inc(vector.tensor_add(out=s1[:], in0=s1[:],
                                                in1=s2[:]), "ve")
                    block.vector(ev2)
                    ve_2 = c.counts["ve"]

                    def ev3(vector, ve_2=ve_2):
                        c.inc(vector.tensor_relu(out=s2[:], in_=s1[:]), "ve")
                        c.inc(vector.tensor_scalar_min(out=s1[:], in0=s1[:],
                                                       scalar1=0.0), "ve")
                    block.vector(ev3)
                    ve_4 = c.counts["ve"]

                    def ev5(scalar, ve_4=ve_4):
                        c.wait(scalar, "ve", ve_4)
                        c.inc(scalar.activation(out=s1[:], in_=s1[:],
                                                func=AF.Exp), "se")
                    block.scalar(ev5)
                    se_5 = c.counts["se"]

                    def ev6(vector, h=h, dst=dst, se_5=se_5):
                        c.wait(vector, "se", se_5)
                        c.inc(vector.tensor_add(out=dst[:, h * T:(h + 1) * T],
                                                in0=s1[:], in1=s2[:]), "ve")
                    block.vector(ev6)

        ve_qkv = c.counts["ve"]
        se_qkv = c.counts["se"]

        def scangate(tensor):
            tensor.wait_ge(c.sems["ve"], ve_qkv)
            tensor.wait_ge(c.sems["se"], se_qkv)
            tensor.wait_ge(c.sems["lds"], ld_c)
        block.tensor(scangate)

        for h in range(KC):
            for ci in range(NCH):
                c0 = ci * CH
                ve_a = c.counts["ve"]
                se_a = c.counts["se"]

                def trs(tensor, h=h, c0=c0, ve_a=ve_a, se_a=se_a):
                    tensor.wait_ge(c.sems["ve"], ve_a)
                    tensor.wait_ge(c.sems["se"], se_a)
                    for lb in range(2):
                        c.inc(tensor.transpose(
                            ptr[:, lb * 128:(lb + 1) * 128],
                            ksb[:, h * T + c0 + lb * 128:h * T + c0 + (lb + 1) * 128],
                            sident[:]), "te")
                        c.inc(tensor.transpose(
                            ptr[:, 256 + lb * 128:256 + (lb + 1) * 128],
                            vsb[:, h * T + c0 + lb * 128:h * T + c0 + (lb + 1) * 128],
                            sident[:]), "te")
                block.tensor(trs)
                te_tr = c.counts["te"]

                def evtr(vector, te_tr=te_tr):
                    vector.wait_ge(c.sems["te"], te_tr)
                    c.inc(vector.tensor_copy(out=ktm[:], in_=ptr[:, 0:256]),
                          "ve")
                    c.inc(vector.tensor_copy(out=vtm[:], in_=ptr[:, 256:512]),
                          "ve")
                block.vector(evtr)
                ve_tr = c.counts["ve"]
                se_tr = c.counts["se"]

                def mmatt(tensor, h=h, c0=c0, ve_tr=ve_tr, se_tr=se_tr):
                    tensor.wait_ge(c.sems["ve"], ve_tr)
                    for mb in range(2):
                        c.inc(tensor.matmul(
                            patt[:, mb * CH:(mb + 1) * CH],
                            ksb[:, h * T + c0 + mb * 128:h * T + c0 + (mb + 1) * 128],
                            qsb[:, h * T + c0:h * T + c0 + CH],
                            start=True, stop=True), "te")
                block.tensor(mmatt)
                te_att = c.counts["te"]

                def evatt(vector, te_att=te_att):
                    vector.wait_ge(c.sems["te"], te_att)
                    for mb in range(2):
                        c.inc(vector.tensor_mul(
                            out=attm[:, mb * CH:(mb + 1) * CH],
                            in0=patt[:, mb * CH:(mb + 1) * CH],
                            in1=smask[:, mb * CH:(mb + 1) * CH]), "ve")
                    c.inc(vector.reduce_sum(out=ztmp[:, 0:1],
                                            in_=ksb[:, h * T + c0:h * T + c0 + CH],
                                            axis=mybir.AxisListType.X), "ve")
                block.vector(evatt)
                ve_att = c.counts["ve"]

                po_bf = c.counts["po"]

                def mmnum(tensor, h=h, c0=c0, ve_att=ve_att, po_bf=po_bf):
                    tensor.wait_ge(c.sems["ve"], ve_att)
                    tensor.wait_ge(c.sems["po"], po_bf)
                    for mb in range(2):
                        c.inc(tensor.matmul(
                            pnum, vtm[:, mb * 128:(mb + 1) * 128],
                            attm[:, mb * CH:(mb + 1) * CH],
                            start=(mb == 0), stop=False), "te")
                    c.inc(tensor.matmul(
                        pnum, sbf[:, h * 128:(h + 1) * 128],
                        qsb[:, h * T + c0:h * T + c0 + CH],
                        start=False, stop=True), "te")
                    for mb in range(2):
                        c.inc(tensor.matmul(
                            pden, sones[:, 0:1],
                            attm[:, mb * CH:(mb + 1) * CH],
                            start=(mb == 0), stop=False), "te")
                    c.inc(tensor.matmul(
                        pden, zbf[:, h:h + 1],
                        qsb[:, h * T + c0:h * T + c0 + CH],
                        start=False, stop=True), "te")
                    for lb in range(2):
                        c.inc(tensor.matmul(
                            pS, ktm[:, lb * 128:(lb + 1) * 128],
                            vtm[:, lb * 128:(lb + 1) * 128],
                            start=(lb == 0), stop=(lb == 1)), "te")
                block.tensor(mmnum)
                te_n = c.counts["te"]

                st_cap = c.counts["st"]

                def evn(vector, h=h, ci=ci, te_n=te_n, st_cap=st_cap):
                    vector.wait_ge(c.sems["te"], te_n)
                    vector.wait_ge(c.sems["st"], st_cap)
                    c.inc(vector.tensor_copy(
                        out=numscr[:, (ci % 2) * CH:(ci % 2 + 1) * CH],
                        in_=pnum), "ve")
                    c.inc(vector.tensor_add(out=sloc[:, h * 128:(h + 1) * 128],
                                            in0=pS,
                                            in1=sloc[:, h * 128:(h + 1) * 128]),
                          "ve")
                    c.inc(vector.tensor_add(out=zloc[:, h:h + 1],
                                            in0=ztmp[:, 0:1],
                                            in1=zloc[:, h:h + 1]), "ve")
                    c.inc(vector.tensor_copy(
                        out=denscr[:, (ci % 2) * CH:(ci % 2 + 1) * CH],
                        in_=pden), "ve")
                block.vector(evn)
                ve_n = c.counts["ve"]

                def evbf(gp, h=h, ve_n=ve_n):
                    c.wait(gp, "ve", ve_n)
                    c.inc(gp.tensor_copy(out=sbf[:, h * 128:(h + 1) * 128],
                                         in_=sloc[:, h * 128:(h + 1) * 128]),
                          "po")
                    c.inc(gp.tensor_copy(out=zbf[:, h:h + 1],
                                         in_=zloc[:, h:h + 1]), "po")
                block.gpsimd(evbf)

                def stn(s, h=h, ci=ci, c0=c0, ve_n=ve_n):
                    s.wait_ge(c.sems["ve"], ve_n)
                    c.inc(s.dma_start(
                        out=num_out[h * 128:(h + 1) * 128, c0:c0 + CH],
                        in_=numscr[:, (ci % 2) * CH:(ci % 2 + 1) * CH]),
                        "st", 16)
                    c.inc(s.dma_start(
                        out=den_out[0:1, h * T + c0:h * T + c0 + CH],
                        in_=denscr[:, (ci % 2) * CH:(ci % 2 + 1) * CH]),
                        "st", 16)
                block.sync(stn)

        if "mcol" in io:
            smcol = e(nc.sbuf_tensor("smcol", [128, 1], F32))

            def scale_sz(vector):
                c.wait(vector, "lds")
                c.wait(vector, "se")
                c.inc(vector.tensor_scalar_mul(out=sloc[:, :], in0=sloc[:, :],
                                               scalar1=smcol[:, 0:1]), "ve")
                c.inc(vector.tensor_scalar_mul(out=zloc[:, :], in0=zloc[:, :],
                                               scalar1=smcol[:, 0:1]), "ve")

            def ld_m(s):
                c.inc(s.dma_start(out=smcol[:], in_=io["mcol"]), "lds", 16)
            block.sync(ld_m)
            block.vector(scale_sz)
        ve_end = c.counts["ve"]
        se_end = c.counts["se"]

        def stores(s):
            s.wait_ge(c.sems["ve"], ve_end)
            s.wait_ge(c.sems["se"], se_end)
            for h in range(KC):
                c.inc(s.dma_start(out=q_out[h * 128:(h + 1) * 128, :],
                                  in_=qsb[:, h * T:(h + 1) * T]), "st", 16)
            c.inc(s.dma_start(out=s_out, in_=sloc[:, :]), "st", 16)
            c.inc(s.dma_start(out=z_out, in_=zloc[:, :]), "st", 16)
        block.sync(stores)


build_l1a = _wrap(emit_l1a, [
    ("x2", [D, T], F32, "ExternalInput"),
    ("wqkvt", [D, 3 * D], F32, "ExternalInput"),
    ("cosf", [128, T], F32, "ExternalInput"),
    ("sinf", [128, T], F32, "ExternalInput"),
    ("rtc", [128, 128], BF16, "ExternalInput"),
    ("maskt", [128, 2 * CH], BF16, "ExternalInput"),
    ("allones", [128, 128], BF16, "ExternalInput"),
    ("ident", [128, 128], BF16, "ExternalInput"),
    ("q_out", [D, T], BF16, "ExternalOutput"),
    ("num_out", [D, T], F32, "ExternalOutput"),
    ("den_out", [1, KC * T], F32, "ExternalOutput"),
    ("s_out", [128, KC * 128], F32, "ExternalOutput"),
    ("z_out", [128, KC], F32, "ExternalOutput"),
])


def _rope_tables(t0):
    pos = np.arange(t0, t0 + T, dtype=np.float64)
    inv = 1.0 / (10000.0 ** (np.arange(0, 128, 2) / 128.0))
    fr = np.outer(pos, inv)
    emb = np.concatenate([fr, fr], axis=-1)
    return (np.ascontiguousarray(np.cos(emb).T.astype(np.float32)),
            np.ascontiguousarray(np.sin(emb).T.astype(np.float32)))


def _rt_const():
    rt = np.zeros((128, 128), np.float32)
    for i in range(64):
        rt[2 * i + 1, 2 * i] = -1.0
        rt[2 * i, 2 * i + 1] = 1.0
    return rt.astype(BF)


def _maskt_const():
    mk = np.zeros((128, 2 * CH), np.float32)
    for mb in range(2):
        for m_ in range(128):
            gm = mb * 128 + m_
            mk[m_, mb * CH + gm:(mb + 1) * CH] = 1.0
    return mk.astype(BF)


def run_l1a(x2_sh, W_qkv):
    nc = build_l1a()
    shared = {
        "wqkvt": np.ascontiguousarray(W_qkv.T),
        "rtc": _rt_const(), "maskt": _maskt_const(),
        "allones": np.ones((128, 128), BF), "ident": _IDENT_BF,
    }
    in_maps = []
    for cix in range(NCORES):
        t0 = (cix % 2) * T
        cosf, sinf = _rope_tables(t0)
        in_maps.append(dict(shared, x2=x2_sh[cix], cosf=cosf, sinf=sinf))
    res = run_bass_kernel_spmd(nc, in_maps, core_ids=list(range(NCORES)))
    return res.results



def emit_l1b(c, io, fcol=False, cc_wait=None):
    nc = c.nc
    block = c.block
    num, den, q = io["num"], io["den"], io["q"]
    scorr, zcorr = io["scorr"], io["zcorr"]
    woutt, x2 = io["woutt"], io["x2"]
    allones, onesrow_f = io["allones"], io["onesrow_f"]
    x3_out = io["x3_out"]

    N = 512
    from contextlib import ExitStack
    with ExitStack() as ctx:
        e = ctx.enter_context
        qsb = e(nc.sbuf_tensor("qsb", [128, KC * T], BF16))
        aout = e(nc.sbuf_tensor("aout", [128, KC * T], BF16))
        scb = e(nc.sbuf_tensor("scb", [128, KC * 128], BF16))
        zcb = e(nc.sbuf_tensor("zcb", [128, KC], BF16))
        numsb = e(nc.sbuf_tensor("numsb", [128, 2 * CH], F32))
        nf = e(nc.sbuf_tensor("nf", [128, CH], F32))
        densb = e(nc.sbuf_tensor("densb", [1, T], F32))
        drow = e(nc.sbuf_tensor("drow", [1, 2 * CH], F32))
        rb = e(nc.sbuf_tensor("rb", [128, CH], F32))
        wsb = e(nc.sbuf_tensor("wsb", [128, KC * 1024], BF16))
        xh = e(nc.sbuf_tensor("xh", [128, KC * N], F32))
        ressb = e(nc.sbuf_tensor("ressb", [128, KC * N], F32))
        sones = e(nc.sbuf_tensor("sones", [128, 128], BF16))
        sonesrow_f = e(nc.sbuf_tensor("sonesrow_f", [1, 128], F32))
        pcor = e(nc.psum_tensor("pcor", [128, T], F32))
        po = e(nc.psum_tensor("po", [128, 2 * N], F32))
        scr = make_ln_scratch(ctx, nc, N, psum_from=(pcor, po))
        scr["onescol"] = sones[:, 0:1]
        scr["onesrow"] = sonesrow_f[:]
        pc_ = pcor[:, 0:CH]
        pd_ = pcor[0:1, CH:2 * CH]
        pb_ = pcor[:, 512:512 + CH]

        def loads_g(g):
            if cc_wait:
                for s_ in cc_wait:
                    g.wait_ge(c.sems[s_], 1)
            c.inc(g.dma_start(out=scb[:, :], in_=scorr), "ldg", 16)
            c.inc(g.dma_start(out=zcb[:, :], in_=zcorr), "ldg", 16)
        block.gpsimd(loads_g)
        if fcol:
            sfcol = e(nc.sbuf_tensor("sfcol", [128, 1], F32))

            def ld_f(s):
                c.inc(s.dma_start(out=sfcol[:], in_=io["fcol"]), "lds", 16)
            block.sync(ld_f)

            def scale_f(vector):
                c.wait(vector, "ldg")
                c.wait(vector, "lds")
                c.inc(vector.tensor_scalar_mul(out=scb[:, :], in0=scb[:, :],
                                               scalar1=sfcol[:, 0:1]), "ve")
                c.inc(vector.tensor_scalar_mul(out=zcb[:, :], in0=zcb[:, :],
                                               scalar1=sfcol[:, 0:1]), "ve")
            block.vector(scale_f)

        def loads_s(s):
            for k in range(KC):
                c.inc(s.dma_start(out=qsb[:, k * T:(k + 1) * T],
                                  in_=q[k * 128:(k + 1) * 128, :]), "lds", 16)
            c.inc(s.dma_start(out=sones[:], in_=allones), "lds", 16)
            c.inc(s.dma_start(out=sonesrow_f[:], in_=onesrow_f), "lds", 16)
        block.sync(loads_s)
        ld_q = c.counts["lds"]

        for h in range(KC):
            ve_h = c.counts["ve"]

            def ldh(s, h=h, ve_h=ve_h):
                s.wait_ge(c.sems["ve"], ve_h)
                c.inc(s.dma_start(out=densb[:],
                                  in_=den[0:1, h * T:(h + 1) * T]), "lds", 16)
            block.sync(ldh)

            for ci in range(NCH):
                c0 = ci * CH
                ve_i = c.counts["ve"]

                def ldn(s, h=h, ci=ci, c0=c0, ve_i=ve_i):
                    s.wait_ge(c.sems["ve"], ve_i)
                    c.inc(s.dma_start(
                        out=numsb[:, (ci % 2) * CH:(ci % 2 + 1) * CH],
                        in_=num[h * 128:(h + 1) * 128, c0:c0 + CH]), "lds", 16)
                block.sync(ldn)
                ld_n = c.counts["lds"]
                ve_war = c.counts["ve"]
                se_war = c.counts["se"]

                def mmc(tensor, h=h, c0=c0, ld_n=ld_n, ve_war=ve_war,
                        se_war=se_war):
                    if h == 0 and c0 == 0:
                        tensor.wait_ge(c.sems["ldg"], 32)
                        tensor.wait_ge(c.sems["lds"], ld_q)
                    tensor.wait_ge(c.sems["ve"], ve_war)
                    tensor.wait_ge(c.sems["se"], se_war)
                    c.inc(tensor.matmul(pc_, scb[:, h * 128:(h + 1) * 128],
                                        qsb[:, h * T + c0:h * T + c0 + CH],
                                        start=True, stop=True), "te")
                    c.inc(tensor.matmul(pd_, zcb[:, h:h + 1],
                                        qsb[:, h * T + c0:h * T + c0 + CH],
                                        start=True, stop=True), "te")
                block.tensor(mmc)
                te_c = c.counts["te"]

                def evc(vector, h=h, ci=ci, c0=c0, te_c=te_c, ld_n=ld_n):
                    vector.wait_ge(c.sems["te"], te_c)
                    vector.wait_ge(c.sems["lds"], ld_n)
                    c.inc(vector.tensor_add(
                        out=nf[:], in0=pc_,
                        in1=numsb[:, (ci % 2) * CH:(ci % 2 + 1) * CH]), "ve")
                    c.inc(vector.tensor_add(out=drow[0:1, 0:CH], in0=pd_,
                                            in1=densb[0:1, c0:c0 + CH]), "ve")
                    c.inc(vector.tensor_scalar_max(out=drow[0:1, CH:2 * CH],
                                                   in0=drow[0:1, 0:CH],
                                                   scalar1=1e-6), "ve")
                    c.inc(vector.reciprocal(
                        out=drow[0:1, 0:CH], in_=drow[0:1, CH:2 * CH]), "ve")
                block.vector(evc)
                ve_c = c.counts["ve"]

                def mmb(tensor, ve_c=ve_c):
                    c.wait(tensor, "ve", ve_c)
                    c.inc(tensor.matmul(pb_, sonesrow_f[:], drow[0:1, 0:CH],
                                        start=True, stop=True), "te")
                block.tensor(mmb)
                te_b = c.counts["te"]

                def evo(vector, h=h, c0=c0, te_b=te_b):
                    vector.wait_ge(c.sems["te"], te_b)
                    c.inc(vector.tensor_copy(out=rb[:], in_=pb_), "ve")
                    c.inc(vector.tensor_mul(
                        out=aout[:, h * T + c0:h * T + c0 + CH],
                        in0=nf[:], in1=rb[:]), "ve")
                block.vector(evo)

        for nh in range(2):
            n0 = nh * N
            te_prev = c.counts["te"]
            ve_prev = c.counts["ve"]

            def ldx(s, n0=n0, te_prev=te_prev, ve_prev=ve_prev):
                s.wait_ge(c.sems["te"], te_prev)
                s.wait_ge(c.sems["ve"], ve_prev)
                for k in range(KC):
                    c.inc(s.dma_start(out=xh[:, k * N:(k + 1) * N],
                                      in_=x2[k * 128:(k + 1) * 128, n0:n0 + N]),
                          "lds", 16)
            block.sync(ldx)
            ld_x = c.counts["lds"]

            for mh in range(2):
                te_prev2 = c.counts["te"]

                def ldw(g, mh=mh, te_prev2=te_prev2):
                    g.wait_ge(c.sems["te"], te_prev2)
                    for k in range(KC):
                        c.inc(g.dma_start(
                            out=wsb[:, k * 1024:(k + 1) * 1024],
                            in_=woutt[k * 128:(k + 1) * 128,
                                      mh * 1024:(mh + 1) * 1024]), "ldg", 16)
                block.gpsimd(ldw)
                ld_w = c.counts["ldg"]

                for mm_ in range(8):
                    m = mh * 8 + mm_
                    ve0 = c.counts["ve"]

                    def mmo(tensor, m=m, mm_=mm_, n0=n0, ld_w=ld_w, ve0=ve0,
                            ld_x=ld_x):
                        tensor.wait_ge(c.sems["ldg"], ld_w)
                        tensor.wait_ge(c.sems["ve"], ve0)
                        if mm_ == 0:
                            tensor.wait_ge(c.sems["lds"], ld_x)
                        ins = mm_tiles(c, tensor,
                                       po[:, (mm_ % 2) * N:(mm_ % 2 + 1) * N],
                                       lambda k: wsb[:, k * 1024:(k + 1) * 1024],
                                       lambda k: aout[:, k * T + n0:k * T + n0 + N],
                                       mm_, 0, N)
                        c.inc(ins, "te")
                    block.tensor(mmo)
                    te_o = c.counts["te"]

                    def evo2(vector, m=m, mm_=mm_, te_o=te_o):
                        c.wait(vector, "te", te_o)
                        c.inc(vector.tensor_add(
                            out=ressb[:, m * N:(m + 1) * N],
                            in0=po[:, (mm_ % 2) * N:(mm_ % 2 + 1) * N],
                            in1=xh[:, m * N:(m + 1) * N]), "ve")
                    block.vector(evo2)

            emit_ln(c, lambda k: ressb[:, k * N:(k + 1) * N],
                    lambda k: ressb[:, k * N:(k + 1) * N], N, scr,
                    src_wait={"ve": c.counts["ve"]})
            ve_pl = c.counts["ve"]

            def store(s, n0=n0, ve_pl=ve_pl):
                s.wait_ge(c.sems["ve"], ve_pl)
                for k in range(KC):
                    c.inc(s.dma_start(out=x3_out[k * 128:(k + 1) * 128,
                                                 n0:n0 + N],
                                      in_=ressb[:, k * N:(k + 1) * N]), "st", 16)
            block.sync(store)


build_l1b = _wrap(emit_l1b, [
    ("num", [D, T], F32, "ExternalInput"),
    ("den", [1, KC * T], F32, "ExternalInput"),
    ("q", [D, T], BF16, "ExternalInput"),
    ("scorr", [128, KC * 128], F32, "ExternalInput"),
    ("zcorr", [128, KC], F32, "ExternalInput"),
    ("woutt", [D, D], F32, "ExternalInput"),
    ("x2", [D, T], F32, "ExternalInput"),
    ("allones", [128, 128], BF16, "ExternalInput"),
    ("onesrow_f", [1, 128], F32, "ExternalInput"),
    ("x3_out", [D, T], F32, "ExternalOutput"),
])


def run_l1b(num_sh, den_sh, q_sh, s_sh, z_sh, x2_sh, W_out):
    nc = build_l1b()
    zeroS = np.zeros((128, KC * 128), np.float32)
    zeroZ = np.zeros((128, KC), np.float32)
    shared = {
        "woutt": np.ascontiguousarray(W_out.T),
        "allones": np.ones((128, 128), BF),
        "onesrow_f": np.ones((1, 128), np.float32),
    }
    in_maps = []
    for cix in range(NCORES):
        odd = cix % 2 == 1
        in_maps.append(dict(
            shared, num=num_sh[cix], den=den_sh[cix], q=q_sh[cix],
            scorr=s_sh[cix - 1] if odd else zeroS,
            zcorr=z_sh[cix - 1] if odd else zeroZ, x2=x2_sh[cix]))
    res = run_bass_kernel_spmd(nc, in_maps, core_ids=list(range(NCORES)))
    return res.results



GI = 5504
GIC = GI // 128


def emit_l1g_cols(c, io):
    nc = c.nc
    block = c.block
    gidx, uidx, didx = io["gidx"], io["uidx"], io["didx"]
    lut, onesrow_f = io["lut"], io["onesrow_f"]
    wg_sh, wu_sh, wd_sh = io["wg_sh"], io["wu_sh"], io["wd_sh"]

    W1, W2 = GI // 8, D // 8
    from contextlib import ExitStack
    with ExitStack() as ctx:
        e = ctx.enter_context
        sidxf = e(nc.sbuf_tensor("sidxf", [128, KC * W1], F32))
        sidx = e(nc.sbuf_tensor("sidx", [128, KC * W1], I32))
        wacc = e(nc.sbuf_tensor("wacc", [128, KC * W1], BF16))
        mscr = e(nc.sbuf_tensor("mscr", [128, 2 * W1], F32))
        slutf = e(nc.sbuf_tensor("slutf", [1, 16], F32))
        lutcol = e(nc.sbuf_tensor("lutcol", [128, 16], F32))
        sonesrow_f = e(nc.sbuf_tensor("sonesrow_f", [1, 128], F32))
        pl = e(nc.psum_tensor("pl", [128, 16], F32))

        def loads0(s):
            c.inc(s.dma_start(out=slutf[:], in_=lut), "lds", 16)
            c.inc(s.dma_start(out=sonesrow_f[:], in_=onesrow_f), "lds", 16)
        block.sync(loads0)

        def bclut(tensor):
            tensor.wait_ge(c.sems["lds"], 32)
            c.inc(tensor.matmul(pl[:, :], sonesrow_f[:], slutf[:],
                                start=True, stop=True), "te")
        block.tensor(bclut)

        te_lut = c.counts["te"]

        def evlut(vector):
            c.wait(vector, "te", te_lut)
            c.inc(vector.tensor_copy(out=lutcol[:, :], in_=pl[:, :]), "ve")
        block.vector(evlut)

        for mat, (src, dst, W, nkc) in enumerate(
                [(gidx, wg_sh, W1, KC), (uidx, wu_sh, W1, KC),
                 (didx, wd_sh, W2, GIC)]):
            ve_prev = c.counts["ve"]
            st_prev = c.counts["st"]

            def ldi(s, src=src, W=W, nkc=nkc, ve_prev=ve_prev, st_prev=st_prev):
                s.wait_ge(c.sems["ve"], ve_prev)
                s.wait_ge(c.sems["st"], st_prev)
                for k in range(nkc):
                    c.inc(s.dma_start(out=sidx[:, k * W:(k + 1) * W],
                                      in_=src[k * 128:(k + 1) * 128, :]),
                          "lds", 16)
            block.sync(ldi)
            ld_i = c.counts["lds"]

            def conv(vector, W=W, nkc=nkc, ld_i=ld_i, mat=mat):
                vector.wait_ge(c.sems["lds"], ld_i)
                if mat == 0:
                    c.wait(vector, "se")
                for k in range(nkc):
                    c.inc(vector.tensor_copy(out=sidxf[:, k * W:(k + 1) * W],
                                             in_=sidx[:, k * W:(k + 1) * W]),
                          "ve")
                vector.memset(wacc[:, 0:nkc * W], 0.0)
                for j in range(16):
                    for k in range(nkc):
                        c.inc(vector.tensor_scalar(
                            out=mscr[:, (k % 2) * W1:(k % 2) * W1 + W],
                            in0=sidxf[:, k * W:(k + 1) * W],
                            scalar1=float(j), scalar2=None,
                            op0=ALU.is_equal), "ve")
                        c.inc(vector.scalar_tensor_tensor(
                            out=wacc[:, k * W:(k + 1) * W],
                            in0=mscr[:, (k % 2) * W1:(k % 2) * W1 + W],
                            scalar=lutcol[:, j:j + 1],
                            in1=wacc[:, k * W:(k + 1) * W],
                            op0=ALU.mult, op1=ALU.add), "ve")
            block.vector(conv)
            ve_b = c.counts["ve"]

            def stw(s, dst=dst, W=W, nkc=nkc, ve_b=ve_b):
                s.wait_ge(c.sems["ve"], ve_b)
                for k in range(nkc):
                    c.inc(s.dma_start(out=dst[k * 128:(k + 1) * 128, :],
                                      in_=wacc[:, k * W:(k + 1) * W]), "st", 16)
            block.sync(stw)


build_l1g = _wrap(emit_l1g_cols, [
    ("gidx", [D, GI // 8], I32, "ExternalInput"),
    ("uidx", [D, GI // 8], I32, "ExternalInput"),
    ("didx", [GI, D // 8], I32, "ExternalInput"),
    ("lut", [1, 16], F32, "ExternalInput"),
    ("onesrow_f", [1, 128], F32, "ExternalInput"),
    ("wg_sh", [D, GI // 8], BF16, "ExternalOutput"),
    ("wu_sh", [D, GI // 8], BF16, "ExternalOutput"),
    ("wd_sh", [GI, D // 8], BF16, "ExternalOutput"),
])


def run_l1g(gate_idx, up_idx, down_idx, lut):
    nc = build_l1g()
    gT = np.ascontiguousarray(gate_idx.T)
    uT = np.ascontiguousarray(up_idx.T)
    dT = np.ascontiguousarray(down_idx.T)
    shared = {"lut": lut.reshape(1, 16).astype(np.float32),
              "onesrow_f": np.ones((1, 128), np.float32)}
    in_maps = []
    for cix in range(NCORES):
        in_maps.append(dict(
            shared,
            gidx=np.ascontiguousarray(gT[:, cix * 688:(cix + 1) * 688]),
            uidx=np.ascontiguousarray(uT[:, cix * 688:(cix + 1) * 688]),
            didx=np.ascontiguousarray(dT[:, cix * 256:(cix + 1) * 256])))
    res = run_bass_kernel_spmd(nc, in_maps, core_ids=list(range(NCORES)))
    wgt = np.concatenate([r["wg_sh"] for r in res.results], axis=1)
    wut = np.concatenate([r["wu_sh"] for r in res.results], axis=1)
    wdt = np.concatenate([r["wd_sh"] for r in res.results], axis=1)
    return wgt, wut, wdt



G_GROUPS = [(0, 11), (11, 11), (22, 11), (33, 10)]


def emit_l1f(c, io, cc_wait=None):
    nc = c.nc
    block = c.block
    x3, wgt, wut, wdt = io["x3"], io["wgt"], io["wut"], io["wdt"]
    allones, onesrow_f = io["allones"], io["onesrow_f"]
    x4_out = io["x4_out"]

    N = 512
    from contextlib import ExitStack
    with ExitStack() as ctx:
        e = ctx.enter_context
        xbf = e(nc.sbuf_tensor("xbf", [128, KC * T], BF16))
        yacc = e(nc.sbuf_tensor("yacc", [128, KC * T], F32))
        wbuf = e(nc.sbuf_tensor("wbuf", [128, KC * INTER], BF16))
        hsc = e(nc.sbuf_tensor("hsc", [128, IC * T], BF16))
        sones = e(nc.sbuf_tensor("sones", [128, 128], BF16))
        sonesrow_f = e(nc.sbuf_tensor("sonesrow_f", [1, 128], F32))
        ph = e(nc.psum_tensor("ph", [128, T], F32))
        py = e(nc.psum_tensor("py", [128, T], F32))
        scr = make_ln_scratch(ctx, nc, N, psum_from=(ph, py))
        scr["onescol"] = sones[:, 0:1]
        scr["onesrow"] = sonesrow_f[:]

        def loads_s(s):
            for k in range(KC):
                c.inc(s.dma_start(out=yacc[:, k * T:(k + 1) * T],
                                  in_=x3[k * 128:(k + 1) * 128, :]), "lds", 16)
            c.inc(s.dma_start(out=sones[:], in_=allones), "lds", 16)
            c.inc(s.dma_start(out=sonesrow_f[:], in_=onesrow_f), "lds", 16)
        block.sync(loads_s)

        def loads_g(g):
            for k in range(KC):
                c.inc(g.dma_start(out=xbf[:, k * T:(k + 1) * T],
                                  in_=x3[k * 128:(k + 1) * 128, :]), "ldg", 16)
        block.gpsimd(loads_g)

        sewA = [0, 0]
        vewB = [0, 0]
        vewC = [0, 0]
        for gs, gsz in G_GROUPS:
            gw = gsz * 128
            te_prevC = c.counts["te"]

            def lwg(gp, gs=gs, gw=gw, te_prevC=te_prevC):
                if cc_wait and gs == 0:
                    for s_ in cc_wait:
                        gp.wait_ge(c.sems[s_], 1)
                gp.wait_ge(c.sems["te"], te_prevC)
                for k in range(KC):
                    c.inc(gp.dma_start(
                        out=wbuf[:, k * INTER:k * INTER + gw],
                        in_=wgt[k * 128:(k + 1) * 128,
                                gs * 128:gs * 128 + gw]), "ldg", 16)
            block.gpsimd(lwg)
            ld_w1 = c.counts["ldg"]

            for i in range(gsz):
                for hf in (0, 1):
                    need = sewA[hf]

                    def mmah(tensor, i=i, hf=hf, need=need, ld_w1=ld_w1):
                        tensor.wait_ge(c.sems["ldg"], ld_w1)
                        tensor.wait_ge(c.sems["se"], need)
                        ins = mm_tiles(c, tensor, ph[:, hf * N:(hf + 1) * N],
                                       lambda k: wbuf[:, k * INTER:(k + 1) * INTER],
                                       lambda k: xbf[:, k * T:(k + 1) * T],
                                       i, hf * N, N)
                        c.inc(ins, "te")
                    block.tensor(mmah)
                    te_h = c.counts["te"]

                    def evah(scalar, i=i, hf=hf, te_h=te_h):
                        scalar.wait_ge(c.sems["te"], te_h)
                        c.inc(scalar.activation(
                            out=hsc[:, i * T + hf * N:i * T + (hf + 1) * N],
                            in_=ph[:, hf * N:(hf + 1) * N], func=AF.Silu),
                            "se")
                    block.scalar(evah)
                    sewA[hf] = c.counts["se"]

            te_prevA = c.counts["te"]

            def lwu(gp, gs=gs, gw=gw, te_prevA=te_prevA):
                gp.wait_ge(c.sems["te"], te_prevA)
                for k in range(KC):
                    c.inc(gp.dma_start(
                        out=wbuf[:, k * INTER:k * INTER + gw],
                        in_=wut[k * 128:(k + 1) * 128,
                                gs * 128:gs * 128 + gw]), "ldg", 16)
            block.gpsimd(lwu)
            ld_w3 = c.counts["ldg"]
            se_afterA = c.counts["se"]

            for i in range(gsz):
                for hf in (0, 1):
                    need = vewB[hf]

                    def mmbh(tensor, i=i, hf=hf, need=need, ld_w3=ld_w3,
                             se_afterA=se_afterA):
                        tensor.wait_ge(c.sems["ldg"], ld_w3)
                        tensor.wait_ge(c.sems["ve"], need)
                        if i == 0 and hf == 0:
                            tensor.wait_ge(c.sems["se"], se_afterA)
                        ins = mm_tiles(c, tensor, ph[:, hf * N:(hf + 1) * N],
                                       lambda k: wbuf[:, k * INTER:(k + 1) * INTER],
                                       lambda k: xbf[:, k * T:(k + 1) * T],
                                       i, hf * N, N)
                        c.inc(ins, "te")
                    block.tensor(mmbh)
                    te_h = c.counts["te"]

                    def evbh(vector, i=i, hf=hf, te_h=te_h):
                        vector.wait_ge(c.sems["te"], te_h)
                        lo = i * T + hf * N
                        c.inc(vector.tensor_mul(
                            out=hsc[:, lo:lo + N],
                            in0=ph[:, hf * N:(hf + 1) * N],
                            in1=hsc[:, lo:lo + N]), "ve")
                    block.vector(evbh)
                    vewB[hf] = c.counts["ve"]

            te_prevB = c.counts["te"]
            ve_hsc = c.counts["ve"]

            def lwd(gp, gs=gs, gsz=gsz, te_prevB=te_prevB):
                gp.wait_ge(c.sems["te"], te_prevB)
                for i in range(gsz):
                    c.inc(gp.dma_start(
                        out=wbuf[:, i * D:(i + 1) * D],
                        in_=wdt[(gs + i) * 128:(gs + i + 1) * 128, :]),
                        "ldg", 16)
            block.gpsimd(lwd)
            ld_w2 = c.counts["ldg"]

            for m in range(KC):
                for hf in (0, 1):
                    need = vewC[hf]

                    def mmch(tensor, m=m, hf=hf, gsz=gsz, need=need,
                             ld_w2=ld_w2, ve_hsc=ve_hsc):
                        tensor.wait_ge(c.sems["ldg"], ld_w2)
                        tensor.wait_ge(c.sems["ve"], max(need, ve_hsc))
                        ins = mm_tiles(c, tensor, py[:, hf * N:(hf + 1) * N],
                                       lambda i: wbuf[:, i * D:(i + 1) * D],
                                       lambda i: hsc[:, i * T:(i + 1) * T],
                                       m, hf * N, N, kcs=range(gsz))
                        c.inc(ins, "te")
                    block.tensor(mmch)
                    te_h = c.counts["te"]

                    def evch(vector, m=m, hf=hf, te_h=te_h):
                        vector.wait_ge(c.sems["te"], te_h)
                        if m == 0 and hf == 0:
                            vector.wait_ge(c.sems["lds"], 16 * KC)
                        lo = m * T + hf * N
                        c.inc(vector.tensor_add(
                            out=yacc[:, lo:lo + N],
                            in0=py[:, hf * N:(hf + 1) * N],
                            in1=yacc[:, lo:lo + N]), "ve")
                    block.vector(evch)
                    vewC[hf] = c.counts["ve"]


        for nh in range(2):
            n0 = nh * N
            emit_ln(c, lambda k: yacc[:, k * T + n0:k * T + n0 + N],
                    lambda k: yacc[:, k * T + n0:k * T + n0 + N], N, scr,
                    src_wait={"ve": c.counts["ve"], "se": c.counts["se"]})
            ve_pl = c.counts["ve"]

            def store(s, n0=n0, ve_pl=ve_pl):
                s.wait_ge(c.sems["ve"], ve_pl)
                for k in range(KC):
                    c.inc(s.dma_start(out=x4_out[k * 128:(k + 1) * 128,
                                                 n0:n0 + N],
                                      in_=yacc[:, k * T + n0:k * T + n0 + N]),
                          "st", 16)
            block.sync(store)


build_l1f = _wrap(emit_l1f, [
    ("x3", [D, T], F32, "ExternalInput"),
    ("wgt", [D, GI], BF16, "ExternalInput"),
    ("wut", [D, GI], BF16, "ExternalInput"),
    ("wdt", [GI, D], BF16, "ExternalInput"),
    ("allones", [128, 128], BF16, "ExternalInput"),
    ("onesrow_f", [1, 128], F32, "ExternalInput"),
    ("x4_out", [D, T], F32, "ExternalOutput"),
])


def run_l1f(x3_sh, wgt, wut, wdt):
    nc = build_l1f()
    shared = {"wgt": wgt, "wut": wut, "wdt": wdt,
              "allones": np.ones((128, 128), BF),
              "onesrow_f": np.ones((1, 128), np.float32)}
    in_maps = [dict(shared, x3=x3_sh[cix]) for cix in range(NCORES)]
    res = run_bass_kernel_spmd(nc, in_maps, core_ids=list(range(NCORES)))
    return res.results



def emit_l1g_j(c, io):
    nc = c.nc
    block = c.block
    mats = [(io["gidxt"], io["wgp"], GI, KC), (io["uidxt"], io["wup"], GI, KC),
            (io["didxt"], io["wdp"], D, GIC)]
    from contextlib import ExitStack
    with ExitStack() as ctx:
        e = ctx.enter_context
        sidx = e(nc.sbuf_tensor("gsidx", [128, 2 * GI], I32))
        sidxf = e(nc.sbuf_tensor("gsidxf", [128, 2 * GI], F32))
        wch = e(nc.sbuf_tensor("gwch", [128, 2 * GI], BF16))
        msl = e(nc.sbuf_tensor("gmsl", [128, 2 * GI], F32))
        sjv = e(nc.sbuf_tensor("gsjv", [128, 2], F32))
        slut = e(nc.sbuf_tensor("gslut", [128, 2], F32))

        def loads0(s):
            c.inc(s.dma_start(out=sjv[:], in_=io["jvals"]), "lds", 16)
            c.inc(s.dma_start(out=slut[:], in_=io["lutsel"]), "lds", 16)
        block.sync(loads0)

        chunks = []
        for mi, (src, dst, W, nkc) in enumerate(mats):
            for k in range(nkc):
                chunks.append((mi, k))
        hist = []
        for ci, (mi, k) in enumerate(chunks):
            src, dst, W, nkc = mats[mi]
            sl = ci % 2
            ve_guard = hist[ci - 2][0] if ci >= 2 else 0
            st_guard = hist[ci - 2][1] if ci >= 2 else 0

            def ld(s, src=src, W=W, k=k, sl=sl, ve_guard=ve_guard):
                s.wait_ge(c.sems["ve"], ve_guard)
                c.inc(s.dma_start(out=sidx[:, sl * GI:sl * GI + W],
                                  in_=src[k * 128:(k + 1) * 128, :]), "lds", 16)
            block.sync(ld)
            ld_i = c.counts["lds"]

            def conv(gp, W=W, sl=sl, ld_i=ld_i, ve_guard=ve_guard):
                gp.wait_ge(c.sems["lds"], ld_i)
                gp.wait_ge(c.sems["ve"], ve_guard)
                c.inc(gp.tensor_copy(out=sidxf[:, sl * GI:sl * GI + W],
                                     in_=sidx[:, sl * GI:sl * GI + W]), "po")
            block.gpsimd(conv)
            se_i = c.counts["po"]

            def bld(vector, W=W, sl=sl, se_i=se_i, st_guard=st_guard, ci=ci):
                c.wait(vector, "po", se_i)
                vector.wait_ge(c.sems["st"], st_guard)
                if ci == 0:
                    c.wait(vector, "lds", 32)
                c.inc(vector.tensor_scalar(
                    out=wch[:, sl * GI:sl * GI + W],
                    in0=sidxf[:, sl * GI:sl * GI + W],
                    scalar1=sjv[:, 0:1], scalar2=slut[:, 0:1],
                    op0=ALU.is_equal, op1=ALU.mult), "ve")
                c.inc(vector.tensor_scalar(
                    out=msl[:, sl * GI:sl * GI + W],
                    in0=sidxf[:, sl * GI:sl * GI + W],
                    scalar1=sjv[:, 1:2], scalar2=None,
                    op0=ALU.is_equal), "ve")
                c.inc(vector.scalar_tensor_tensor(
                    out=wch[:, sl * GI:sl * GI + W],
                    in0=msl[:, sl * GI:sl * GI + W],
                    scalar=slut[:, 1:2],
                    in1=wch[:, sl * GI:sl * GI + W],
                    op0=ALU.mult, op1=ALU.add), "ve")
            block.vector(bld)
            ve_b = c.counts["ve"]

            def stw(g, dst=dst, W=W, k=k, sl=sl, ve_b=ve_b):
                g.wait_ge(c.sems["ve"], ve_b)
                c.inc(g.dma_start(out=dst[k * 128:(k + 1) * 128, :],
                                  in_=wch[:, sl * GI:sl * GI + W]), "st", 16)
            block.gpsimd(stw)
            hist.append((c.counts["ve"], c.counts["st"]))



def build_fused():
    from contextlib import ExitStack
    nc = bass.Bass("TRN2", target_bir_lowering=False)
    EI, EO = "ExternalInput", "Internal"
    io = {}

    def d(name, shape, dt, kind="ExternalInput"):
        io[name] = nc.dram_tensor(name, shape, dt, kind=kind).ap()
    d("x", [D, T], F32)
    d("wqk", [D, D], F32); d("wv", [D, D], F32); d("wu", [D, D], F32)
    d("bqk", [128, KC], F32); d("bv", [128, KC], F32); d("bu", [128, KC], F32)
    d("wo", [D, D], F32)
    d("lng", [128, KC], F32); d("lnb", [128, KC], F32); d("bo", [128, KC], F32)
    d("gwt", [D, 16], F32)
    d("w1t", [16, D, INTER], F32); d("w3t", [16, D, INTER], F32)
    d("w2t", [16, INTER, D], F32)
    d("sw1t", [D, 2 * INTER], F32); d("sw3t", [D, 2 * INTER], F32)
    d("sw2t", [2 * INTER, D], F32)
    d("ind", [16, D], BF16)
    d("wqkvt", [D, 3 * D], F32); d("woutt", [D, D], F32)
    d("cosf", [128, T], F32); d("sinf", [128, T], F32)
    d("rtc", [128, 128], BF16); d("maskt", [128, 2 * CH], BF16)
    d("gidxt", [D, GI], I32); d("uidxt", [D, GI], I32); d("didxt", [GI, D], I32)
    d("jvals", [128, 2], F32); d("lutsel", [128, 2], F32)
    d("mcol", [128, 1], F32); d("fcol", [128, 1], F32)
    d("ident", [128, 128], BF16); d("allones", [128, 128], BF16)
    d("onesrow_f", [1, 128], F32)
    d("x4_out", [D, T], F32, "ExternalOutput")
    d("q0", [D, T], BF16, EO); d("u0", [D, T], F32, EO)
    d("kvp", [128, KC * 128], F32, EO); d("kvr", [128, KC * 128], F32, EO)
    d("x1", [D, T], F32, EO); d("x2", [D, T], F32, EO); d("x3", [D, T], F32, EO)
    d("q1", [D, T], BF16, EO)
    d("num", [D, T], F32, EO); d("den", [1, KC * T], F32, EO)
    d("ssc", [128, KC * 128], F32, EO); d("zsc", [128, KC], F32, EO)
    d("sred", [128, KC * 128], F32, EO); d("zred", [128, KC], F32, EO)
    d("wgp", [D, GI], BF16, EO); d("wup", [D, GI], BF16, EO)
    d("wdp", [GI, D], BF16, EO)
    d("wgr", [D, GI], BF16, EO); d("wur", [D, GI], BF16, EO)
    d("wdr", [GI, D], BF16, EO)

    PAIRS = [[0, 1], [2, 3], [4, 5], [6, 7]]
    ALL8 = [list(range(8))]

    with ExitStack() as st0:
        c = mk_ctx(nc, st0)
        for s in ("cckv", "ccs", "ccz", "ccg", "ccu", "ccd"):
            c.ccsem(s)
        c.block = st0.enter_context(nc.Block())

        c.new_phase()
        emit_l0a(c, dict(io, q_out=io["q0"], u_out=io["u0"], kv_out=io["kvp"]))
        st_kv = c.counts["st"]

        def coll_kv(g):
            g.wait_ge(c.sems["st"], st_kv)
            c.inc(g.collective_compute(
                "AllReduce", ALU.add, PAIRS,
                ins=[io["kvp"].opt()], outs=[io["kvr"].opt()]), "cckv")
        c.block.gpsimd(coll_kv)
        emit_barrier(c)

        c.new_phase()
        emit_l0b(c, dict(io, q=io["q0"], u=io["u0"], x1_out=io["x1"]),
                 kv_pair=False, cc_wait="cckv")
        emit_barrier(c)

        c.new_phase()
        emit_l0f(c, dict(io, x2_out=io["x2"]))
        emit_barrier(c)

        c.new_phase()
        emit_l1g_j(c, io)
        st_g = c.counts["st"]

        def coll_g(g):
            g.wait_ge(c.sems["st"], st_g)
            c.inc(g.collective_compute(
                "AllReduce", ALU.add, ALL8,
                ins=[io["wgp"].opt()], outs=[io["wgr"].opt()]), "ccg")
            c.inc(g.collective_compute(
                "AllReduce", ALU.add, ALL8,
                ins=[io["wup"].opt()], outs=[io["wur"].opt()]), "ccu")
            c.inc(g.collective_compute(
                "AllReduce", ALU.add, ALL8,
                ins=[io["wdp"].opt()], outs=[io["wdr"].opt()]), "ccd")
        c.block.gpsimd(coll_g)
        emit_barrier(c)

        c.new_phase()
        emit_l1a(c, dict(io, q_out=io["q1"], num_out=io["num"],
                         den_out=io["den"], s_out=io["ssc"], z_out=io["zsc"]))
        st_sz = c.counts["st"]

        def coll_sz(g):
            g.wait_ge(c.sems["st"], st_sz)
            c.inc(g.collective_compute(
                "AllReduce", ALU.add, PAIRS,
                ins=[io["ssc"].opt()], outs=[io["sred"].opt()]), "ccs")
            c.inc(g.collective_compute(
                "AllReduce", ALU.add, PAIRS,
                ins=[io["zsc"].opt()], outs=[io["zred"].opt()]), "ccz")
        c.block.gpsimd(coll_sz)
        emit_barrier(c)

        c.new_phase()
        emit_l1b(c, dict(io, q=io["q1"], scorr=io["sred"], zcorr=io["zred"],
                         x2=io["x2"], x3_out=io["x3"]),
                 fcol=True, cc_wait=["ccs", "ccz"])
        emit_barrier(c)

        c.new_phase()
        emit_l1f(c, dict(io, wgt=io["wgr"], wut=io["wur"], wdt=io["wdr"]),
                 cc_wait=["ccg", "ccu", "ccd"])
        st_end = c.counts["st"]

        def fin(g):
            g.wait_ge(c.sems["st"], st_end)
        c.block.gpsimd(fin)
    return nc


def run_fused(ins):
    nc = build_fused()
    lut = np.asarray(ins["L1_ffn_lut"], dtype=np.float32)
    shared = {
        "wqk": np.ascontiguousarray(ins["L0_attn_Wqk"]),
        "wv": np.ascontiguousarray(ins["L0_attn_Wv"]),
        "wu": np.ascontiguousarray(ins["L0_attn_Wu"]),
        "bqk": _bias_cols(ins["L0_attn_bqk"]),
        "bv": _bias_cols(ins["L0_attn_bv"]),
        "bu": _bias_cols(ins["L0_attn_bu"]),
        "wo": np.ascontiguousarray(ins["L0_attn_Wo"]),
        "bo": _bias_cols(ins["L0_attn_bo"]),
        "lng": _bias_cols(ins["L0_attn_ln_g"]),
        "lnb": _bias_cols(ins["L0_attn_ln_b"]),
        "gwt": np.ascontiguousarray(ins["L0_ffn_gate_w"].T),
        "w1t": np.ascontiguousarray(ins["L0_ffn_w1"].transpose(0, 2, 1)),
        "w3t": np.ascontiguousarray(ins["L0_ffn_w3"].transpose(0, 2, 1)),
        "w2t": np.ascontiguousarray(ins["L0_ffn_w2"].transpose(0, 2, 1)),
        "sw1t": np.ascontiguousarray(ins["L0_ffn_sw1"].T),
        "sw3t": np.ascontiguousarray(ins["L0_ffn_sw3"].T),
        "sw2t": np.ascontiguousarray(ins["L0_ffn_sw2"].T),
        "ind": _make_ind(),
        "wqkvt": np.ascontiguousarray(ins["L1_attn_W_qkv"].T),
        "woutt": np.ascontiguousarray(ins["L1_attn_W_out"].T),
        "rtc": _rt_const(), "maskt": _maskt_const(),
        "gidxt": np.ascontiguousarray(ins["L1_ffn_gate_idx"].T),
        "uidxt": np.ascontiguousarray(ins["L1_ffn_up_idx"].T),
        "didxt": np.ascontiguousarray(ins["L1_ffn_down_idx"].T),
        "ident": _IDENT_BF,
        "allones": np.ones((128, 128), BF),
        "onesrow_f": np.ones((1, 128), np.float32),
    }
    x = np.ascontiguousarray(ins["L0_attn_x"], dtype=np.float32)
    shards = _shard_x(x)
    in_maps = []
    for cix in range(NCORES):
        t0 = (cix % 2) * T
        cosf, sinf = _rope_tables(t0)
        odd = float(cix % 2)
        jv = np.full((128, 2), 0.0, np.float32)
        jv[:, 0] = 2 * cix
        jv[:, 1] = 2 * cix + 1
        ls = np.ascontiguousarray(
            np.broadcast_to(lut[2 * cix:2 * cix + 2], (128, 2)))
        in_maps.append(dict(
            shared, x=shards[cix], cosf=cosf, sinf=sinf,
            jvals=jv, lutsel=ls,
            mcol=np.full((128, 1), 1.0 - odd, np.float32),
            fcol=np.full((128, 1), odd, np.float32)))
    res = run_bass_kernel_spmd(nc, in_maps, core_ids=list(range(NCORES)))
    out = np.empty((4, 2048, D), np.float32)
    for cix in range(NCORES):
        b, t0 = cix // 2, (cix % 2) * T
        out[b, t0:t0 + T, :] = res.results[cix]["x4_out"].T
    return out



_IDENT_BF = np.eye(128, dtype=BF)


def _shard_x(xfull):
    out = []
    for core in range(NCORES):
        b, t0 = core // 2, (core % 2) * T
        out.append(np.ascontiguousarray(xfull[b, t0:t0 + T, :].T))
    return out


def _bias_cols(b):
    return np.ascontiguousarray(b.reshape(KC, 128).T)


def run_l0a(x_shards, Wqk, bqk, Wv, bv, Wu, bu):
    nc = build_l0a()
    shared = {
        "wqk": np.ascontiguousarray(Wqk), "wv": np.ascontiguousarray(Wv),
        "wu": np.ascontiguousarray(Wu),
        "bqk": _bias_cols(bqk), "bv": _bias_cols(bv), "bu": _bias_cols(bu),
        "ident": _IDENT_BF,
    }
    in_maps = [dict(shared, x=x_shards[c]) for c in range(NCORES)]
    res = run_bass_kernel_spmd(nc, in_maps, core_ids=list(range(NCORES)))
    return res.results


def kernel(**inputs):
    return run_fused(inputs)


def kernel_multilaunch(**inputs):
    ins = inputs
    shards = _shard_x(np.ascontiguousarray(ins["L0_attn_x"], dtype=np.float32))

    r1 = run_l0a(shards, ins["L0_attn_Wqk"], ins["L0_attn_bqk"],
                 ins["L0_attn_Wv"], ins["L0_attn_bv"],
                 ins["L0_attn_Wu"], ins["L0_attn_bu"])
    r2 = run_l0b([r["q_out"] for r in r1], [r["kv_out"] for r in r1],
                 [r["u_out"] for r in r1], shards,
                 ins["L0_attn_Wo"], ins["L0_attn_bo"],
                 ins["L0_attn_ln_g"], ins["L0_attn_ln_b"])
    x1_sh = [r["x1_out"] for r in r2]
    r3 = run_l0f(x1_sh, ins["L0_ffn_gate_w"], ins["L0_ffn_w1"],
                 ins["L0_ffn_w2"], ins["L0_ffn_w3"], ins["L0_ffn_sw1"],
                 ins["L0_ffn_sw2"], ins["L0_ffn_sw3"])
    x2_sh = [r["x2_out"] for r in r3]
    ra = run_l1a(x2_sh, ins["L1_attn_W_qkv"])
    rb = run_l1b([r["num_out"] for r in ra], [r["den_out"] for r in ra],
                 [r["q_out"] for r in ra], [r["s_out"] for r in ra],
                 [r["z_out"] for r in ra], x2_sh, ins["L1_attn_W_out"])
    x3_sh = [r["x3_out"] for r in rb]
    wgt, wut, wdt = run_l1g(ins["L1_ffn_gate_idx"], ins["L1_ffn_up_idx"],
                            ins["L1_ffn_down_idx"], ins["L1_ffn_lut"])
    rf = run_l1f(x3_sh, wgt, wut, wdt)

    out = np.empty((4, 2048, D), np.float32)
    for cix in range(NCORES):
        b, t0 = cix // 2, (cix % 2) * T
        out[b, t0:t0 + T, :] = rf[cix]["x4_out"].T
    return out
